# version 28
# baseline (speedup 1.0000x reference)
from contextlib import ExitStack

import numpy as np
import ml_dtypes

import concourse.bass as bass
import concourse.mybir as mybir
from concourse.bass_utils import run_bass_kernel_spmd

F32 = mybir.dt.float32
BF16 = mybir.dt.bfloat16
I32 = mybir.dt.int32
AF = mybir.ActivationFunctionType
ALU = mybir.AluOpType
AX = mybir.AxisListType

BF = ml_dtypes.bfloat16
MAGIC = 12582912.0
SCALE_SQ = {0: (1.0 / 128.0) ** 0.25, 1: (1.0 / 64.0) ** 0.25}

N_CORES = 8
B, S, E = 4, 4096, 1024
H = 16
T_OWN = 2048
NB = T_OWN // 128


class Sem:
    def __init__(self, handle):
        self.h = handle
        self.n = 0


class Prog:
    ENGS = ("sync", "gpsimd", "tensor", "vector", "scalar")
    CHAINED = ("vector", "scalar")

    def __init__(self, nc, ctx):
        self.ops = {e: [] for e in self.ENGS}
        self.chain_sem = {e: Sem(ctx.enter_context(nc.semaphore(f"ch_{e}")))
                          for e in self.CHAINED}

    def emit(self, eng, fn, wait=(), inc=(), chain=True):
        wait = list(wait)
        inc = list(inc)
        if eng in self.CHAINED:
            cs = self.chain_sem[eng]
            if chain:
                wait.append((cs, cs.n))
            if not any(s is cs for s, _ in inc):
                inc.append((cs, 1))
        for s, a in inc:
            s.n += a
        self.ops[eng].append((fn, tuple(wait), tuple(inc)))

    def run(self, blk):
        def make(eng_name):
            def body(e):
                for fn, waits, incs in self.ops[eng_name]:
                    for s, v in waits:
                        e.wait_ge(s.h, v)
                    inst = fn(e)
                    for s, a in incs:
                        inst = inst.then_inc(s.h, a)
            return body
        blk.sync(make("sync"))
        blk.gpsimd(make("gpsimd"))
        blk.tensor(make("tensor"))
        blk.vector(make("vector"))
        blk.scalar(make("scalar"))


def tok(sem):
    return (sem, sem.n)


def _rope_tables_l0(pos):
    j = np.arange(32, dtype=np.float64)
    freqs = 10000.0 ** (-j / 32.0)
    th = pos[:, None].astype(np.float64) * freqs[None, :]
    return np.cos(th).astype(np.float32), np.sin(th).astype(np.float32)


def _rope_tables_l1(pos):
    j2 = np.arange(0, 64, 2, dtype=np.float64)
    inv = 1.0 / (10000.0 ** (j2 / 64.0))
    th = pos[:, None].astype(np.float64) * inv[None, :]
    emb = np.concatenate([np.sin(th), np.cos(th)], axis=-1)
    return emb[:, 1::2].astype(np.float32), emb[:, 0::2].astype(np.float32)


def _blocked(tab):
    t = tab.reshape(NB, 128, 32).transpose(1, 0, 2).reshape(128, NB * 32)
    return np.ascontiguousarray(t).astype(BF)


def build_pre(layer):
    nc = bass.Bass("TRN2", target_bir_lowering=False)
    dt = lambda n, sh, d, k: nc.dram_tensor(n, sh, d, kind=k).ap()

    x = dt("x", [T_OWN, E], F32, "ExternalInput")
    ident = dt("ident", [128, 128], BF16, "ExternalInput")
    onesf = dt("onesf", [1, 128], F32, "ExternalInput")
    tabC = dt("tabC", [128, NB * 32], BF16, "ExternalInput")
    tabS = dt("tabS", [128, NB * 32], BF16, "ExternalInput")
    fbias = dt("fbias", [1, E], F32, "ExternalInput")
    w8 = dt("w8", [E, E], I32, "ExternalInput")
    if layer == 0:
        Wdq = dt("Wdq", [E, E], F32, "ExternalInput")
        Wuq = dt("Wuq", [E, E], F32, "ExternalInput")
        Wqr = dt("Wqr", [E, E], F32, "ExternalInput")
        Wdkv = dt("Wdkv", [E, 256], F32, "ExternalInput")
        Wuk = dt("Wuk", [256, E], F32, "ExternalInput")
        Wkr = dt("Wkr", [E, E], F32, "ExternalInput")
        Wuv = dt("Wuv", [256, E], F32, "ExternalInput")
        wscale = dt("wscale", [1, 1], F32, "ExternalInput")
    else:
        qkv_w = dt("qkv_w", [3 * E, E], F32, "ExternalInput")
        qkv_b = dt("qkv_b", [1, 3 * E], F32, "ExternalInput")
        ascale = dt("ascale", [1, 1], F32, "ExternalInput")
    QW = 2048 if layer == 0 else E
    q_out = dt("q_out", [T_OWN, QW], BF16, "ExternalOutput")
    k_out = dt("k_out", [T_OWN, QW], BF16, "ExternalOutput")
    v_out = dt("v_out", [T_OWN, E], BF16, "ExternalOutput")
    ffn_out = dt("ffn_out", [T_OWN, E], F32, "ExternalOutput")

    with ExitStack() as ctx:
        _dummy = None
        sb = lambda nm, sh, d=F32: ctx.enter_context(nc.sbuf_tensor(nm, sh, d))
        sem = lambda nm: Sem(ctx.enter_context(nc.semaphore(nm)))
        p = Prog(nc, ctx)
        vd = p.chain_sem["vector"]
        sd = p.chain_sem["scalar"]

        ld = sem("ld")
        gld = sem("gld")
        l8 = sem("l8")
        lq1 = sem("lq1")
        lfb = sem("lfb")
        lsc = sem("lsc")
        lqb = sem("lqb")
        lx = sem("lx")
        pe = sem("pe")
        st = sem("st")
        st_f = sem("st_f")

        s_ident = sb("s_ident", [128, 128], BF16)
        s_ones = sb("s_ones", [1, 128], F32)
        s_tabC = sb("s_tabC", [128, NB * 32], BF16)
        s_tabS = sb("s_tabS", [128, NB * 32], BF16)
        s_bias = sb("s_bias", [128, E], BF16)
        s_sc = sb("s_sc", [128, 2], F32)

        if layer == 0:
            s_Wdq = sb("s_Wdq", [128, 8 * E], BF16)
            s_Wuq = sb("s_Wuq", [128, 8 * E], BF16)
            s_Wqr = sb("s_Wqr", [128, 8 * E], BF16)
            s_Wdkv = sb("s_Wdkv", [128, 8 * 256], BF16)
            s_Wuk = sb("s_Wuk", [128, 2 * E], BF16)
            s_Wkr = sb("s_Wkr", [128, 8 * E], BF16)
            s_Wuv = sb("s_Wuv", [128, 2 * E], BF16)
        else:
            s_qkvT = sb("s_qkvT", [128, 8 * 3 * E], BF16)
            s_qkvb = sb("s_qkvb", [128, 3 * E], BF16)
        s_wrow = sb("s_wrow", [128, E], BF16)
        s_w8T = sb("s_w8T", [128, 8 * E], BF16)

        s_x = sb("s_x", [128, E], F32)
        s_scr = sb("s_scr", [128, 8], F32)
        s_xnf = sb("s_xnf", [128, E], F32)
        s_xnb = sb("s_xnb", [128, E], BF16)
        s_qi = sb("s_qi", [128, E], I32)
        s_qb = sb("s_qb", [128, E], BF16)
        s_xqf = sb("s_xqf", [128, 8 * 128], BF16)
        s_ffn = sb("s_ffn", [128, E], F32)
        s_mean = sb("s_mean", [128, NB], F32)
        s_rstd = sb("s_rstd", [128, NB], F32)
        s_nmr = sb("s_nmr", [128, NB], F32)
        s_var = sb("s_var", [128, NB], F32)
        s_qfac = sb("s_qfac", [128, NB], F32)
        s_ofac = sb("s_ofac", [128, NB], F32)

        s_xnfm = sb("s_xnfm", [128, 8 * T_OWN], BF16)
        if layer == 0:
            s_cq = sb("s_cq", [128, 8 * T_OWN], BF16)
            s_ckv = sb("s_ckv", [128, 2 * T_OWN], BF16)

        s_ra = sb("s_ra", [128, 256], F32)
        s_rb = sb("s_rb", [128, 256], F32)
        s_qst = sb("s_qst", [128, QW], BF16)
        s_kst = sb("s_kst", [128, QW], BF16)
        s_vst = sb("s_vst", [128, E], BF16)

        pA = ctx.enter_context(nc.psum_tensor("pA", [128, 512], F32))
        pB = ctx.enter_context(nc.psum_tensor("pB", [128, 512], F32))
        pT = ctx.enter_context(nc.psum_tensor("pT", [128, 1024], BF16))

        blk = ctx.enter_context(nc.Block())

        def dma(eng, sem_, out_ap, in_ap, wait=()):
            p.emit(eng, lambda e, o=out_ap, i=in_ap: e.dma_start(out=o, in_=i),
                   wait=wait, inc=[(sem_, 16)])

        dma("sync", ld, s_ident[:], ident)
        dma("sync", ld, s_ones[:], onesf)
        dma("sync", ld, s_tabC[:], tabC)
        dma("sync", ld, s_tabS[:], tabS)
        consts_tok = tok(ld)
        if layer == 0:
            for ic in range(8):
                dma("gpsimd", gld, s_Wdq[:, ic * E:(ic + 1) * E], Wdq[ic * 128:(ic + 1) * 128, :])
                dma("gpsimd", gld, s_Wuq[:, ic * E:(ic + 1) * E], Wuq[ic * 128:(ic + 1) * 128, :])
                dma("gpsimd", gld, s_Wqr[:, ic * E:(ic + 1) * E], Wqr[ic * 128:(ic + 1) * 128, :])
                dma("gpsimd", gld, s_Wkr[:, ic * E:(ic + 1) * E], Wkr[ic * 128:(ic + 1) * 128, :])
                dma("gpsimd", gld, s_Wdkv[:, ic * 256:(ic + 1) * 256], Wdkv[ic * 128:(ic + 1) * 128, :])
            for ic in range(2):
                dma("gpsimd", gld, s_Wuk[:, ic * E:(ic + 1) * E], Wuk[ic * 128:(ic + 1) * 128, :])
                dma("gpsimd", gld, s_Wuv[:, ic * E:(ic + 1) * E], Wuv[ic * 128:(ic + 1) * 128, :])
        w_tok = tok(gld)

        def transpose_rows_to(dest_sb, dest_ocols, oc, extra_wait):
            for ic in range(8):
                p.emit("tensor", lambda e, ic=ic: e.transpose(
                    pT[:, ic * 128:(ic + 1) * 128], s_wrow[:, ic * 128:(ic + 1) * 128], s_ident[:]),
                    wait=(list(extra_wait) + [consts_tok]) if ic == 0 else [],
                    inc=[(pe, 1)] if ic == 7 else [])
            tpe = tok(pe)
            dst = dest_sb[:].rearrange("p (ic o) -> p ic o", ic=8, o=dest_ocols)[:, :, oc * 128:(oc + 1) * 128]
            src = pT[:].rearrange("p (ic t) -> p ic t", ic=8, t=128)
            p.emit("vector", lambda e, dst=dst, src=src: e.tensor_copy(dst, src),
                   wait=[tpe], inc=[(vd, 1)])
            return tok(vd)

        prev = None
        for oc in range(8):
            waits = [prev] if prev else [tok(gld)]
            dma("gpsimd", l8, s_wrow[:], w8[oc * 128:(oc + 1) * 128, :], wait=waits)
            tg = tok(l8)
            if layer == 1:
                p.emit("vector", lambda e: e.tensor_scalar(s_wrow[:], s_wrow[:], -1.0, None, ALU.add),
                       wait=[tg], inc=[(vd, 1)])
                tg = tok(vd)
            prev = transpose_rows_to(s_w8T, E, oc, [tg])
        w8_tok = prev

        if layer == 1:
            for rc in range(24):
                dma("gpsimd", lq1, s_wrow[:], qkv_w[rc * 128:(rc + 1) * 128, :], wait=[prev])
                tg = tok(lq1)
                prev = transpose_rows_to(s_qkvT, 3 * E, rc, [tg])
            qkvT_tok = prev
            pv = tok(vd)
            for cc in range(3):
                dma("sync", lqb, s_ffn[0:1, :], qkv_b[0:1, cc * E:(cc + 1) * E], wait=[pv])
                tb = tok(lqb)
                for c in range(2):
                    p.emit("tensor", lambda e, c=c: e.matmul(
                        pB[:, :], s_ones[:, :], s_ffn[0:1, c * 512:(c + 1) * 512], start=True, stop=True),
                        wait=[tb, pv], inc=[(pe, 1)])
                    p.emit("vector", lambda e, c=c, cc=cc: e.tensor_copy(
                        s_qkvb[:, cc * E + c * 512: cc * E + (c + 1) * 512], pB[:, :]),
                        wait=[tok(pe)], inc=[(vd, 1)])
                    pv = tok(vd)
        else:
            qkvT_tok = w8_tok

        pv = tok(vd)
        dma("sync", lfb, s_ffn[0:1, :], fbias, wait=[pv])
        tb = tok(lfb)
        for c in range(2):
            p.emit("tensor", lambda e, c=c: e.matmul(
                pB[:, :], s_ones[:, :], s_ffn[0:1, c * 512:(c + 1) * 512], start=True, stop=True),
                wait=[tb, pv], inc=[(pe, 1)])
            p.emit("vector", lambda e, c=c: e.tensor_copy(s_bias[:, c * 512:(c + 1) * 512], pB[:, :]),
                   wait=[tok(pe)], inc=[(vd, 1)])
            pv = tok(vd)
        dma("sync", lsc, s_sc[0:1, 0:1], wscale if layer == 0 else ascale, wait=[tok(ld)])
        p.emit("tensor", lambda e: e.matmul(pB[:, 0:1], s_ones[:, :], s_sc[0:1, 0:1], start=True, stop=True),
               wait=[tok(lsc), pv], inc=[(pe, 1)])
        p.emit("vector", lambda e: e.tensor_copy(s_sc[:, 0:1], pB[:, 0:1]), wait=[tok(pe)])
        if layer == 1:
            p.emit("vector", lambda e: e.tensor_scalar(s_sc[:, 0:1], s_sc[:, 0:1], 1e-5, None, ALU.max))
            p.emit("vector", lambda e: e.reciprocal(s_sc[:, 1:2], s_sc[:, 0:1]), inc=[(vd, 1)])
        else:
            p.emit("vector", lambda e: e.tensor_scalar(s_sc[:, 1:2], s_sc[:, 0:1], 1.0 / 127.0, None, ALU.mult),
                   inc=[(vd, 1)])
        const_tok = tok(vd)

        for j in range(NB):
            xs = s_x[:]
            p.emit("sync", lambda e, j=j, xs=xs: e.dma_start(out=xs, in_=x[j * 128:(j + 1) * 128, :]),
                   wait=[tok(vd)] if j >= 1 else [tok(ld)], inc=[(lx, 16)])
            tx = tok(lx)
            p.emit("scalar", lambda e, xs=xs: e.activation(s_xnf[:], xs, AF.Square),
                   wait=[tx, tok(vd)], inc=[(sd, 1)])
            tsq = tok(sd)
            mj, rj, bj, vjc = s_mean[:, j:j + 1], s_rstd[:, j:j + 1], s_nmr[:, j:j + 1], s_var[:, j:j + 1]
            p.emit("vector", lambda e, xs=xs, mj=mj: e.reduce_sum(mj, xs, axis=AX.X), wait=[tx])
            p.emit("vector", lambda e, vjc=vjc: e.reduce_sum(vjc, s_xnf[:], axis=AX.X), wait=[tsq])
            p.emit("vector", lambda e, mj=mj: e.tensor_scalar(mj, mj, 1.0 / E, None, ALU.mult))
            p.emit("vector", lambda e, mj=mj: e.tensor_tensor(s_scr[:, 0:1], mj, mj, ALU.mult))
            p.emit("vector", lambda e, vjc=vjc: e.tensor_scalar(vjc, vjc, 1.0 / E, None, ALU.mult))
            p.emit("vector", lambda e, vjc=vjc: e.tensor_tensor(vjc, vjc, s_scr[:, 0:1], ALU.subtract))
            p.emit("vector", lambda e, vjc=vjc: e.tensor_scalar(s_scr[:, 2:3], vjc, 1e-5, None, ALU.add),
                   inc=[(vd, 1)])
            p.emit("scalar", lambda e: e.activation(s_scr[:, 0:1], s_scr[:, 2:3], AF.Sqrt),
                   wait=[tok(vd)], inc=[(sd, 1)])
            p.emit("vector", lambda e, rj=rj: e.reciprocal(rj, s_scr[:, 0:1]), wait=[tok(sd)])
            p.emit("vector", lambda e, mj=mj, rj=rj, bj=bj: e.scalar_tensor_tensor(
                bj, mj, -1.0, rj, ALU.mult, ALU.mult))
            p.emit("vector", lambda e, xs=xs, rj=rj, bj=bj: e.tensor_scalar(
                s_xnf[:], xs, rj, bj, ALU.mult, ALU.add))
            p.emit("vector", lambda e: e.tensor_copy(s_xnb[:], s_xnf[:]), inc=[(vd, 1)])
            txn = tok(vd)
            if layer == 0:
                p.emit("vector", lambda e, vjc=vjc, rj=rj: e.tensor_tensor(s_scr[:, 1:2], vjc, rj, ALU.mult))
                p.emit("vector", lambda e, rj=rj: e.tensor_tensor(s_scr[:, 1:2], s_scr[:, 1:2], rj, ALU.mult))
                p.emit("vector", lambda e: e.tensor_scalar(s_scr[:, 5:6], s_scr[:, 1:2], 1e-6, None, ALU.add),
                       inc=[(vd, 1)])
                p.emit("scalar", lambda e: e.activation(s_scr[:, 2:3], s_scr[:, 5:6], AF.Sqrt),
                       wait=[tok(vd)], inc=[(sd, 1)])
                p.emit("vector", lambda e: e.reciprocal(s_scr[:, 1:2], s_scr[:, 2:3]), wait=[tok(sd)])
                p.emit("vector", lambda e: e.reduce_max(s_scr[:, 3:4], s_xnf[:], axis=AX.X,
                                                        apply_absolute_value=True))
                p.emit("vector", lambda e: e.tensor_tensor(s_scr[:, 3:4], s_scr[:, 3:4], s_scr[:, 1:2], ALU.mult))
                p.emit("vector", lambda e: e.tensor_scalar(s_scr[:, 3:4], s_scr[:, 3:4], 1e-5, None, ALU.max))
                p.emit("vector", lambda e: e.reciprocal(s_scr[:, 4:5], s_scr[:, 3:4]))
                p.emit("vector", lambda e: e.tensor_scalar(s_scr[:, 4:5], s_scr[:, 4:5], 127.0, None, ALU.mult))
                qf, of = s_qfac[:, j:j + 1], s_ofac[:, j:j + 1]
                p.emit("vector", lambda e, qf=qf: e.tensor_tensor(qf, s_scr[:, 1:2], s_scr[:, 4:5], ALU.mult))
                p.emit("vector", lambda e, of=of: e.tensor_tensor(of, s_scr[:, 3:4], s_sc[:, 1:2], ALU.mult),
                       wait=[const_tok])
                lo, hi = -128.0, 127.0
            else:
                qf, of = s_sc[:, 1:2], s_sc[:, 0:1]
                lo, hi = -127.0, 127.0
            p.emit("vector", lambda e, qf=qf: e.tensor_scalar(s_xnf[:], s_xnf[:], qf, None, ALU.mult),
                   wait=[const_tok] if layer == 1 else [])
            p.emit("vector", lambda e: e.tensor_scalar(s_xnf[:], s_xnf[:], MAGIC, -MAGIC, ALU.add, ALU.add))
            p.emit("vector", lambda e, lo=lo, hi=hi: e.tensor_scalar(s_xnf[:], s_xnf[:], lo, hi, ALU.max, ALU.min))
            p.emit("vector", lambda e: e.tensor_copy(s_qi[:], s_xnf[:]))
            p.emit("vector", lambda e: e.tensor_copy(s_qb[:], s_qi[:]), inc=[(vd, 1)])
            tq = tok(vd)
            for ic in range(8):
                p.emit("tensor", lambda e, ic=ic: e.transpose(
                    pT[:, ic * 128:(ic + 1) * 128], s_xnb[:, ic * 128:(ic + 1) * 128], s_ident[:]),
                    wait=[txn, tok(vd)] if ic == 0 else [], inc=[(pe, 1)] if ic == 7 else [])
            tpe = tok(pe)
            p.emit("vector", lambda e, j=j: e.tensor_copy(
                s_xnfm[:].rearrange("p (ic t) -> p ic t", ic=8, t=T_OWN)[:, :, j * 128:(j + 1) * 128],
                pT[:].rearrange("p (ic t) -> p ic t", ic=8, t=128)), wait=[tpe], inc=[(vd, 1)])
            for ic in range(8):
                p.emit("tensor", lambda e, ic=ic: e.transpose(
                    pT[:, ic * 128:(ic + 1) * 128], s_qb[:, ic * 128:(ic + 1) * 128], s_ident[:]),
                    wait=[tq, tok(vd)] if ic == 0 else [], inc=[(pe, 1)] if ic == 7 else [])
            tpe = tok(pe)
            p.emit("vector", lambda e: e.tensor_copy(
                s_xqf[:].rearrange("p (ic t) -> p ic t", ic=8, t=128),
                pT[:].rearrange("p (ic t) -> p ic t", ic=8, t=128)), wait=[tpe], inc=[(vd, 1)])
            txq = tok(vd)
            for oc in range(2):
                pban = pA if oc == 0 else pB
                for ic in range(8):
                    p.emit("tensor", lambda e, ic=ic, oc=oc, pban=pban: e.matmul(
                        pban[:, :], s_xqf[:, ic * 128:(ic + 1) * 128],
                        s_w8T[:, ic * E + oc * 512: ic * E + (oc + 1) * 512],
                        start=(ic == 0), stop=(ic == 7)),
                        wait=[txq, w8_tok, tok(vd)] if ic == 0 else [],
                        inc=[(pe, 1)] if ic == 7 else [])
                tpe = tok(pe)
                fs = s_ffn[:, oc * 512:(oc + 1) * 512]
                war_ffn = [tok(st_f)] if (j >= 1 and oc == 0) else []
                p.emit("vector", lambda e, fs=fs, of=of, oc=oc, pban=pban: e.scalar_tensor_tensor(
                    fs, pban[:, :], of, s_bias[:, oc * 512:(oc + 1) * 512], ALU.mult, ALU.add),
                    wait=[tpe, const_tok] + war_ffn, inc=[(vd, 1)])
            tff = tok(vd)
            p.emit("gpsimd", lambda e, j=j: e.dma_start(
                out=ffn_out[j * 128:(j + 1) * 128, :], in_=s_ffn[:]),
                wait=[tff], inc=[(st_f, 16)])
        xnfm_tok = tok(vd)

        def fm_proj(w_sb, w_stride, n_oc, dest_sb, wait_first):
            for oc in range(n_oc):
                for tc in range(T_OWN // 512):
                    pban = pA if (oc * 4 + tc) % 2 == 0 else pB
                    for ic in range(8):
                        p.emit("tensor", lambda e, ic=ic, oc=oc, tc=tc, pban=pban, w_sb=w_sb, w_stride=w_stride: e.matmul(
                            pban[:, :],
                            w_sb[:, ic * w_stride + oc * 128: ic * w_stride + (oc + 1) * 128],
                            s_xnfm[:, ic * T_OWN + tc * 512: ic * T_OWN + (tc + 1) * 512],
                            start=(ic == 0), stop=(ic == 7)),
                            wait=(wait_first + [tok(vd)]) if ic == 0 else [],
                            inc=[(pe, 1)] if ic == 7 else [])
                    tpe = tok(pe)
                    p.emit("vector", lambda e, oc=oc, tc=tc, pban=pban, dest_sb=dest_sb: e.tensor_copy(
                        dest_sb[:, oc * T_OWN + tc * 512: oc * T_OWN + (tc + 1) * 512], pban[:, :]),
                        wait=[tpe], inc=[(vd, 1)])

        if layer == 0:
            fm_proj(s_Wdq, E, 8, s_cq, [xnfm_tok, w_tok])
            cq_tok = tok(vd)
            fm_proj(s_Wdkv, 256, 2, s_ckv, [xnfm_tok, w_tok])
            ckv_tok = tok(vd)

        def tm_proj(act_sb, n_ic, w_sb, w_stride, w_col0, j, pban, wait_first):
            for ic in range(n_ic):
                p.emit("tensor", lambda e, ic=ic: e.matmul(
                    pban[:, :],
                    act_sb[:, ic * T_OWN + j * 128: ic * T_OWN + (j + 1) * 128],
                    w_sb[:, ic * w_stride + w_col0: ic * w_stride + w_col0 + 512],
                    start=(ic == 0), stop=(ic == n_ic - 1)),
                    wait=wait_first if ic == 0 else [], inc=[(pe, 1)] if ic == n_ic - 1 else [])
            return tok(pe)

        def rope_emit(pban, j, dest3d, dest_off, wait_psum):
            x1 = pban[:].rearrange("p (h d) -> p h d", h=8, d=64)[:, :, 0:64:2]
            x2 = pban[:].rearrange("p (h d) -> p h d", h=8, d=64)[:, :, 1:64:2]
            C = s_tabC[:, j * 32:(j + 1) * 32][:, None, :].broadcast_to((128, 8, 32))
            Sx = s_tabS[:, j * 32:(j + 1) * 32][:, None, :].broadcast_to((128, 8, 32))
            ra = s_ra[:].rearrange("p (h d) -> p h d", h=8, d=32)
            rb = s_rb[:].rearrange("p (h d) -> p h d", h=8, d=32)
            d1 = dest3d[:, :, dest_off:dest_off + 32]
            d2 = dest3d[:, :, dest_off + 32:dest_off + 64]
            p.emit("vector", lambda e: e.tensor_tensor(ra, x1, C, ALU.mult), wait=wait_psum)
            p.emit("vector", lambda e: e.tensor_tensor(rb, x2, Sx, ALU.mult))
            p.emit("vector", lambda e: e.tensor_tensor(d1, ra, rb, ALU.subtract))
            p.emit("vector", lambda e: e.tensor_tensor(ra, x1, Sx, ALU.mult))
            p.emit("vector", lambda e: e.tensor_tensor(rb, x2, C, ALU.mult))
            p.emit("vector", lambda e: e.tensor_tensor(d2, ra, rb, ALU.add), inc=[(vd, 1)])
            return tok(vd)

        for j in range(NB):
            qst = s_qst[:]
            kst = s_kst[:]
            vst = s_vst[:]
            war = [tok(st)] if j >= 1 else []
            if layer == 0:
                q3 = qst.rearrange("p (h d) -> p h d", h=16, d=128)
                k3 = kst.rearrange("p (h d) -> p h d", h=16, d=128)
                for oc in range(2):
                    hsl = slice(oc * 8, (oc + 1) * 8)
                    t = tm_proj(s_cq, 8, s_Wuq, E, oc * 512, j, pA, [cq_tok, w_tok, tok(vd)] + war)
                    p.emit("vector", lambda e, q3=q3, hsl=hsl: e.tensor_scalar(
                        q3[:, hsl, 0:64], pA[:].rearrange("p (h d) -> p h d", h=8, d=64),
                        SCALE_SQ[0], None, ALU.mult),
                        wait=[t], inc=[(vd, 1)])
                    t = tm_proj(s_cq, 8, s_Wqr, E, oc * 512, j, pB, [tok(vd)])
                    rope_emit(pB, j, q3[:, hsl, :], 64, [t])
                    t = tm_proj(s_ckv, 2, s_Wuk, E, oc * 512, j, pA, [tok(vd)])
                    p.emit("vector", lambda e, k3=k3, hsl=hsl: e.tensor_scalar(
                        k3[:, hsl, 0:64], pA[:].rearrange("p (h d) -> p h d", h=8, d=64),
                        SCALE_SQ[0], None, ALU.mult),
                        wait=[t], inc=[(vd, 1)])
                    t = tm_proj(s_xnfm, 8, s_Wkr, E, oc * 512, j, pB, [tok(vd)])
                    rope_emit(pB, j, k3[:, hsl, :], 64, [t])
                    t = tm_proj(s_ckv, 2, s_Wuv, E, oc * 512, j, pA, [tok(vd)])
                    p.emit("vector", lambda e, vst=vst, oc=oc: e.tensor_copy(
                        vst[:, oc * 512:(oc + 1) * 512], pA[:, :]), wait=[t], inc=[(vd, 1)])
            else:
                q3 = qst.rearrange("p (h d) -> p h d", h=16, d=64)
                k3 = kst.rearrange("p (h d) -> p h d", h=16, d=64)
                for oc in range(2):
                    hsl = slice(oc * 8, (oc + 1) * 8)
                    t = tm_proj(s_xnfm, 8, s_qkvT, 3 * E, 0 * E + oc * 512, j, pA,
                                [xnfm_tok, qkvT_tok, tok(vd)] + war)
                    p.emit("vector", lambda e, oc=oc: e.tensor_tensor(
                        pA[:, :], pA[:, :], s_qkvb[:, oc * 512:(oc + 1) * 512], ALU.add), wait=[t])
                    rope_emit(pA, j, q3[:, hsl, :], 0, [])
                    t = tm_proj(s_xnfm, 8, s_qkvT, 3 * E, 1 * E + oc * 512, j, pB, [tok(vd)])
                    p.emit("vector", lambda e, oc=oc: e.tensor_tensor(
                        pB[:, :], pB[:, :], s_qkvb[:, E + oc * 512: E + (oc + 1) * 512], ALU.add), wait=[t])
                    rope_emit(pB, j, k3[:, hsl, :], 0, [])
                    t = tm_proj(s_xnfm, 8, s_qkvT, 3 * E, 2 * E + oc * 512, j, pA, [tok(vd)])
                    p.emit("vector", lambda e, vst=vst, oc=oc: e.scalar_tensor_tensor(
                        vst[:, oc * 512:(oc + 1) * 512], pA[:, :], 1.0,
                        s_qkvb[:, 2 * E + oc * 512: 2 * E + (oc + 1) * 512], ALU.mult, ALU.add),
                        wait=[t], inc=[(vd, 1)])
            tdone = tok(vd)
            p.emit("gpsimd", lambda e, j=j, qst=qst: e.dma_start(out=q_out[j * 128:(j + 1) * 128, :], in_=qst),
                   wait=[tdone], inc=[(st, 16)])
            p.emit("gpsimd", lambda e, j=j, kst=kst: e.dma_start(out=k_out[j * 128:(j + 1) * 128, :], in_=kst),
                   inc=[(st, 16)])
            p.emit("gpsimd", lambda e, j=j, vst=vst: e.dma_start(out=v_out[j * 128:(j + 1) * 128, :], in_=vst),
                   inc=[(st, 16)])

        p.emit("sync", lambda e: e.nop(), wait=[tok(st), tok(st_f)])
        p.run(blk)
    return nc


_IDENT = np.eye(128, dtype=BF)
_ONESF = np.ones((1, 128), dtype=np.float32)


def run_pre(layer, x_flat, weights):
    nc = build_pre(layer)
    in_maps = []
    for c in range(N_CORES):
        pos = np.arange((c % 2) * T_OWN, (c % 2 + 1) * T_OWN)
        Ct, St = (_rope_tables_l0(pos) if layer == 0 else _rope_tables_l1(pos))
        ss = SCALE_SQ[layer]
        m = dict(x=np.ascontiguousarray(x_flat[c * T_OWN:(c + 1) * T_OWN]),
                 ident=_IDENT, onesf=_ONESF, tabC=_blocked(Ct * ss), tabS=_blocked(St * ss))
        m.update(weights)
        in_maps.append(m)
    res = run_bass_kernel_spmd(nc, in_maps, core_ids=list(range(N_CORES)))
    return {k: np.concatenate([res.results[c][k] for c in range(N_CORES)], axis=0)
            for k in ("q_out", "k_out", "v_out", "ffn_out")}




def build_att(layer):
    nc = bass.Bass("TRN2", target_bir_lowering=False)
    dt = lambda n, sh, d, k: nc.dram_tensor(n, sh, d, kind=k).ap()
    NU = 8 if layer == 0 else 4
    NH_U = 1 if layer == 0 else 2
    qh = dt("qh", [NU, S, 128], BF16, "ExternalInput")
    kh = dt("kh", [NU, S, 128], BF16, "ExternalInput")
    vh = dt("vh", [8, S, 64], BF16, "ExternalInput")
    onesf = dt("onesf", [1, 128], F32, "ExternalInput")
    ctx_out = dt("ctx", [8, 64, S], BF16, "ExternalOutput")
    if layer == 0:
        Mq = dt("Mq", [128, 128], F32, "ExternalInput")
        MT = dt("MT", [128, 128], F32, "ExternalInput")
        identf = dt("identf", [128, 128], F32, "ExternalInput")

    NKB = S // 128
    NQC = S // 512

    with ExitStack() as ctx:
        sb = lambda nm, sh, d=F32: ctx.enter_context(nc.sbuf_tensor(nm, sh, d))
        sem = lambda nm: Sem(ctx.enter_context(nc.semaphore(nm)))
        p = Prog(nc, ctx)
        vd = p.chain_sem["vector"]
        sd = p.chain_sem["scalar"]
        ld = sem("ld")
        lu0 = sem("lu0")
        lu1 = sem("lu1")
        pe = sem("pe")
        st = sem("st")

        s_ones = sb("s_ones", [1, 128], F32)
        s_q = sb("s_q", [128, 2 * S], BF16)
        s_k = sb("s_k", [128, 2 * S], BF16)
        VW = NKB * 65 * NH_U
        s_v = sb("s_v", [128, 2 * VW], BF16)
        s_pT = sb("s_pT", [128, 2 * 512], BF16)
        s_g = sb("s_g", [64, 512], F32)
        s_ls = sb("s_ls", [1, 2 * 512], F32)
        if layer == 0:
            s_Mq = sb("s_Mq", [128, 128], F32)
            s_MT = sb("s_MT", [128, 128], F32)
            s_idf = sb("s_idf", [128, 128], F32)
            s_nm = sb("s_nm", [128, NKB], F32)
            s_nmrow = sb("s_nmrow", [1, S], F32)
            s_tmp = sb("s_tmp", [128, 2], F32)
        s_co = sb("s_co", [64, 2 * 512], BF16)

        psA = [ctx.enter_context(nc.psum_tensor(f"psA{i}", [128, 512], F32)) for i in range(2)]
        psB = [ctx.enter_context(nc.psum_tensor(f"psB{i}", [128, 512], F32)) for i in range(2)]
        pC = ctx.enter_context(nc.psum_tensor("pC", [128, 512], F32))
        pG = ctx.enter_context(nc.psum_tensor("pG", [128, 512], F32))
        pN = ctx.enter_context(nc.psum_tensor("pN", [128, 512], F32))

        blk = ctx.enter_context(nc.Block())

        p.emit("sync", lambda e: e.dma_start(out=s_ones[:], in_=onesf), inc=[(ld, 16)])
        if layer == 0:
            p.emit("sync", lambda e: e.dma_start(out=s_Mq[:], in_=Mq), inc=[(ld, 16)])
            p.emit("sync", lambda e: e.dma_start(out=s_MT[:], in_=MT), inc=[(ld, 16)])
            p.emit("sync", lambda e: e.dma_start(out=s_idf[:], in_=identf), inc=[(ld, 16)])
        consts_tok = tok(ld)

        pe_end = {}
        for u in range(NU):
            ub = u % 2
            lu = lu0 if ub == 0 else lu1
            war = [pe_end[u - 2]] if u - 2 in pe_end else []
            p.emit("sync", lambda e, u=u, ub=ub: e.dma_start(
                out=s_q[:, ub * S:(ub + 1) * S], in_=qh[u], transpose=True),
                wait=war + [tok(ld)], inc=[(lu, 16)])
            p.emit("sync", lambda e, u=u, ub=ub: e.dma_start(
                out=s_k[:, ub * S:(ub + 1) * S], in_=kh[u], transpose=True),
                inc=[(lu, 16)])
            for hh in range(NH_U):
                v3 = s_v[:, ub * VW + hh * NKB * 65: ub * VW + (hh + 1) * NKB * 65].rearrange(
                    "p (kb c) -> p kb c", kb=NKB, c=65)
                p.emit("sync", lambda e, u=u, hh=hh, v3=v3: e.dma_start(
                    out=v3[:, :, 0:64],
                    in_=vh[u * NH_U + hh].rearrange("(kb p) c -> p kb c", kb=NKB, p=128)),
                    inc=[(lu, 16)])
                p.emit("vector", lambda e, v3=v3: e.memset(v3[:, :, 64:65], 1.0), wait=war)
            t_load = tok(lu)
            t_vload = tok(vd)

            for hh in range(NH_U):
                hb = 64 * hh
                KD = 128 if layer == 0 else 64
                v3c = s_v[:, ub * VW + hh * NKB * 65: ub * VW + (hh + 1) * NKB * 65]

                if layer == 0:
                    for qb in range(NKB):
                        lq = s_q[:, ub * S + qb * 128: ub * S + (qb + 1) * 128]
                        nk = (qb + 1) * 128
                        nch = (nk + 511) // 512
                        for kc in range(nch):
                            w = min(512, nk - kc * 512)
                            lk = s_k[:, ub * S + kc * 512: ub * S + kc * 512 + w]
                            ps = psA[kc % 2]
                            p.emit("tensor", lambda e, lq=lq, lk=lk, w=w, ps=ps: e.matmul(
                                ps[:, 0:w], lq, lk, start=True, stop=True),
                                wait=[t_load, tok(vd)], inc=[(pe, 1)])
                            tmm = tok(pe)
                            if kc * 512 <= qb * 128 < kc * 512 + w:
                                off = qb * 128 - kc * 512
                                p.emit("vector", lambda e, ps=ps, off=off: e.tensor_tensor(
                                    ps[:, off:off + 128], ps[:, off:off + 128], s_Mq[:], ALU.add),
                                    wait=[tmm, consts_tok])
                            if kc == 0:
                                p.emit("vector", lambda e, ps=ps, qb=qb, w=w: e.reduce_max(
                                    s_nm[:, qb:qb + 1], ps[:, 0:w], axis=AX.X, negate=True),
                                    wait=[tmm])
                            else:
                                p.emit("vector", lambda e, ps=ps, w=w: e.reduce_max(
                                    s_tmp[:, 0:1], ps[:, 0:w], axis=AX.X, negate=True),
                                    wait=[tmm])
                                p.emit("vector", lambda e, qb=qb: e.tensor_tensor(
                                    s_nm[:, qb:qb + 1], s_nm[:, qb:qb + 1], s_tmp[:, 0:1], ALU.min))
                        p.emit("tensor", lambda e, qb=qb: e.transpose(
                            pN[0:1, 0:128], s_nm[:, qb:qb + 1], s_idf[:]),
                            wait=[tok(vd)], inc=[(pe, 1)])
                        p.emit("vector", lambda e, qb=qb: e.tensor_copy(
                            s_nmrow[0:1, qb * 128:(qb + 1) * 128], pN[0:1, 0:128]),
                            wait=[tok(pe)])

                for qc in range(NQC):
                    nkb2 = 4 * (qc + 1) if layer == 0 else NKB
                    lq = s_q[hb:hb + KD, ub * S + qc * 512: ub * S + (qc + 1) * 512]
                    for kb in range(nkb2):
                        ps = psB[kb % 2]
                        lk = s_k[hb:hb + KD, ub * S + kb * 128: ub * S + (kb + 1) * 128]
                        p.emit("tensor", lambda e, lk=lk, lq=lq, ps=ps: e.matmul(
                            ps[:, :], lk, lq, start=True, stop=(layer == 1)),
                            wait=[t_load, tok(vd), tok(sd)], inc=[(pe, 1)])
                        if layer == 0:
                            p.emit("tensor", lambda e, qc=qc, ps=ps: e.matmul(
                                ps[:, :], s_ones[:, :], s_nmrow[0:1, qc * 512:(qc + 1) * 512],
                                start=False, stop=True),
                                inc=[(pe, 1)])
                        tmm = tok(pe)
                        mask_wait = [tmm]
                        if layer == 0:
                            if 4 * qc <= kb < 4 * qc + 4:
                                off = (kb - 4 * qc) * 128
                                p.emit("vector", lambda e, ps=ps, off=off: e.tensor_tensor(
                                    ps[:, off:off + 128], ps[:, off:off + 128], s_MT[:], ALU.add),
                                    wait=[tmm, consts_tok])
                                mask_wait = [tok(vd)]
                            nfull = min(kb, 4 * qc + 4) - 4 * qc
                            if nfull > 0:
                                p.emit("vector", lambda e, ps=ps, nfull=nfull: e.tensor_scalar(
                                    ps[:, 0:nfull * 128], ps[:, 0:nfull * 128], -1e9, None, ALU.add),
                                    wait=[tmm, consts_tok])
                                mask_wait = [tok(vd)]
                        pt = s_pT[:, (kb % 2) * 512:(kb % 2 + 1) * 512]
                        p.emit("scalar", lambda e, ps=ps, pt=pt: e.activation(pt, ps[:, :], AF.Exp),
                               wait=mask_wait + [tok(pe)], inc=[(sd, 1)])
                        texp = tok(sd)
                        pv_war = [tok(vd)] if kb == 0 else []
                        p.emit("tensor", lambda e, kb=kb, nkb2=nkb2, pt=pt, v3c=v3c: e.matmul(
                            pC[0:65, :], v3c[:, kb * 65:(kb + 1) * 65], pt,
                            start=(kb == 0), stop=(kb == nkb2 - 1)),
                            wait=[texp, t_vload] + pv_war, inc=[(pe, 1)])
                    tpv = tok(pe)
                    lsl = s_ls[0:1, 0:512]
                    isl = s_ls[0:1, 512:1024]
                    p.emit("scalar", lambda e, lsl=lsl: e.activation(lsl, pC[64:65, :], AF.Ln),
                           wait=[tpv], inc=[(sd, 1)])
                    p.emit("vector", lambda e, lsl=lsl: e.tensor_scalar(lsl, lsl, -1.0, None, ALU.mult),
                           wait=[tok(sd)])
                    p.emit("scalar", lambda e, lsl=lsl, isl=isl: e.activation(isl, lsl, AF.Exp),
                           wait=[tok(vd)], inc=[(sd, 1)])
                    p.emit("tensor", lambda e, isl=isl: e.matmul(
                        pG[0:64, :], s_ones[0:1, 0:64], isl, start=True, stop=True),
                        wait=[tok(sd), tok(vd)], inc=[(pe, 1)])
                    p.emit("vector", lambda e: e.tensor_copy(s_g[:], pG[0:64, :]), wait=[tok(pe)])
                    co = s_co[:, (qc % 2) * 512:(qc % 2 + 1) * 512]
                    p.emit("vector", lambda e, co=co: e.scalar_tensor_tensor(
                        co, pC[0:64, :], 1.0, s_g[:], ALU.mult, ALU.mult),
                        wait=[tok(st)], inc=[(vd, 1)])
                    tco = tok(vd)
                    p.emit("gpsimd", lambda e, u=u, hh=hh, qc=qc, co=co: e.dma_start(
                        out=ctx_out[u * NH_U + hh][:, qc * 512:(qc + 1) * 512], in_=co),
                        wait=[tco], inc=[(st, 16)])
                pe_end[u] = tok(pe)

        p.emit("sync", lambda e: e.nop(), wait=[tok(st)])
        p.run(blk)
    return nc


def run_att(layer, q_tm, k_tm, v_tm):
    nc = build_att(layer)
    HW = 128 if layer == 0 else 64
    qr = np.ascontiguousarray(q_tm.reshape(B, S, H * HW // 128, 128))
    kr = np.ascontiguousarray(k_tm.reshape(B, S, H * HW // 128, 128))
    vr = np.ascontiguousarray(v_tm.reshape(B, S, H, 64))
    NU = 8 if layer == 0 else 4
    in_maps = []
    for c in range(N_CORES):
        qs = np.stack([qr[(c * NU + i) // (H * HW // 128), :, (c * NU + i) % (H * HW // 128), :]
                       for i in range(NU)])
        ks = np.stack([kr[(c * NU + i) // (H * HW // 128), :, (c * NU + i) % (H * HW // 128), :]
                       for i in range(NU)])
        vs = np.stack([vr[(c * 8 + i) // H, :, (c * 8 + i) % H, :] for i in range(8)])
        m = dict(qh=qs, kh=ks, vh=vs, onesf=_ONESF)
        if layer == 0:
            qmask = np.where(np.arange(128)[None, :] <= np.arange(128)[:, None], 0.0, -1e9).astype(np.float32)
            m.update(Mq=qmask, MT=np.ascontiguousarray(qmask.T), identf=np.eye(128, dtype=np.float32))
        in_maps.append(m)
    res = run_bass_kernel_spmd(nc, in_maps, core_ids=list(range(N_CORES)))
    ctx = np.concatenate([res.results[c]["ctx"] for c in range(N_CORES)], axis=0)
    return ctx.reshape(B, H, 64, S)


def build_post(layer):
    nc = bass.Bass("TRN2", target_bir_lowering=False)
    dt = lambda n, sh, d, k: nc.dram_tensor(n, sh, d, kind=k).ap()
    ctxf = dt("ctxf", [E, T_OWN], BF16, "ExternalInput")
    x = dt("x", [T_OWN, E], F32, "ExternalInput")
    ffn = dt("ffn", [T_OWN, E], F32, "ExternalInput")
    Wo = dt("Wo", [E, E], F32, "ExternalInput")
    ident = dt("ident", [128, 128], BF16, "ExternalInput")
    onesf = dt("onesf", [1, 128], F32, "ExternalInput")
    obias = dt("obias", [1, E], F32, "ExternalInput")
    xout = dt("xout", [T_OWN, E], F32, "ExternalOutput")

    with ExitStack() as ctx:
        sb = lambda nm, sh, d=F32: ctx.enter_context(nc.sbuf_tensor(nm, sh, d))
        sem = lambda nm: Sem(ctx.enter_context(nc.semaphore(nm)))
        p = Prog(nc, ctx)
        vd = p.chain_sem["vector"]
        sd = p.chain_sem["scalar"]
        ld = sem("ld")
        gld = sem("gld")
        lctx = sem("lctx")
        lob = sem("lob")
        lx = sem("lx")
        pe = sem("pe")
        st = sem("st")

        s_ident = sb("s_ident", [128, 128], BF16)
        s_ones = sb("s_ones", [1, 128], F32)
        s_ctx = sb("s_ctx", [128, 8 * T_OWN], BF16)
        s_Wo = sb("s_Wo", [128, 8 * E], BF16)
        s_wrow = sb("s_wrow", [128, E], BF16)
        s_x = sb("s_x", [128, E], F32)
        s_ffn = sb("s_ffn", [128, E], F32)
        s_o = sb("s_o", [128, E], F32)
        s_ob = sb("s_ob", [128, E], BF16)

        pA = ctx.enter_context(nc.psum_tensor("pA", [128, 512], F32))
        pB = ctx.enter_context(nc.psum_tensor("pB", [128, 512], F32))
        pT = ctx.enter_context(nc.psum_tensor("pT", [128, 1024], BF16))

        blk = ctx.enter_context(nc.Block())

        def dma(eng, sem_, out_ap, in_ap, wait=()):
            p.emit(eng, lambda e, o=out_ap, i=in_ap: e.dma_start(out=o, in_=i),
                   wait=wait, inc=[(sem_, 16)])

        dma("sync", ld, s_ident[:], ident)
        dma("sync", ld, s_ones[:], onesf)
        consts_tok = tok(ld)
        for ic in range(8):
            dma("sync", lctx, s_ctx[:, ic * T_OWN:(ic + 1) * T_OWN], ctxf[ic * 128:(ic + 1) * 128, :],
                wait=[consts_tok] if ic == 0 else [])
        ctx_tok = tok(lctx)
        if layer == 0:
            for ic in range(8):
                dma("gpsimd", gld, s_Wo[:, ic * E:(ic + 1) * E], Wo[ic * 128:(ic + 1) * 128, :])
            w_tok = tok(gld)
        else:
            prev = None
            for oc in range(8):
                dma("gpsimd", gld, s_wrow[:], Wo[oc * 128:(oc + 1) * 128, :],
                    wait=[prev] if prev else [])
                tg = tok(gld)
                for ic in range(8):
                    p.emit("tensor", lambda e, ic=ic: e.transpose(
                        pT[:, ic * 128:(ic + 1) * 128], s_wrow[:, ic * 128:(ic + 1) * 128], s_ident[:]),
                        wait=[tg, consts_tok] if ic == 0 else [], inc=[(pe, 1)] if ic == 7 else [])
                tpe = tok(pe)
                dst = s_Wo[:].rearrange("p (ic o) -> p ic o", ic=8, o=E)[:, :, oc * 128:(oc + 1) * 128]
                srcp = pT[:].rearrange("p (ic t) -> p ic t", ic=8, t=128)
                p.emit("vector", lambda e, dst=dst, srcp=srcp: e.tensor_copy(dst, srcp),
                       wait=[tpe], inc=[(vd, 1)])
                prev = tok(vd)
            w_tok = prev
        pv = tok(vd)
        dma("sync", lob, s_o[0:1, :], obias, wait=[pv])
        tb = tok(lob)
        for c in range(2):
            p.emit("tensor", lambda e, c=c: e.matmul(
                pB[:, :], s_ones[:, :], s_o[0:1, c * 512:(c + 1) * 512], start=True, stop=True),
                wait=[tb, pv], inc=[(pe, 1)])
            p.emit("vector", lambda e, c=c: e.tensor_copy(s_ob[:, c * 512:(c + 1) * 512], pB[:, :]),
                   wait=[tok(pe)], inc=[(vd, 1)])
            pv = tok(vd)
        bias_tok = tok(vd)

        for j in range(NB):
            p.emit("sync", lambda e, j=j: e.dma_start(out=s_x[:], in_=x[j * 128:(j + 1) * 128, :]),
                   wait=[tok(vd)] if j >= 1 else [tok(ld)], inc=[(lx, 16)])
            p.emit("sync", lambda e, j=j: e.dma_start(out=s_ffn[:], in_=ffn[j * 128:(j + 1) * 128, :]),
                   inc=[(lx, 16)])
            tx = tok(lx)
            for oc in range(2):
                pban = pA if oc == 0 else pB
                for ic in range(8):
                    p.emit("tensor", lambda e, ic=ic, oc=oc, pban=pban, j=j: e.matmul(
                        pban[:, :], s_ctx[:, ic * T_OWN + j * 128: ic * T_OWN + (j + 1) * 128],
                        s_Wo[:, ic * E + oc * 512: ic * E + (oc + 1) * 512],
                        start=(ic == 0), stop=(ic == 7)),
                        wait=[ctx_tok, w_tok, bias_tok, tok(vd)] if ic == 0 else [],
                        inc=[(pe, 1)] if ic == 7 else [])
                tpe = tok(pe)
                osl = s_o[:, oc * 512:(oc + 1) * 512]
                war = [tok(st)] if (j >= 1 and oc == 0) else []
                p.emit("vector", lambda e, osl=osl, oc=oc, pban=pban: e.scalar_tensor_tensor(
                    osl, pban[:, :], 1.0, s_x[:, oc * 512:(oc + 1) * 512], ALU.mult, ALU.add),
                    wait=[tpe, tx] + war)
                p.emit("vector", lambda e, osl=osl, oc=oc: e.tensor_tensor(
                    osl, osl, s_ffn[:, oc * 512:(oc + 1) * 512], ALU.add))
                p.emit("vector", lambda e, osl=osl, oc=oc: e.tensor_tensor(
                    osl, osl, s_ob[:, oc * 512:(oc + 1) * 512], ALU.add), inc=[(vd, 1)])
            to = tok(vd)
            p.emit("gpsimd", lambda e, j=j: e.dma_start(
                out=xout[j * 128:(j + 1) * 128, :], in_=s_o[:]),
                wait=[to], inc=[(st, 16)])

        p.emit("sync", lambda e: e.nop(), wait=[tok(st)])
        p.run(blk)
    return nc


def run_post(layer, ctx_bh, x_flat, ffn_flat, Wo, obias):
    nc = build_post(layer)
    cf = ctx_bh.reshape(B, E, S)
    in_maps = []
    for c in range(N_CORES):
        b = c // 2
        sl = slice((c % 2) * T_OWN, (c % 2 + 1) * T_OWN)
        in_maps.append(dict(
            ctxf=np.ascontiguousarray(cf[b, :, sl]),
            x=np.ascontiguousarray(x_flat[c * T_OWN:(c + 1) * T_OWN]),
            ffn=np.ascontiguousarray(ffn_flat[c * T_OWN:(c + 1) * T_OWN]),
            Wo=Wo, ident=_IDENT, onesf=_ONESF,
            obias=obias.reshape(1, E).astype(np.float32)))
    res = run_bass_kernel_spmd(nc, in_maps, core_ids=list(range(N_CORES)))
    return np.concatenate([res.results[c]["xout"] for c in range(N_CORES)], axis=0)





def build_fused():
    nc = bass.Bass("TRN2", target_bir_lowering=False)
    dt = lambda n, sh, d, k: nc.dram_tensor(n, sh, d, kind=k).ap()

    x_in = dt("x_in", [T_OWN, E], F32, "ExternalInput")
    ident = dt("ident", [128, 128], BF16, "ExternalInput")
    identf = dt("identf", [128, 128], F32, "ExternalInput")
    onesf = dt("onesf", [1, 128], F32, "ExternalInput")
    tabC0 = dt("tabC0", [128, NB * 32], BF16, "ExternalInput")
    tabS0 = dt("tabS0", [128, NB * 32], BF16, "ExternalInput")
    tabC1 = dt("tabC1", [128, NB * 32], BF16, "ExternalInput")
    tabS1 = dt("tabS1", [128, NB * 32], BF16, "ExternalInput")
    MAq = dt("MAq", [128, 128], BF16, "ExternalInput")
    MBq = dt("MBq", [128, 128], BF16, "ExternalInput")
    MAt = dt("MAt", [128, 128], BF16, "ExternalInput")
    MBt = dt("MBt", [128, 128], BF16, "ExternalInput")
    Wdq = dt("Wdq", [E, E], F32, "ExternalInput")
    Wuq = dt("Wuq", [E, E], F32, "ExternalInput")
    Wqr = dt("Wqr", [E, E], F32, "ExternalInput")
    Wdkv = dt("Wdkv", [E, 256], F32, "ExternalInput")
    Wuk = dt("Wuk", [256, E], F32, "ExternalInput")
    Wkr = dt("Wkr", [E, E], F32, "ExternalInput")
    Wuv = dt("Wuv", [256, E], F32, "ExternalInput")
    Wo0 = dt("Wo0", [E, E], F32, "ExternalInput")
    w80 = dt("w80", [E, E], I32, "ExternalInput")
    wscale = dt("wscale", [1, 1], F32, "ExternalInput")
    fbias0 = dt("fbias0", [1, E], F32, "ExternalInput")
    qkv_w = dt("qkv_w", [3 * E, E], F32, "ExternalInput")
    qkv_b = dt("qkv_b", [1, 3 * E], F32, "ExternalInput")
    out_w = dt("out_w", [E, E], F32, "ExternalInput")
    out_b = dt("out_b", [1, E], F32, "ExternalInput")
    w81 = dt("w81", [E, E], I32, "ExternalInput")
    ascale = dt("ascale", [1, 1], F32, "ExternalInput")
    fbias1 = dt("fbias1", [1, E], F32, "ExternalInput")
    x_out = dt("x_out", [T_OWN, E], F32, "ExternalOutput")

    q_scr = dt("q_scr", [T_OWN, 2048], BF16, "Internal")
    cq_scr = dt("cq_scr", [1280, T_OWN], BF16, "Internal")
    cc_in = [dt(f"cc_in{l}", [T_OWN, (2048 if l == 0 else E) + E], BF16, "Internal")
             for l in range(2)]
    cc_out = [dt(f"cc_out{l}", [2 * T_OWN, (2048 if l == 0 else E) + E], BF16, "Internal")
              for l in range(2)]

    with ExitStack() as ctx:
        sb = lambda nm, sh, d=F32: ctx.enter_context(nc.sbuf_tensor(nm, sh, d))
        sem = lambda nm: Sem(ctx.enter_context(nc.semaphore(nm)))
        p = Prog(nc, ctx)
        vd = p.chain_sem["vector"]
        sd = p.chain_sem["scalar"]
        ld = sem("ld")
        lw = sem("lw")
        l8 = sem("l8")
        lcq = sem("lcq")
        lu0 = sem("lu0")
        lu1 = sem("lu1")
        pe = sem("pe")
        st = sem("st")
        stx = sem("stx")
        ccs = [sem("cc0"), sem("cc1")]

        s_ident = sb("s_ident", [128, 128], BF16)
        s_idf = sb("s_idf", [128, 128], F32)
        s_ones = sb("s_ones", [1, 128], F32)
        s_tabC = sb("s_tabC", [128, NB * 32], BF16)
        s_tabS = sb("s_tabS", [128, NB * 32], BF16)
        s_MAq = sb("s_MAq", [128, 128], BF16)
        s_MBq = sb("s_MBq", [128, 128], BF16)
        s_MAt = sb("s_MAt", [128, 128], BF16)
        s_MBt = sb("s_MBt", [128, 128], BF16)
        s_x = sb("s_x", [128, NB * E], F32)
        s_xnf = sb("s_xnf", [128, E], F32)
        s_xnb = sb("s_xnb", [128, E], BF16)
        s_qi = sb("s_qi", [128, 512], I32)
        s_qb = sb("s_qb", [128, E], BF16)
        s_xqf = sb("s_xqf", [128, 8 * 128], BF16)
        s_mean = sb("s_mean", [128, NB], F32)
        s_rstd = sb("s_rstd", [128, NB], F32)
        s_nmr = sb("s_nmr", [128, NB], F32)
        s_var = sb("s_var", [128, NB], F32)
        s_qfac = sb("s_qfac", [128, NB], F32)
        s_ofac = sb("s_ofac", [128, NB], F32)
        s_scr = sb("s_scr", [128, 8], F32)
        s_sc = sb("s_sc", [128, 2], F32)
        s_bias = sb("s_bias", [128, E], BF16)
        s_ra = sb("s_ra", [128, 256], F32)
        s_rb = sb("s_rb", [128, 256], F32)
        s_qst = sb("s_qst", [128, 1024], BF16)
        s_kst = sb("s_kst", [128, 1024], BF16)
        s_vst = sb("s_vst", [128, 512], BF16)
        s_nm = sb("s_nm", [128, NB], F32)
        s_nmrow = sb("s_nmrow", [1, T_OWN], F32)
        s_tmp = sb("s_tmp", [128, 2], F32)
        s_g = sb("s_g", [64, 512], F32)
        s_ls = sb("s_ls", [1, 2 * 512], F32)
        s_wrow = sb("s_wrow", [128, E], BF16)
        s_pT2 = sb("s_pT2", [128, 2 * 1536], BF16)
        s_nmbc = sb("s_nmbc", [128, 512], F32)

        AR = 47 * 1024
        s_ar = sb("s_ar", [128, AR], BF16)

        psB = [ctx.enter_context(nc.psum_tensor(f"psB{i}", [128, 1536], F32)) for i in range(2)]
        pC = ctx.enter_context(nc.psum_tensor("pC", [128, 512], F32))
        pT = ctx.enter_context(nc.psum_tensor("pT", [128, 1024], BF16))
        psA = [psB[0][:, 0:512], psB[0][:, 512:1024]]
        pG = psB[1][:, 0:512]
        pN = psB[1][:, 512:1024]

        blk = ctx.enter_context(nc.Block())

        def dma(eng, sem_, out_ap, in_ap, wait=()):
            p.emit(eng, lambda e, o=out_ap, i=in_ap: e.dma_start(out=o, in_=i),
                   wait=wait, inc=[(sem_, 16)])

        all_dma_sems = [ld, lw, l8, lcq, lu0, lu1, st, stx]

        def barrier():
            toks = [tok(pe), tok(vd), tok(sd)] + [tok(s) for s in all_dma_sems] + \
                   [tok(c) for c in ccs]
            p.emit("tensor", lambda e: e.nop(), wait=[t for t in toks if t[0] is not pe],
                   inc=[(pe, 1)])
            bpe = tok(pe)
            p.emit("vector", lambda e: e.engine_nop(), wait=[t for t in toks if t[0] is not vd])
            bvd = tok(vd)
            p.emit("scalar", lambda e: e.nop(), wait=[t for t in toks if t[0] is not sd])
            bsd = tok(sd)
            p.emit("sync", lambda e: e.nop(), wait=toks)
            p.emit("gpsimd", lambda e: e.nop(), wait=toks)
            return [bpe, bvd, bsd]

        dma("sync", ld, s_ident[:], ident)
        dma("sync", ld, s_idf[:], identf)
        dma("sync", ld, s_ones[:], onesf)
        dma("sync", ld, s_MAq[:], MAq)
        dma("sync", ld, s_MBq[:], MBq)
        dma("sync", ld, s_MAt[:], MAt)
        dma("sync", ld, s_MBt[:], MBt)
        for j in range(NB):
            dma("sync", ld, s_x[:, j * E:(j + 1) * E], x_in[j * 128:(j + 1) * 128, :])
        x_tok = tok(ld)
        consts_tok = x_tok

        nsem = [0]
        def bcast_row(dst_sb, src_row_dram, width, wait_pre):
            pv = tok(vd)
            nsem[0] += 1
            lsem = sem(f"lb{nsem[0]}")
            all_dma_sems.append(lsem)
            dma("sync", lsem, s_xnf[0:1, 0:width], src_row_dram, wait=[pv] + list(wait_pre))
            tb = tok(lsem)
            for c in range(width // 512):
                p.emit("tensor", lambda e, c=c: e.matmul(
                    pG, s_ones[:, :], s_xnf[0:1, c * 512:(c + 1) * 512],
                    start=True, stop=True), wait=[tb, tok(vd)], inc=[(pe, 1)])
                p.emit("vector", lambda e, c=c, dst_sb=dst_sb: e.tensor_copy(
                    dst_sb[:, c * 512:(c + 1) * 512], pG), wait=[tok(pe)])
            return tok(vd)

        def transpose_rows_to(dest_ap, dest_ocols, oc, extra_wait):
            for ic in range(8):
                p.emit("tensor", lambda e, ic=ic: e.transpose(
                    pT[:, ic * 128:(ic + 1) * 128], s_wrow[:, ic * 128:(ic + 1) * 128], s_ident[:]),
                    wait=(list(extra_wait) + [consts_tok, tok(vd)]) if ic == 0 else [],
                    inc=[(pe, 1)] if ic == 7 else [])
            tpe = tok(pe)
            dst = dest_ap.rearrange("p (ic o) -> p ic o", ic=8, o=dest_ocols)[:, :, oc * 128:(oc + 1) * 128]
            srcp = pT[:].rearrange("p (ic t) -> p ic t", ic=8, t=128)
            p.emit("vector", lambda e, dst=dst, srcp=srcp: e.tensor_copy(dst, srcp),
                   wait=[tpe], inc=[(vd, 1)])
            return tok(vd)

        for layer in range(2):
            KW = 2048 if layer == 0 else E
            w8 = w80 if layer == 0 else w81
            tabC = tabC0 if layer == 0 else tabC1
            tabS = tabS0 if layer == 0 else tabS1

            a_w8T = 0
            a_xnfm = 8 * 1024
            a_cq = a_xnfm + NB * 1024
            a_w = a_cq + 10 * 512
            a_k = a_xnfm
            a_q = a_k + 2 * 4096
            a_v = a_q + 2 * 2048
            a_ctx = a_v + 8384
            a_wo = a_w8T

            bar = barrier() if layer == 1 else []
            dma("sync", ld, s_tabC[:], tabC, wait=[tok(ld)] + bar)
            dma("sync", ld, s_tabS[:], tabS)
            tab_tok = tok(ld)
            bias_tok = bcast_row(s_bias, fbias0 if layer == 0 else fbias1, E, bar)
            nsem[0] += 1
            lscs = sem(f"lsc{nsem[0]}")
            all_dma_sems.append(lscs)
            dma("sync", lscs, s_sc[0:1, 0:1], wscale if layer == 0 else ascale, wait=[tok(ld)])
            p.emit("tensor", lambda e: e.matmul(pG[:, 0:1], s_ones[:, :], s_sc[0:1, 0:1],
                                                start=True, stop=True),
                   wait=[tok(lscs), tok(vd)], inc=[(pe, 1)])
            p.emit("vector", lambda e: e.tensor_copy(s_sc[:, 0:1], pG[:, 0:1]), wait=[tok(pe)])
            if layer == 1:
                p.emit("vector", lambda e: e.tensor_scalar(s_sc[:, 0:1], s_sc[:, 0:1], 1e-5, None, ALU.max))
                p.emit("vector", lambda e: e.reciprocal(s_sc[:, 1:2], s_sc[:, 0:1]), inc=[(vd, 1)])
            else:
                p.emit("vector", lambda e: e.tensor_scalar(s_sc[:, 1:2], s_sc[:, 0:1], 1.0 / 127.0,
                                                           None, ALU.mult), inc=[(vd, 1)])
            const_tok = tok(vd)

            prev = None
            for oc in range(8):
                waits = [prev] if prev else ([tok(l8)] + bar)
                dma("gpsimd", l8, s_wrow[:], w8[oc * 128:(oc + 1) * 128, :], wait=waits)
                tg = tok(l8)
                if layer == 1:
                    p.emit("vector", lambda e: e.tensor_scalar(s_wrow[:], s_wrow[:], -1.0, None, ALU.add),
                           wait=[tg], inc=[(vd, 1)])
                    tg = tok(vd)
                prev = transpose_rows_to(s_ar[:, a_w8T:a_w8T + 8 * E], E, oc, [tg])
            w8_tok = prev

            last_xnf = None
            last_pTc = None
            last_stt = [None, None]
            last_xnbT = None
            last_qbT = None
            pend_ffn = None
            last_ffn_pe = None
            for j in range(NB):
                xs = s_x[:, j * E:(j + 1) * E]
                p.emit("scalar", lambda e, xs=xs: e.activation(s_xnf[:], xs, AF.Square),
                       wait=[x_tok] + ([last_xnf] if last_xnf else [bias_tok, const_tok] + bar),
                       inc=[(sd, 1)])
                tsq = tok(sd)
                mj, rj, bj, vjc = s_mean[:, j:j + 1], s_rstd[:, j:j + 1], s_nmr[:, j:j + 1], s_var[:, j:j + 1]
                p.emit("vector", lambda e, xs=xs, mj=mj: e.reduce_sum(mj, xs, axis=AX.X), wait=[x_tok])
                p.emit("vector", lambda e, vjc=vjc: e.reduce_sum(vjc, s_xnf[:], axis=AX.X), wait=[tsq])
                p.emit("vector", lambda e, mj=mj: e.tensor_scalar(mj, mj, 1.0 / E, None, ALU.mult))
                p.emit("vector", lambda e, mj=mj: e.tensor_tensor(s_scr[:, 0:1], mj, mj, ALU.mult))
                p.emit("vector", lambda e, vjc=vjc: e.tensor_scalar(vjc, vjc, 1.0 / E, None, ALU.mult))
                p.emit("vector", lambda e, vjc=vjc: e.tensor_tensor(vjc, vjc, s_scr[:, 0:1], ALU.subtract))
                p.emit("vector", lambda e, vjc=vjc: e.tensor_scalar(s_scr[:, 2:3], vjc, 1e-5, None, ALU.add),
                       inc=[(vd, 1)])
                p.emit("scalar", lambda e: e.activation(s_scr[:, 0:1], s_scr[:, 2:3], AF.Sqrt),
                       wait=[tok(vd)], inc=[(sd, 1)])
                p.emit("vector", lambda e, rj=rj: e.reciprocal(rj, s_scr[:, 0:1]), wait=[tok(sd)])
                p.emit("vector", lambda e, mj=mj, rj=rj, bj=bj: e.scalar_tensor_tensor(
                    bj, mj, -1.0, rj, ALU.mult, ALU.mult))
                p.emit("vector", lambda e, xs=xs, rj=rj, bj=bj: e.tensor_scalar(
                    s_xnf[:], xs, rj, bj, ALU.mult, ALU.add))
                p.emit("vector", lambda e: e.tensor_copy(s_xnb[:], s_xnf[:]),
                       wait=[last_xnbT] if last_xnbT else [], inc=[(vd, 1)])
                txn = tok(vd)
                if layer == 0:
                    p.emit("vector", lambda e, vjc=vjc, rj=rj: e.tensor_tensor(s_scr[:, 1:2], vjc, rj, ALU.mult))
                    p.emit("vector", lambda e, rj=rj: e.tensor_tensor(s_scr[:, 1:2], s_scr[:, 1:2], rj, ALU.mult))
                    p.emit("vector", lambda e: e.tensor_scalar(s_scr[:, 5:6], s_scr[:, 1:2], 1e-6, None, ALU.add),
                           inc=[(vd, 1)])
                    p.emit("scalar", lambda e: e.activation(s_scr[:, 2:3], s_scr[:, 5:6], AF.Sqrt),
                           wait=[tok(vd)], inc=[(sd, 1)])
                    p.emit("vector", lambda e: e.reciprocal(s_scr[:, 1:2], s_scr[:, 2:3]), wait=[tok(sd)])
                    p.emit("vector", lambda e: e.reduce_max(s_scr[:, 3:4], s_xnf[:], axis=AX.X,
                                                            apply_absolute_value=True))
                    p.emit("vector", lambda e: e.tensor_tensor(s_scr[:, 3:4], s_scr[:, 3:4], s_scr[:, 1:2], ALU.mult))
                    p.emit("vector", lambda e: e.tensor_scalar(s_scr[:, 3:4], s_scr[:, 3:4], 1e-5, None, ALU.max))
                    p.emit("vector", lambda e: e.reciprocal(s_scr[:, 4:5], s_scr[:, 3:4]))
                    p.emit("vector", lambda e: e.tensor_scalar(s_scr[:, 4:5], s_scr[:, 4:5], 127.0, None, ALU.mult))
                    qf, of = s_qfac[:, j:j + 1], s_ofac[:, j:j + 1]
                    p.emit("vector", lambda e, qf=qf: e.tensor_tensor(qf, s_scr[:, 1:2], s_scr[:, 4:5], ALU.mult))
                    p.emit("vector", lambda e, of=of: e.tensor_tensor(of, s_scr[:, 3:4], s_sc[:, 1:2], ALU.mult),
                           wait=[const_tok])
                    lo, hi = -128.0, 127.0
                else:
                    qf, of = s_sc[:, 1:2], s_sc[:, 0:1]
                    lo, hi = -127.0, 127.0
                p.emit("vector", lambda e, qf=qf: e.tensor_scalar(s_xnf[:], s_xnf[:], qf, None, ALU.mult),
                       wait=[const_tok] if layer == 1 else [])
                p.emit("vector", lambda e: e.tensor_scalar(s_xnf[:], s_xnf[:], MAGIC, -MAGIC, ALU.add, ALU.add))
                p.emit("vector", lambda e, lo=lo, hi=hi: e.tensor_scalar(s_xnf[:], s_xnf[:], lo, hi, ALU.max, ALU.min))
                p.emit("vector", lambda e: e.tensor_copy(s_qi[:], s_xnf[:, 0:512]))
                p.emit("vector", lambda e: e.tensor_copy(s_qb[:, 0:512], s_qi[:]),
                       wait=[last_qbT] if last_qbT else [])
                p.emit("vector", lambda e: e.tensor_copy(s_qi[:], s_xnf[:, 512:1024]))
                p.emit("vector", lambda e: e.tensor_copy(s_qb[:, 512:1024], s_qi[:]), inc=[(vd, 1)])
                tq = tok(vd)
                last_xnf = tq
                if pend_ffn:
                    pend_ffn()
                    pend_ffn = None
                for ic in range(8):
                    p.emit("tensor", lambda e, ic=ic: e.transpose(
                        pT[:, ic * 128:(ic + 1) * 128], s_xnb[:, ic * 128:(ic + 1) * 128], s_ident[:]),
                        wait=([txn] + ([last_pTc] if last_pTc else [])) if ic == 0 else [],
                        inc=[(pe, 1)] if ic == 7 else [])
                tpe = tok(pe)
                last_xnbT = tpe
                p.emit("vector", lambda e, j=j: e.tensor_copy(
                    s_ar[:, a_xnfm:a_xnfm + NB * 1024].rearrange(
                        "p (ic t) -> p ic t", ic=8, t=T_OWN)[:, :, j * 128:(j + 1) * 128],
                    pT[:].rearrange("p (ic t) -> p ic t", ic=8, t=128)), wait=[tpe], inc=[(vd, 1)])
                last_pTc = tok(vd)
                for ic in range(8):
                    p.emit("tensor", lambda e, ic=ic: e.transpose(
                        pT[:, ic * 128:(ic + 1) * 128], s_qb[:, ic * 128:(ic + 1) * 128], s_ident[:]),
                        wait=[tq, last_pTc] if ic == 0 else [], inc=[(pe, 1)] if ic == 7 else [])
                tpe = tok(pe)
                last_qbT = tpe
                xqw = [tpe]
                if last_ffn_pe:
                    xqw.append(last_ffn_pe)
                p.emit("vector", lambda e: e.tensor_copy(
                    s_xqf[:].rearrange("p (ic t) -> p ic t", ic=8, t=128),
                    pT[:].rearrange("p (ic t) -> p ic t", ic=8, t=128)), wait=xqw, inc=[(vd, 1)])
                txq = tok(vd)
                last_pTc = txq

                def mk_ffn(j=j, txq=txq, of=of):
                    def go():
                        nonlocal last_ffn_pe
                        for oc in range(2):
                            pban = psA[oc]
                            for ic in range(8):
                                p.emit("tensor", lambda e, ic=ic, oc=oc, pban=pban: e.matmul(
                                    pban, s_xqf[:, ic * 128:(ic + 1) * 128],
                                    s_ar[:, a_w8T + ic * E + oc * 512: a_w8T + ic * E + (oc + 1) * 512],
                                    start=(ic == 0), stop=(ic == 7)),
                                    wait=([txq, w8_tok] + ([last_stt[oc]] if last_stt[oc] else [])) if ic == 0 else [],
                                    inc=[(pe, 1)] if ic == 7 else [])
                            tpe = tok(pe)
                            last_ffn_pe = tpe
                            xsl = s_x[:, j * E + oc * 512: j * E + (oc + 1) * 512]
                            p.emit("vector", lambda e, xsl=xsl, of=of, pban=pban: e.scalar_tensor_tensor(
                                xsl, pban, of, xsl, ALU.mult, ALU.add),
                                wait=[tpe, const_tok, bias_tok])
                            p.emit("vector", lambda e, xsl=xsl, oc=oc: e.tensor_tensor(
                                xsl, xsl, s_bias[:, oc * 512:(oc + 1) * 512], ALU.add), inc=[(vd, 1)])
                            last_stt[oc] = tok(vd)
                    return go
                pend_ffn = mk_ffn()
            if pend_ffn:
                pend_ffn()
                pend_ffn = None
            xnfm_tok = tok(vd)

            def tm_proj(act_off, n_ic, act_tw, w_off, w_stride, w_col0, j, pban, wait_first):
                for ic in range(n_ic):
                    act = s_ar[:, act_off + ic * act_tw + j * 128: act_off + ic * act_tw + (j + 1) * 128]
                    wsl = s_ar[:, w_off + ic * w_stride + w_col0: w_off + ic * w_stride + w_col0 + 512]
                    p.emit("tensor", lambda e, act=act, wsl=wsl, ic=ic, pban=pban, n_ic=n_ic: e.matmul(
                        pban, act, wsl, start=(ic == 0), stop=(ic == n_ic - 1)),
                        wait=wait_first if ic == 0 else [], inc=[(pe, 1)] if ic == n_ic - 1 else [])
                return tok(pe)

            def rope_emit(pban, j, dest3d, dest_off, wait_psum):
                x1 = pban[:].rearrange("p (h d) -> p h d", h=8, d=64)[:, :, 0:64:2]
                x2 = pban[:].rearrange("p (h d) -> p h d", h=8, d=64)[:, :, 1:64:2]
                C = s_tabC[:, j * 32:(j + 1) * 32][:, None, :].broadcast_to((128, 8, 32))
                Sx = s_tabS[:, j * 32:(j + 1) * 32][:, None, :].broadcast_to((128, 8, 32))
                ra = s_ra[:].rearrange("p (h d) -> p h d", h=8, d=32)
                rb = s_rb[:].rearrange("p (h d) -> p h d", h=8, d=32)
                d1 = dest3d[:, :, dest_off:dest_off + 32]
                d2 = dest3d[:, :, dest_off + 32:dest_off + 64]
                p.emit("vector", lambda e: e.tensor_tensor(ra, x1, C, ALU.mult), wait=wait_psum + [tab_tok])
                p.emit("vector", lambda e: e.tensor_tensor(rb, x2, Sx, ALU.mult))
                p.emit("vector", lambda e: e.tensor_tensor(d1, ra, rb, ALU.subtract))
                p.emit("vector", lambda e: e.tensor_tensor(ra, x1, Sx, ALU.mult))
                p.emit("vector", lambda e: e.tensor_tensor(rb, x2, C, ALU.mult))
                p.emit("vector", lambda e: e.tensor_tensor(d2, ra, rb, ALU.add), inc=[(vd, 1)])
                return tok(vd)

            if layer == 0:
                aw_dq = a_w
                aw_dkv = a_w + 8 * E
                for ic in range(8):
                    dma("gpsimd", lw, s_ar[:, aw_dq + ic * E: aw_dq + (ic + 1) * E],
                        Wdq[ic * 128:(ic + 1) * 128, :], wait=[tok(lw), tok(l8)] + bar if ic == 0 else [])
                for ic in range(8):
                    dma("gpsimd", lw, s_ar[:, aw_dkv + ic * 256: aw_dkv + (ic + 1) * 256],
                        Wdkv[ic * 128:(ic + 1) * 128, :])
                wA_tok = tok(lw)
                psC4 = [psB[0][:, 0:512], psB[0][:, 512:1024], psB[1][:, 0:512], psB[1][:, 512:1024]]
                last_cqA = [None, None, None, None]
                last_cq_store2 = [None, None]
                for tc in range(4):
                    for co in range(10):
                        bk = (tc * 10 + co) % 4
                        par = (tc * 10 + co) % 2
                        buf = s_qst[:, par * 512:(par + 1) * 512]
                        pban = psC4[bk]
                        woff = (aw_dq + co * 128) if co < 8 else (aw_dkv + (co - 8) * 128)
                        wstr = E if co < 8 else 256
                        for ic in range(8):
                            p.emit("tensor", lambda e, co=co, ic=ic, tc=tc, pban=pban, woff=woff, wstr=wstr: e.matmul(
                                pban,
                                s_ar[:, woff + ic * wstr: woff + ic * wstr + 128],
                                s_ar[:, a_xnfm + ic * T_OWN + tc * 512: a_xnfm + ic * T_OWN + (tc + 1) * 512],
                                start=(ic == 0), stop=(ic == 7)),
                                wait=([xnfm_tok, wA_tok] + ([last_cqA[bk]] if last_cqA[bk] else [])) if ic == 0 else [],
                                inc=[(pe, 1)] if ic == 7 else [])
                        tpe = tok(pe)
                        p.emit("vector", lambda e, pban=pban, buf=buf: e.tensor_copy(buf, pban),
                               wait=[tpe] + ([last_cq_store2[par]] if last_cq_store2[par] else []), inc=[(vd, 1)])
                        last_cqA[bk] = tok(vd)
                        p.emit("gpsimd", lambda e, co=co, tc=tc, buf=buf: e.dma_start(
                            out=cq_scr[co * 128:(co + 1) * 128, tc * 512:(tc + 1) * 512],
                            in_=buf), wait=[tok(vd)], inc=[(st, 16)])
                        last_cq_store2[par] = tok(st)
                cqs_tok = tok(st)
                aw_uk = a_w
                aw_uv = aw_uk + 2 * E
                aw_uq = aw_uv + 2 * E
                aw_qr = aw_uq + 8 * 512
                aw_kr = aw_qr + 8 * 512
                pbar = barrier()
                for ic in range(2):
                    dma("gpsimd", lw, s_ar[:, aw_uk + ic * E: aw_uk + (ic + 1) * E],
                        Wuk[ic * 128:(ic + 1) * 128, :], wait=[tok(lw)] + pbar if ic == 0 else [])
                for ic in range(2):
                    dma("gpsimd", lw, s_ar[:, aw_uv + ic * E: aw_uv + (ic + 1) * E],
                        Wuv[ic * 128:(ic + 1) * 128, :])
                wB_tok = tok(lw)
                psA4 = [psB[0][:, 0:512], psB[0][:, 512:1024], psB[1][:, 0:512], psB[1][:, 512:1024]]
                last_psA4 = [None, None, None, None]
                pj = [0]
                last_qkv_store = None
                for oc in range(2):
                    nsem[0] += 1
                    hl = sem(f"hl{nsem[0]}")
                    all_dma_sems.append(hl)
                    for ic in range(8):
                        dma("gpsimd", hl, s_ar[:, aw_uq + ic * 512: aw_uq + (ic + 1) * 512],
                            Wuq[ic * 128:(ic + 1) * 128, oc * 512:(oc + 1) * 512],
                            wait=[tok(hl), tok(pe)] if ic == 0 else [])
                    for ic in range(8):
                        dma("gpsimd", hl, s_ar[:, aw_qr + ic * 512: aw_qr + (ic + 1) * 512],
                            Wqr[ic * 128:(ic + 1) * 128, oc * 512:(oc + 1) * 512])
                    for ic in range(8):
                        dma("gpsimd", hl, s_ar[:, aw_kr + ic * 512: aw_kr + (ic + 1) * 512],
                            Wkr[ic * 128:(ic + 1) * 128, oc * 512:(oc + 1) * 512])
                    half_tok = tok(hl)
                    for tc in range(4):
                        for co in range(10):
                            dma("sync", lcq, s_ar[:, a_cq + co * 512: a_cq + (co + 1) * 512],
                                cq_scr[co * 128:(co + 1) * 128, tc * 512:(tc + 1) * 512],
                                wait=[cqs_tok, tok(pe)] if co == 0 else [])
                        cqc_tok = tok(lcq)
                        for jj in range(4):
                            j = tc * 4 + jj
                            q3 = s_qst[:].rearrange("p (h d) -> p h d", h=8, d=128)
                            k3 = s_kst[:].rearrange("p (h d) -> p h d", h=8, d=128)
                            def nxt():
                                bi = pj[0] % 4
                                pj[0] += 1
                                return bi, psA4[bi], ([last_psA4[bi]] if last_psA4[bi] else [])
                            bi, pban, w4 = nxt()
                            t = tm_proj(a_cq, 8, 512, aw_uq, 512, 0, jj, pban,
                                        [cqc_tok, half_tok, wB_tok] + w4 +
                                        ([last_qkv_store] if last_qkv_store else []))
                            p.emit("vector", lambda e, q3=q3, pban=pban: e.tensor_scalar(
                                q3[:, :, 0:64], pban.rearrange("p (h d) -> p h d", h=8, d=64),
                                SCALE_SQ[0], None, ALU.mult), wait=[t], inc=[(vd, 1)])
                            last_psA4[bi] = tok(vd)
                            bi, pban, w4 = nxt()
                            t = tm_proj(a_cq, 8, 512, aw_qr, 512, 0, jj, pban, w4)
                            last_psA4[bi] = rope_emit(pban, j, q3, 64, [t])
                            bi, pban, w4 = nxt()
                            t = tm_proj(a_cq + 8 * 512, 2, 512, aw_uk, E, oc * 512, jj, pban, w4)
                            p.emit("vector", lambda e, k3=k3, pban=pban: e.tensor_scalar(
                                k3[:, :, 0:64], pban.rearrange("p (h d) -> p h d", h=8, d=64),
                                SCALE_SQ[0], None, ALU.mult), wait=[t], inc=[(vd, 1)])
                            last_psA4[bi] = tok(vd)
                            bi, pban, w4 = nxt()
                            t = tm_proj(a_xnfm, 8, T_OWN, aw_kr, 512, 0, j, pban, w4)
                            last_psA4[bi] = rope_emit(pban, j, k3, 64, [t])
                            bi, pban, w4 = nxt()
                            t = tm_proj(a_cq + 8 * 512, 2, 512, aw_uv, E, oc * 512, jj, pban, w4)
                            p.emit("vector", lambda e, pban=pban: e.tensor_copy(s_vst[:], pban),
                                   wait=[t], inc=[(vd, 1)])
                            last_psA4[bi] = tok(vd)
                            tdone = tok(vd)
                            p.emit("gpsimd", lambda e, j=j, oc=oc: e.dma_start(
                                out=q_scr[j * 128:(j + 1) * 128, oc * 1024:(oc + 1) * 1024],
                                in_=s_qst[:]), wait=[tdone], inc=[(st, 16)])
                            p.emit("gpsimd", lambda e, j=j, oc=oc: e.dma_start(
                                out=cc_in[0][j * 128:(j + 1) * 128, oc * 1024:(oc + 1) * 1024],
                                in_=s_kst[:]), inc=[(st, 16)])
                            p.emit("gpsimd", lambda e, j=j, oc=oc: e.dma_start(
                                out=cc_in[0][j * 128:(j + 1) * 128, 2048 + oc * 512: 2048 + (oc + 1) * 512],
                                in_=s_vst[:]), inc=[(st, 16)])
                            last_qkv_store = tok(st)
            else:
                aw_qkvT = a_w
                psA4 = [psB[0][:, 0:512], psB[0][:, 512:1024], psB[1][:, 0:512], psB[1][:, 512:1024]]
                last_psA4 = [None, None, None, None]
                pj = [0]
                last_qkv_store = None
                store3_tok = [None, None]
                for oc in range(2):
                    prev = w8_tok if oc == 0 else prev
                    for third in range(3):
                        for rr in range(4):
                            rc = third * 8 + oc * 4 + rr
                            dma("gpsimd", l8, s_wrow[:], qkv_w[rc * 128:(rc + 1) * 128, :],
                                wait=[prev, tok(pe)] + ([store3_tok[1]] if store3_tok[1] else []))
                            tg = tok(l8)
                            prev = transpose_rows_to(
                                s_ar[:, aw_qkvT:aw_qkvT + 8 * 1536], 1536, third * 4 + rr, [tg])
                    half_tok = prev
                    a_qkvb = aw_qkvT + 8 * 1536
                    qb_tok = tok(vd)
                    for third in range(3):
                        qb_tok = bcast_row(
                            s_ar[:, a_qkvb + third * 512: a_qkvb + (third + 1) * 512],
                            qkv_b[0:1, third * E + oc * 512: third * E + (oc + 1) * 512], 512, [qb_tok])
                    for j in range(NB):
                        par = (oc * NB + j) % 2
                        qb2 = s_qst[:, par * 512:(par + 1) * 512]
                        kb2 = s_kst[:, par * 512:(par + 1) * 512]
                        vb2 = s_vst[:] if par == 0 else s_wrow[:, 0:512]
                        q3 = qb2.rearrange("p (h d) -> p h d", h=8, d=64)
                        k3 = kb2.rearrange("p (h d) -> p h d", h=8, d=64)
                        war = [store3_tok[par]] if store3_tok[par] else []
                        def nxt():
                            bi = pj[0] % 4
                            pj[0] += 1
                            return bi, psA4[bi], ([last_psA4[bi]] if last_psA4[bi] else [])
                        bi, pban, w4 = nxt()
                        t = tm_proj(a_xnfm, 8, T_OWN, aw_qkvT, 1536, 0, j, pban,
                                    [xnfm_tok, half_tok, qb_tok] + w4)
                        p.emit("vector", lambda e, pban=pban: e.tensor_tensor(
                            pban[:, :], pban[:, :], s_ar[:, a_qkvb:a_qkvb + 512], ALU.add), wait=[t])
                        last_psA4[bi] = rope_emit(pban, j, q3, 0, war)
                        bi, pban, w4 = nxt()
                        t = tm_proj(a_xnfm, 8, T_OWN, aw_qkvT, 1536, 512, j, pban, w4)
                        p.emit("vector", lambda e, pban=pban: e.tensor_tensor(
                            pban[:, :], pban[:, :], s_ar[:, a_qkvb + 512:a_qkvb + 1024], ALU.add), wait=[t])
                        last_psA4[bi] = rope_emit(pban, j, k3, 0, war)
                        bi, pban, w4 = nxt()
                        t = tm_proj(a_xnfm, 8, T_OWN, aw_qkvT, 1536, 1024, j, pban, w4)
                        p.emit("vector", lambda e, vb2=vb2, pban=pban: e.scalar_tensor_tensor(
                            vb2, pban[:, :], 1.0, s_ar[:, a_qkvb + 1024:a_qkvb + 1536],
                            ALU.mult, ALU.add), wait=[t] + war, inc=[(vd, 1)])
                        last_psA4[bi] = tok(vd)
                        tdone = tok(vd)
                        p.emit("gpsimd", lambda e, j=j, oc=oc, qb2=qb2: e.dma_start(
                            out=q_scr[j * 128:(j + 1) * 128, oc * 512:(oc + 1) * 512],
                            in_=qb2), wait=[tdone], inc=[(st, 16)])
                        p.emit("gpsimd", lambda e, j=j, oc=oc, kb2=kb2: e.dma_start(
                            out=cc_in[1][j * 128:(j + 1) * 128, oc * 512:(oc + 1) * 512],
                            in_=kb2), inc=[(st, 16)])
                        p.emit("gpsimd", lambda e, j=j, oc=oc, vb2=vb2: e.dma_start(
                            out=cc_in[1][j * 128:(j + 1) * 128, E + oc * 512: E + (oc + 1) * 512],
                            in_=vb2), inc=[(st, 16)])
                        store3_tok[par] = tok(st)
                        last_qkv_store = tok(st)

            p.emit("gpsimd", lambda e, layer=layer: e.collective_compute(
                "AllGather", ALU.bypass,
                replica_groups=[[0, 1], [2, 3], [4, 5], [6, 7]],
                ins=[cc_in[layer]], outs=[cc_out[layer]]),
                wait=[tok(st)], inc=[(ccs[layer], 1)])
            cc_tok = (ccs[layer], ccs[layer].n)

            abar = barrier()
            NU = 16 if layer == 0 else 8
            NH_U = 1 if layer == 0 else 2
            pe_end = {}
            first_pe = {}
            last_redA = [None, None]
            last_redB = [None, None]
            last_nmbc = None
            last_expB = [None] * 6
            last_pvT = [None] * 6
            ctx_stt_tok = None
            last_g = None
            for u in range(NU):
                ub = u % 2
                lu = lu0 if ub == 0 else lu1
                kof = a_k + ub * 4096
                qof = a_q + ub * 2048
                vof = a_v + ub * (2 * NB * 65 * NH_U)
                if u >= 2:
                    lwait = [pe_end[u - 2]]
                elif u == 0:
                    lwait = [abar[0], cc_tok]
                else:
                    lwait = [first_pe[0]]
                p.emit("sync", lambda e, u=u, qof=qof: e.dma_start(
                    out=s_ar[:, qof:qof + T_OWN],
                    in_=q_scr[:, u * 128:(u + 1) * 128], transpose=True),
                    wait=lwait + [tok(lu)], inc=[(lu, 16)])
                for sec in range(2):
                    p.emit("sync", lambda e, u=u, sec=sec, kof=kof, layer=layer: e.dma_start(
                        out=s_ar[:, kof + sec * T_OWN: kof + (sec + 1) * T_OWN],
                        in_=cc_out[layer][sec * T_OWN:(sec + 1) * T_OWN, u * 128:(u + 1) * 128],
                        transpose=True), inc=[(lu, 16)])
                for hh in range(NH_U):
                    for sec in range(2):
                        v3 = s_ar[:, vof + (hh * 2 + sec) * NB * 65:
                                  vof + (hh * 2 + sec + 1) * NB * 65].rearrange(
                            "p (kb c) -> p kb c", kb=NB, c=65)
                        hcol = KW + (u * NH_U + hh) * 64
                        p.emit("sync", lambda e, v3=v3, sec=sec, hcol=hcol, layer=layer: e.dma_start(
                            out=v3[:, :, 0:64],
                            in_=cc_out[layer][sec * T_OWN:(sec + 1) * T_OWN, hcol:hcol + 64].rearrange(
                                "(kb p) c -> p kb c", kb=NB, p=128)),
                            inc=[(lu, 16)])
                        p.emit("vector", lambda e, v3=v3: e.memset(v3[:, :, 64:65], 1.0),
                               wait=(lwait + [abar[1]]) if u < 2 else [pe_end[u - 2]])
                t_load = tok(lu)
                t_vload = tok(vd)

                for hh in range(NH_U):
                    hb = 64 * hh
                    KD = 128 if layer == 0 else 64
                    vbase = vof + hh * 2 * NB * 65
                    femitted = False

                    nmrow_tok = None
                    if layer == 0:
                        ph1c = 0
                        for qb in range(NB):
                            lq = s_ar[:, qof + qb * 128: qof + (qb + 1) * 128]
                            nkb1 = qb + 1
                            for sec in range(2):
                                nch = (nkb1 * 128 + 1535) // 1536
                                for kc in range(nch):
                                    w = min(1536, nkb1 * 128 - kc * 1536)
                                    bk = ph1c % 2
                                    ph1c += 1
                                    ps = psB[bk]
                                    for wh in range(0, w, 512):
                                        ww = min(512, w - wh)
                                        lk = s_ar[:, kof + sec * T_OWN + kc * 1536 + wh:
                                                  kof + sec * T_OWN + kc * 1536 + wh + ww]
                                        wt = ([t_load] + ([last_redB[bk]] if last_redB[bk] else []) +
                                              ([last_expB[bk]] if last_expB[bk] else []) +
                                              ([last_g] if (bk == 1 and last_g) else [])) if wh == 0 else []
                                        p.emit("tensor", lambda e, lq=lq, lk=lk, ww=ww, wh=wh, ps=ps: e.matmul(
                                            ps[:, wh:wh + ww], lq, lk, start=True, stop=True),
                                            wait=wt, inc=[(pe, 1)])
                                    tmm = tok(pe)
                                    if not femitted:
                                        first_pe[u] = tmm
                                        femitted = True
                                    if kc * 1536 <= qb * 128 < kc * 1536 + w:
                                        off = qb * 128 - kc * 1536
                                        msk = s_MAq if sec == 0 else s_MBq
                                        p.emit("vector", lambda e, ps=ps, off=off, msk=msk: e.tensor_tensor(
                                            ps[:, off:off + 128], ps[:, off:off + 128], msk[:], ALU.add),
                                            wait=[tmm, consts_tok])
                                    first = (sec == 0 and kc == 0)
                                    if first:
                                        p.emit("vector", lambda e, ps=ps, qb=qb, w=w: e.reduce_max(
                                            s_nm[:, qb:qb + 1], ps[:, 0:w], axis=AX.X, negate=True),
                                            wait=[tmm])
                                    else:
                                        p.emit("vector", lambda e, ps=ps, w=w: e.reduce_max(
                                            s_tmp[:, 0:1], ps[:, 0:w], axis=AX.X, negate=True),
                                            wait=[tmm])
                                        p.emit("vector", lambda e, qb=qb: e.tensor_tensor(
                                            s_nm[:, qb:qb + 1], s_nm[:, qb:qb + 1], s_tmp[:, 0:1], ALU.min))
                                    last_redB[bk] = tok(vd)
                            p.emit("tensor", lambda e, qb=qb: e.transpose(
                                pC[0:1, 0:128], s_nm[:, qb:qb + 1], s_idf[:]),
                                wait=[tok(vd)] + ([last_redA[1]] if last_redA[1] else []) +
                                     ([ctx_stt_tok] if ctx_stt_tok else []), inc=[(pe, 1)])
                            p.emit("vector", lambda e, qb=qb: e.tensor_copy(
                                s_nmrow[0:1, qb * 128:(qb + 1) * 128], pC[0:1, 0:128]),
                                wait=[tok(pe)])
                            last_redA[1] = tok(vd)
                        nmrow_tok = tok(vd)

                    pend = []
                    pending_tail = None
                    last_pv_any = None
                    for qc in range(4):
                        njs = (4 * qc + 4) if layer == 0 else NB
                        kb_list = [(sec, jk) for sec in range(2) for jk in range(njs)]
                        nk = len(kb_list)
                        if layer == 0:
                            p.emit("tensor", lambda e, qc=qc: e.matmul(
                                pG, s_ones[:, :], s_nmrow[0:1, qc * 512:(qc + 1) * 512],
                                start=True, stop=True),
                                wait=[nmrow_tok] + ([last_nmbc] if last_nmbc else []) +
                                     ([last_redA[0]] if last_redA[0] else []), inc=[(pe, 1)])
                            p.emit("vector", lambda e: e.tensor_copy(s_nmbc[:], pG),
                                   wait=[tok(pe)], inc=[(vd, 1)])
                            nmbc_tok = tok(vd)
                            last_nmbc = nmbc_tok
                            last_redA[0] = nmbc_tok
                        for kidx in range(nk):
                            g = kidx % 6
                            sec, jk = kb_list[kidx]
                            ps = psB[g // 3][:, (g % 3) * 512:(g % 3 + 1) * 512]
                            pt = s_pT2[:, g * 512:(g + 1) * 512]
                            off = max(0, jk - 4 * qc) * 128 if layer == 0 else 0
                            lq = s_ar[hb:hb + KD, qof + qc * 512 + off: qof + (qc + 1) * 512]
                            lk = s_ar[hb:hb + KD, kof + sec * T_OWN + jk * 128: kof + sec * T_OWN + (jk + 1) * 128]
                            w0 = [t_load] + ([last_expB[g]] if last_expB[g] else [])
                            if last_redB[g // 3]:
                                w0.append(last_redB[g // 3])
                                last_redB[g // 3] = None
                            if layer == 0 and kidx == 0:
                                w0.append(nmbc_tok)
                            if g == 3 and last_g:
                                w0.append(last_g)
                            p.emit("tensor", lambda e, lk=lk, lq=lq, ps=ps, off=off: e.matmul(
                                ps[:, off:512], lk, lq, start=True, stop=True),
                                wait=w0, inc=[(pe, 1)])
                            if not femitted:
                                first_pe[u] = tok(pe)
                                femitted = True
                            tmm = tok(pe)
                            ew = [tmm]
                            if layer == 0:
                                if 4 * qc <= jk < 4 * qc + 4:
                                    p.emit("vector", lambda e, ps=ps, off=off: e.tensor_tensor(
                                        ps[:, off:512], ps[:, off:512],
                                        s_nmbc[:, off:512], ALU.add), wait=[tmm])
                                    moff = (jk - 4 * qc) * 128
                                    msk = s_MAt if sec == 0 else s_MBt
                                    p.emit("vector", lambda e, ps=ps, moff=moff, msk=msk: e.tensor_tensor(
                                        ps[:, moff:moff + 128], ps[:, moff:moff + 128],
                                        msk[:], ALU.add), wait=[consts_tok], inc=[(vd, 1)])
                                else:
                                    p.emit("vector", lambda e, ps=ps, off=off: e.tensor_tensor(
                                        ps[:, off:512], ps[:, off:512],
                                        s_nmbc[:, off:512], ALU.add), wait=[tmm], inc=[(vd, 1)])
                                ew = [tok(vd)]
                            while len(pend) > 3:
                                pend.pop(0)()
                            if pending_tail:
                                pending_tail()
                                pending_tail = None
                            p.emit("scalar", lambda e, ps=ps, pt=pt, off=off: e.activation(
                                pt[:, off:512], ps[:, off:512], AF.Exp),
                                wait=ew + ([last_pvT[g]] if last_pvT[g] else []),
                                inc=[(sd, 1)])
                            texp = tok(sd)
                            last_expB[g] = texp

                            def mk_pv(sec=sec, jk=jk, g=g, pt=pt, texp=texp, off=off,
                                      kidx=kidx, nk=nk, vbase=vbase):
                                def go():
                                    nonlocal last_pv_any
                                    pv_war = [ctx_stt_tok] if (kidx == 0 and ctx_stt_tok) else []
                                    if kidx == 0 and layer == 0:
                                        pv_war.append(nmrow_tok)
                                    p.emit("tensor", lambda e, pt=pt, sec=sec, jk=jk, vbase=vbase, kidx=kidx, nk=nk, off=off: e.matmul(
                                        pC[0:65, off:512],
                                        s_ar[:, vbase + sec * NB * 65 + jk * 65: vbase + sec * NB * 65 + (jk + 1) * 65],
                                        pt[:, off:512],
                                        start=(kidx == 0), stop=(kidx == nk - 1)),
                                        wait=[texp, t_vload] + pv_war, inc=[(pe, 1)])
                                    last_pvT[g] = tok(pe)
                                    last_pv_any = tok(pe)
                                return go
                            pend.append(mk_pv())
                        while pend:
                            pend.pop(0)()

                        def mk_tail(u=u, hh=hh, qc=qc, tpv_tok=None):
                            def go():
                                nonlocal ctx_stt_tok, last_g, last_redA
                                tpv = last_pv_any
                                lsl = s_ls[0:1, 0:512]
                                isl = s_ls[0:1, 512:1024]
                                p.emit("scalar", lambda e, lsl=lsl: e.activation(lsl, pC[64:65, :], AF.Ln),
                                       wait=[tpv], inc=[(sd, 1)])
                                p.emit("scalar", lambda e, lsl=lsl, isl=isl: e.activation(isl, lsl, AF.Exp, scale=-1.0),
                                       wait=[tok(sd)], inc=[(sd, 1)])
                                p.emit("tensor", lambda e, isl=isl: e.matmul(
                                    pG[0:64, :], s_ones[0:1, 0:64], isl, start=True, stop=True),
                                    wait=[tok(sd)] + ([last_g] if last_g else []) +
                                         ([last_redA[0]] if last_redA[0] else []), inc=[(pe, 1)])
                                p.emit("vector", lambda e: e.tensor_copy(s_g[:], pG[0:64, :]), wait=[tok(pe)])
                                last_g = tok(vd)
                                last_redA[0] = last_g
                                h = u * NH_U + hh
                                cbase = a_ctx + (h // 2) * T_OWN + qc * 512
                                crow = (h % 2) * 64
                                p.emit("vector", lambda e, cbase=cbase, crow=crow: e.scalar_tensor_tensor(
                                    s_ar[crow:crow + 64, cbase:cbase + 512], pC[0:64, :], 1.0, s_g[:],
                                    ALU.mult, ALU.mult), inc=[(vd, 1)])
                                ctx_stt_tok = tok(vd)
                            return go
                        pending_tail = mk_tail()
                    if pending_tail:
                        pending_tail()
                        pending_tail = None
                pe_end[u] = tok(pe)
            ctx_tok = tok(vd)

            obar = barrier()
            if layer == 0:
                for ic in range(8):
                    dma("gpsimd", lw, s_ar[:, a_wo + ic * E: a_wo + (ic + 1) * E],
                        Wo0[ic * 128:(ic + 1) * 128, :], wait=([tok(lw)] + obar) if ic == 0 else [])
                wo_tok = tok(lw)
                ob_tok = None
            else:
                prev2 = None
                for oc in range(8):
                    dma("gpsimd", l8, s_wrow[:], out_w[oc * 128:(oc + 1) * 128, :],
                        wait=[prev2] if prev2 else ([tok(l8)] + obar))
                    tg = tok(l8)
                    prev2 = transpose_rows_to(s_ar[:, a_wo:a_wo + 8 * E], E, oc, [tg])
                wo_tok = prev2
                ob_tok = bcast_row(s_bias, out_b, E, obar)
            last_wstt = [None, None]
            for j in range(NB):
                for oc in range(2):
                    pban = psA[oc]
                    for ic in range(8):
                        p.emit("tensor", lambda e, ic=ic, oc=oc, pban=pban, j=j: e.matmul(
                            pban,
                            s_ar[:, a_ctx + ic * T_OWN + j * 128: a_ctx + ic * T_OWN + (j + 1) * 128],
                            s_ar[:, a_wo + ic * E + oc * 512: a_wo + ic * E + (oc + 1) * 512],
                            start=(ic == 0), stop=(ic == 7)),
                            wait=([ctx_tok, wo_tok] + ([last_wstt[oc]] if last_wstt[oc] else []) +
                                  ([ob_tok] if ob_tok else [])) if ic == 0 else [],
                            inc=[(pe, 1)] if ic == 7 else [])
                    tpe = tok(pe)
                    xsl = s_x[:, j * E + oc * 512: j * E + (oc + 1) * 512]
                    p.emit("vector", lambda e, xsl=xsl, pban=pban: e.scalar_tensor_tensor(
                        xsl, pban, 1.0, xsl, ALU.mult, ALU.add),
                        wait=[tpe] + ([ob_tok] if ob_tok else []))
                    if layer == 1:
                        p.emit("vector", lambda e, xsl=xsl, oc=oc: e.tensor_tensor(
                            xsl, xsl, s_bias[:, oc * 512:(oc + 1) * 512], ALU.add), inc=[(vd, 1)])
                    else:
                        p.emit("vector", lambda e: e.engine_nop(), inc=[(vd, 1)])
                    last_wstt[oc] = tok(vd)

        fin = tok(vd)
        for j in range(NB):
            p.emit("gpsimd", lambda e, j=j: e.dma_start(
                out=x_out[j * 128:(j + 1) * 128, :], in_=s_x[:, j * E:(j + 1) * E]),
                wait=[fin] if j == 0 else [], inc=[(stx, 16)])
        p.emit("sync", lambda e: e.nop(), wait=[tok(stx)])
        p.run(blk)
    return nc




def run_fused(inp):
    nc = build_fused()
    f32 = np.float32
    x = np.ascontiguousarray(inp["L0_attn_x"], dtype=f32).reshape(B, S, E)
    tri = np.where(np.arange(128)[None, :] <= np.arange(128)[:, None], 0.0, -1e9).astype(BF)
    zer = np.zeros((128, 128), dtype=BF)
    ful = np.full((128, 128), -1e9, dtype=BF)
    common = dict(
        ident=_IDENT, identf=np.eye(128, dtype=f32), onesf=_ONESF,
        Wdq=inp["L0_attn_Wdq"].astype(f32), Wuq=inp["L0_attn_Wuq"].astype(f32),
        Wqr=inp["L0_attn_Wqr"].astype(f32), Wdkv=inp["L0_attn_Wdkv"].astype(f32),
        Wuk=inp["L0_attn_Wuk"].astype(f32), Wkr=inp["L0_attn_Wkr"].astype(f32),
        Wuv=inp["L0_attn_Wuv"].astype(f32), Wo0=inp["L0_attn_Wo"].astype(f32),
        w80=inp["L0_ffn_w_int8"].astype(np.int32),
        wscale=np.asarray(inp["L0_ffn_weight_scale"], f32).reshape(1, 1),
        fbias0=np.ascontiguousarray(inp["L0_ffn_bias"], f32).reshape(1, E),
        qkv_w=inp["L1_attn_qkv_w"].astype(f32),
        qkv_b=np.ascontiguousarray(inp["L1_attn_qkv_b"], f32).reshape(1, 3 * E),
        out_w=inp["L1_attn_out_w"].astype(f32),
        out_b=np.ascontiguousarray(inp["L1_attn_out_b"], f32).reshape(1, E),
        w81=inp["L1_ffn_ternary_weight"].astype(np.int32),
        ascale=np.asarray(inp["L1_ffn_act_scale"], f32).reshape(1, 1),
        fbias1=np.ascontiguousarray(inp["L1_ffn_bias"], f32).reshape(1, E),
    )
    in_maps = []
    for c in range(N_CORES):
        b, par = c // 2, c % 2
        blocks = np.arange(NB) * 2 + par
        pos = (blocks[:, None] * 128 + np.arange(128)[None, :]).reshape(-1)
        xc = np.ascontiguousarray(x[b].reshape(S // 128, 128, E)[blocks].reshape(T_OWN, E))
        C0, S0 = _rope_tables_l0(pos)
        C1, S1 = _rope_tables_l1(pos)
        m = dict(common)
        m.update(x_in=xc,
                 tabC0=_blocked(C0 * SCALE_SQ[0]), tabS0=_blocked(S0 * SCALE_SQ[0]),
                 tabC1=_blocked(C1 * SCALE_SQ[1]), tabS1=_blocked(S1 * SCALE_SQ[1]),
                 MAq=tri if par == 0 else zer,
                 MBq=ful if par == 0 else tri,
                 MAt=np.ascontiguousarray(tri.T) if par == 0 else zer,
                 MBt=np.ascontiguousarray(ful.T) if par == 0 else np.ascontiguousarray(tri.T))
        in_maps.append(m)
    res = run_bass_kernel_spmd(nc, in_maps, core_ids=list(range(N_CORES)))
    out = np.empty((B, S, E), dtype=f32)
    for c in range(N_CORES):
        b, par = c // 2, c % 2
        blocks = np.arange(NB) * 2 + par
        r = np.asarray(res.results[c]["x_out"], dtype=f32).reshape(NB, 128, E)
        out[b].reshape(S // 128, 128, E)[blocks] = r
    return out


def _bf(a):
    if a.dtype == BF:
        return a
    if a.dtype.itemsize == 2:
        return a.view(BF)
    return a.astype(BF)


def kernel(L0_attn_x, L0_attn_Wdq, L0_attn_Wuq, L0_attn_Wqr, L0_attn_Wdkv,
           L0_attn_Wuk, L0_attn_Wkr, L0_attn_Wuv, L0_attn_Wo,
           L0_ffn_w_int8, L0_ffn_weight_scale, L0_ffn_bias,
           L1_attn_qkv_w, L1_attn_qkv_b, L1_attn_out_w, L1_attn_out_b,
           L1_ffn_ternary_weight, L1_ffn_bias, L1_ffn_act_scale):
    inp = dict(L0_attn_x=L0_attn_x, L0_attn_Wdq=L0_attn_Wdq, L0_attn_Wuq=L0_attn_Wuq,
               L0_attn_Wqr=L0_attn_Wqr, L0_attn_Wdkv=L0_attn_Wdkv, L0_attn_Wuk=L0_attn_Wuk,
               L0_attn_Wkr=L0_attn_Wkr, L0_attn_Wuv=L0_attn_Wuv, L0_attn_Wo=L0_attn_Wo,
               L0_ffn_w_int8=L0_ffn_w_int8, L0_ffn_weight_scale=L0_ffn_weight_scale,
               L0_ffn_bias=L0_ffn_bias, L1_attn_qkv_w=L1_attn_qkv_w, L1_attn_qkv_b=L1_attn_qkv_b,
               L1_attn_out_w=L1_attn_out_w, L1_attn_out_b=L1_attn_out_b,
               L1_ffn_ternary_weight=L1_ffn_ternary_weight, L1_ffn_bias=L1_ffn_bias,
               L1_ffn_act_scale=L1_ffn_act_scale)
    return run_fused(inp)


# version 29 (final)
# speedup vs baseline: 1.0012x; 1.0012x over previous
from contextlib import ExitStack

import numpy as np
import ml_dtypes

import concourse.bass as bass
import concourse.mybir as mybir
from concourse.bass_utils import run_bass_kernel_spmd

F32 = mybir.dt.float32
BF16 = mybir.dt.bfloat16
I32 = mybir.dt.int32
AF = mybir.ActivationFunctionType
ALU = mybir.AluOpType
AX = mybir.AxisListType

BF = ml_dtypes.bfloat16
MAGIC = 12582912.0
SCALE_SQ = {0: (1.0 / 128.0) ** 0.25, 1: (1.0 / 64.0) ** 0.25}

N_CORES = 8
B, S, E = 4, 4096, 1024
H = 16
T_OWN = 2048
NB = T_OWN // 128


class Sem:
    def __init__(self, handle):
        self.h = handle
        self.n = 0


class Prog:
    ENGS = ("sync", "gpsimd", "tensor", "vector", "scalar")
    CHAINED = ("vector", "scalar")

    def __init__(self, nc, ctx):
        self.ops = {e: [] for e in self.ENGS}
        self.chain_sem = {e: Sem(ctx.enter_context(nc.semaphore(f"ch_{e}")))
                          for e in self.CHAINED}

    def emit(self, eng, fn, wait=(), inc=(), chain=True):
        wait = list(wait)
        inc = list(inc)
        if eng in self.CHAINED:
            cs = self.chain_sem[eng]
            if chain:
                wait.append((cs, cs.n))
            if not any(s is cs for s, _ in inc):
                inc.append((cs, 1))
        for s, a in inc:
            s.n += a
        self.ops[eng].append((fn, tuple(wait), tuple(inc)))

    def run(self, blk):
        def make(eng_name):
            def body(e):
                for fn, waits, incs in self.ops[eng_name]:
                    for s, v in waits:
                        e.wait_ge(s.h, v)
                    inst = fn(e)
                    for s, a in incs:
                        inst = inst.then_inc(s.h, a)
            return body
        blk.sync(make("sync"))
        blk.gpsimd(make("gpsimd"))
        blk.tensor(make("tensor"))
        blk.vector(make("vector"))
        blk.scalar(make("scalar"))


def tok(sem):
    return (sem, sem.n)


def _rope_tables_l0(pos):
    j = np.arange(32, dtype=np.float64)
    freqs = 10000.0 ** (-j / 32.0)
    th = pos[:, None].astype(np.float64) * freqs[None, :]
    return np.cos(th).astype(np.float32), np.sin(th).astype(np.float32)


def _rope_tables_l1(pos):
    j2 = np.arange(0, 64, 2, dtype=np.float64)
    inv = 1.0 / (10000.0 ** (j2 / 64.0))
    th = pos[:, None].astype(np.float64) * inv[None, :]
    emb = np.concatenate([np.sin(th), np.cos(th)], axis=-1)
    return emb[:, 1::2].astype(np.float32), emb[:, 0::2].astype(np.float32)


def _blocked(tab):
    t = tab.reshape(NB, 128, 32).transpose(1, 0, 2).reshape(128, NB * 32)
    return np.ascontiguousarray(t).astype(BF)


def build_pre(layer):
    nc = bass.Bass("TRN2", target_bir_lowering=False)
    dt = lambda n, sh, d, k: nc.dram_tensor(n, sh, d, kind=k).ap()

    x = dt("x", [T_OWN, E], F32, "ExternalInput")
    ident = dt("ident", [128, 128], BF16, "ExternalInput")
    onesf = dt("onesf", [1, 128], F32, "ExternalInput")
    tabC = dt("tabC", [128, NB * 32], BF16, "ExternalInput")
    tabS = dt("tabS", [128, NB * 32], BF16, "ExternalInput")
    fbias = dt("fbias", [1, E], F32, "ExternalInput")
    w8 = dt("w8", [E, E], I32, "ExternalInput")
    if layer == 0:
        Wdq = dt("Wdq", [E, E], F32, "ExternalInput")
        Wuq = dt("Wuq", [E, E], F32, "ExternalInput")
        Wqr = dt("Wqr", [E, E], F32, "ExternalInput")
        Wdkv = dt("Wdkv", [E, 256], F32, "ExternalInput")
        Wuk = dt("Wuk", [256, E], F32, "ExternalInput")
        Wkr = dt("Wkr", [E, E], F32, "ExternalInput")
        Wuv = dt("Wuv", [256, E], F32, "ExternalInput")
        wscale = dt("wscale", [1, 1], F32, "ExternalInput")
    else:
        qkv_w = dt("qkv_w", [3 * E, E], F32, "ExternalInput")
        qkv_b = dt("qkv_b", [1, 3 * E], F32, "ExternalInput")
        ascale = dt("ascale", [1, 1], F32, "ExternalInput")
    QW = 2048 if layer == 0 else E
    q_out = dt("q_out", [T_OWN, QW], BF16, "ExternalOutput")
    k_out = dt("k_out", [T_OWN, QW], BF16, "ExternalOutput")
    v_out = dt("v_out", [T_OWN, E], BF16, "ExternalOutput")
    ffn_out = dt("ffn_out", [T_OWN, E], F32, "ExternalOutput")

    with ExitStack() as ctx:
        _dummy = None
        sb = lambda nm, sh, d=F32: ctx.enter_context(nc.sbuf_tensor(nm, sh, d))
        sem = lambda nm: Sem(ctx.enter_context(nc.semaphore(nm)))
        p = Prog(nc, ctx)
        vd = p.chain_sem["vector"]
        sd = p.chain_sem["scalar"]

        ld = sem("ld")
        gld = sem("gld")
        l8 = sem("l8")
        lq1 = sem("lq1")
        lfb = sem("lfb")
        lsc = sem("lsc")
        lqb = sem("lqb")
        lx = sem("lx")
        pe = sem("pe")
        st = sem("st")
        st_f = sem("st_f")

        s_ident = sb("s_ident", [128, 128], BF16)
        s_ones = sb("s_ones", [1, 128], F32)
        s_tabC = sb("s_tabC", [128, NB * 32], BF16)
        s_tabS = sb("s_tabS", [128, NB * 32], BF16)
        s_bias = sb("s_bias", [128, E], BF16)
        s_sc = sb("s_sc", [128, 2], F32)

        if layer == 0:
            s_Wdq = sb("s_Wdq", [128, 8 * E], BF16)
            s_Wuq = sb("s_Wuq", [128, 8 * E], BF16)
            s_Wqr = sb("s_Wqr", [128, 8 * E], BF16)
            s_Wdkv = sb("s_Wdkv", [128, 8 * 256], BF16)
            s_Wuk = sb("s_Wuk", [128, 2 * E], BF16)
            s_Wkr = sb("s_Wkr", [128, 8 * E], BF16)
            s_Wuv = sb("s_Wuv", [128, 2 * E], BF16)
        else:
            s_qkvT = sb("s_qkvT", [128, 8 * 3 * E], BF16)
            s_qkvb = sb("s_qkvb", [128, 3 * E], BF16)
        s_wrow = sb("s_wrow", [128, E], BF16)
        s_w8T = sb("s_w8T", [128, 8 * E], BF16)

        s_x = sb("s_x", [128, E], F32)
        s_scr = sb("s_scr", [128, 8], F32)
        s_xnf = sb("s_xnf", [128, E], F32)
        s_xnb = sb("s_xnb", [128, E], BF16)
        s_qi = sb("s_qi", [128, E], I32)
        s_qb = sb("s_qb", [128, E], BF16)
        s_xqf = sb("s_xqf", [128, 8 * 128], BF16)
        s_ffn = sb("s_ffn", [128, E], F32)
        s_mean = sb("s_mean", [128, NB], F32)
        s_rstd = sb("s_rstd", [128, NB], F32)
        s_nmr = sb("s_nmr", [128, NB], F32)
        s_var = sb("s_var", [128, NB], F32)
        s_qfac = sb("s_qfac", [128, NB], F32)
        s_ofac = sb("s_ofac", [128, NB], F32)

        s_xnfm = sb("s_xnfm", [128, 8 * T_OWN], BF16)
        if layer == 0:
            s_cq = sb("s_cq", [128, 8 * T_OWN], BF16)
            s_ckv = sb("s_ckv", [128, 2 * T_OWN], BF16)

        s_ra = sb("s_ra", [128, 256], F32)
        s_rb = sb("s_rb", [128, 256], F32)
        s_qst = sb("s_qst", [128, QW], BF16)
        s_kst = sb("s_kst", [128, QW], BF16)
        s_vst = sb("s_vst", [128, E], BF16)

        pA = ctx.enter_context(nc.psum_tensor("pA", [128, 512], F32))
        pB = ctx.enter_context(nc.psum_tensor("pB", [128, 512], F32))
        pT = ctx.enter_context(nc.psum_tensor("pT", [128, 1024], BF16))

        blk = ctx.enter_context(nc.Block())

        def dma(eng, sem_, out_ap, in_ap, wait=()):
            p.emit(eng, lambda e, o=out_ap, i=in_ap: e.dma_start(out=o, in_=i),
                   wait=wait, inc=[(sem_, 16)])

        dma("sync", ld, s_ident[:], ident)
        dma("sync", ld, s_ones[:], onesf)
        dma("sync", ld, s_tabC[:], tabC)
        dma("sync", ld, s_tabS[:], tabS)
        consts_tok = tok(ld)
        if layer == 0:
            for ic in range(8):
                dma("gpsimd", gld, s_Wdq[:, ic * E:(ic + 1) * E], Wdq[ic * 128:(ic + 1) * 128, :])
                dma("gpsimd", gld, s_Wuq[:, ic * E:(ic + 1) * E], Wuq[ic * 128:(ic + 1) * 128, :])
                dma("gpsimd", gld, s_Wqr[:, ic * E:(ic + 1) * E], Wqr[ic * 128:(ic + 1) * 128, :])
                dma("gpsimd", gld, s_Wkr[:, ic * E:(ic + 1) * E], Wkr[ic * 128:(ic + 1) * 128, :])
                dma("gpsimd", gld, s_Wdkv[:, ic * 256:(ic + 1) * 256], Wdkv[ic * 128:(ic + 1) * 128, :])
            for ic in range(2):
                dma("gpsimd", gld, s_Wuk[:, ic * E:(ic + 1) * E], Wuk[ic * 128:(ic + 1) * 128, :])
                dma("gpsimd", gld, s_Wuv[:, ic * E:(ic + 1) * E], Wuv[ic * 128:(ic + 1) * 128, :])
        w_tok = tok(gld)

        def transpose_rows_to(dest_sb, dest_ocols, oc, extra_wait):
            for ic in range(8):
                p.emit("tensor", lambda e, ic=ic: e.transpose(
                    pT[:, ic * 128:(ic + 1) * 128], s_wrow[:, ic * 128:(ic + 1) * 128], s_ident[:]),
                    wait=(list(extra_wait) + [consts_tok]) if ic == 0 else [],
                    inc=[(pe, 1)] if ic == 7 else [])
            tpe = tok(pe)
            dst = dest_sb[:].rearrange("p (ic o) -> p ic o", ic=8, o=dest_ocols)[:, :, oc * 128:(oc + 1) * 128]
            src = pT[:].rearrange("p (ic t) -> p ic t", ic=8, t=128)
            p.emit("vector", lambda e, dst=dst, src=src: e.tensor_copy(dst, src),
                   wait=[tpe], inc=[(vd, 1)])
            return tok(vd)

        prev = None
        for oc in range(8):
            waits = [prev] if prev else [tok(gld)]
            dma("gpsimd", l8, s_wrow[:], w8[oc * 128:(oc + 1) * 128, :], wait=waits)
            tg = tok(l8)
            if layer == 1:
                p.emit("vector", lambda e: e.tensor_scalar(s_wrow[:], s_wrow[:], -1.0, None, ALU.add),
                       wait=[tg], inc=[(vd, 1)])
                tg = tok(vd)
            prev = transpose_rows_to(s_w8T, E, oc, [tg])
        w8_tok = prev

        if layer == 1:
            for rc in range(24):
                dma("gpsimd", lq1, s_wrow[:], qkv_w[rc * 128:(rc + 1) * 128, :], wait=[prev])
                tg = tok(lq1)
                prev = transpose_rows_to(s_qkvT, 3 * E, rc, [tg])
            qkvT_tok = prev
            pv = tok(vd)
            for cc in range(3):
                dma("sync", lqb, s_ffn[0:1, :], qkv_b[0:1, cc * E:(cc + 1) * E], wait=[pv])
                tb = tok(lqb)
                for c in range(2):
                    p.emit("tensor", lambda e, c=c: e.matmul(
                        pB[:, :], s_ones[:, :], s_ffn[0:1, c * 512:(c + 1) * 512], start=True, stop=True),
                        wait=[tb, pv], inc=[(pe, 1)])
                    p.emit("vector", lambda e, c=c, cc=cc: e.tensor_copy(
                        s_qkvb[:, cc * E + c * 512: cc * E + (c + 1) * 512], pB[:, :]),
                        wait=[tok(pe)], inc=[(vd, 1)])
                    pv = tok(vd)
        else:
            qkvT_tok = w8_tok

        pv = tok(vd)
        dma("sync", lfb, s_ffn[0:1, :], fbias, wait=[pv])
        tb = tok(lfb)
        for c in range(2):
            p.emit("tensor", lambda e, c=c: e.matmul(
                pB[:, :], s_ones[:, :], s_ffn[0:1, c * 512:(c + 1) * 512], start=True, stop=True),
                wait=[tb, pv], inc=[(pe, 1)])
            p.emit("vector", lambda e, c=c: e.tensor_copy(s_bias[:, c * 512:(c + 1) * 512], pB[:, :]),
                   wait=[tok(pe)], inc=[(vd, 1)])
            pv = tok(vd)
        dma("sync", lsc, s_sc[0:1, 0:1], wscale if layer == 0 else ascale, wait=[tok(ld)])
        p.emit("tensor", lambda e: e.matmul(pB[:, 0:1], s_ones[:, :], s_sc[0:1, 0:1], start=True, stop=True),
               wait=[tok(lsc), pv], inc=[(pe, 1)])
        p.emit("vector", lambda e: e.tensor_copy(s_sc[:, 0:1], pB[:, 0:1]), wait=[tok(pe)])
        if layer == 1:
            p.emit("vector", lambda e: e.tensor_scalar(s_sc[:, 0:1], s_sc[:, 0:1], 1e-5, None, ALU.max))
            p.emit("vector", lambda e: e.reciprocal(s_sc[:, 1:2], s_sc[:, 0:1]), inc=[(vd, 1)])
        else:
            p.emit("vector", lambda e: e.tensor_scalar(s_sc[:, 1:2], s_sc[:, 0:1], 1.0 / 127.0, None, ALU.mult),
                   inc=[(vd, 1)])
        const_tok = tok(vd)

        for j in range(NB):
            xs = s_x[:]
            p.emit("sync", lambda e, j=j, xs=xs: e.dma_start(out=xs, in_=x[j * 128:(j + 1) * 128, :]),
                   wait=[tok(vd)] if j >= 1 else [tok(ld)], inc=[(lx, 16)])
            tx = tok(lx)
            p.emit("scalar", lambda e, xs=xs: e.activation(s_xnf[:], xs, AF.Square),
                   wait=[tx, tok(vd)], inc=[(sd, 1)])
            tsq = tok(sd)
            mj, rj, bj, vjc = s_mean[:, j:j + 1], s_rstd[:, j:j + 1], s_nmr[:, j:j + 1], s_var[:, j:j + 1]
            p.emit("vector", lambda e, xs=xs, mj=mj: e.reduce_sum(mj, xs, axis=AX.X), wait=[tx])
            p.emit("vector", lambda e, vjc=vjc: e.reduce_sum(vjc, s_xnf[:], axis=AX.X), wait=[tsq])
            p.emit("vector", lambda e, mj=mj: e.tensor_scalar(mj, mj, 1.0 / E, None, ALU.mult))
            p.emit("vector", lambda e, mj=mj: e.tensor_tensor(s_scr[:, 0:1], mj, mj, ALU.mult))
            p.emit("vector", lambda e, vjc=vjc: e.tensor_scalar(vjc, vjc, 1.0 / E, None, ALU.mult))
            p.emit("vector", lambda e, vjc=vjc: e.tensor_tensor(vjc, vjc, s_scr[:, 0:1], ALU.subtract))
            p.emit("vector", lambda e, vjc=vjc: e.tensor_scalar(s_scr[:, 2:3], vjc, 1e-5, None, ALU.add),
                   inc=[(vd, 1)])
            p.emit("scalar", lambda e: e.activation(s_scr[:, 0:1], s_scr[:, 2:3], AF.Sqrt),
                   wait=[tok(vd)], inc=[(sd, 1)])
            p.emit("vector", lambda e, rj=rj: e.reciprocal(rj, s_scr[:, 0:1]), wait=[tok(sd)])
            p.emit("vector", lambda e, mj=mj, rj=rj, bj=bj: e.scalar_tensor_tensor(
                bj, mj, -1.0, rj, ALU.mult, ALU.mult))
            p.emit("vector", lambda e, xs=xs, rj=rj, bj=bj: e.tensor_scalar(
                s_xnf[:], xs, rj, bj, ALU.mult, ALU.add))
            p.emit("vector", lambda e: e.tensor_copy(s_xnb[:], s_xnf[:]), inc=[(vd, 1)])
            txn = tok(vd)
            if layer == 0:
                p.emit("vector", lambda e, vjc=vjc, rj=rj: e.tensor_tensor(s_scr[:, 1:2], vjc, rj, ALU.mult))
                p.emit("vector", lambda e, rj=rj: e.tensor_tensor(s_scr[:, 1:2], s_scr[:, 1:2], rj, ALU.mult))
                p.emit("vector", lambda e: e.tensor_scalar(s_scr[:, 5:6], s_scr[:, 1:2], 1e-6, None, ALU.add),
                       inc=[(vd, 1)])
                p.emit("scalar", lambda e: e.activation(s_scr[:, 2:3], s_scr[:, 5:6], AF.Sqrt),
                       wait=[tok(vd)], inc=[(sd, 1)])
                p.emit("vector", lambda e: e.reciprocal(s_scr[:, 1:2], s_scr[:, 2:3]), wait=[tok(sd)])
                p.emit("vector", lambda e: e.reduce_max(s_scr[:, 3:4], s_xnf[:], axis=AX.X,
                                                        apply_absolute_value=True))
                p.emit("vector", lambda e: e.tensor_tensor(s_scr[:, 3:4], s_scr[:, 3:4], s_scr[:, 1:2], ALU.mult))
                p.emit("vector", lambda e: e.tensor_scalar(s_scr[:, 3:4], s_scr[:, 3:4], 1e-5, None, ALU.max))
                p.emit("vector", lambda e: e.reciprocal(s_scr[:, 4:5], s_scr[:, 3:4]))
                p.emit("vector", lambda e: e.tensor_scalar(s_scr[:, 4:5], s_scr[:, 4:5], 127.0, None, ALU.mult))
                qf, of = s_qfac[:, j:j + 1], s_ofac[:, j:j + 1]
                p.emit("vector", lambda e, qf=qf: e.tensor_tensor(qf, s_scr[:, 1:2], s_scr[:, 4:5], ALU.mult))
                p.emit("vector", lambda e, of=of: e.tensor_tensor(of, s_scr[:, 3:4], s_sc[:, 1:2], ALU.mult),
                       wait=[const_tok])
                lo, hi = -128.0, 127.0
            else:
                qf, of = s_sc[:, 1:2], s_sc[:, 0:1]
                lo, hi = -127.0, 127.0
            p.emit("vector", lambda e, qf=qf: e.tensor_scalar(s_xnf[:], s_xnf[:], qf, None, ALU.mult),
                   wait=[const_tok] if layer == 1 else [])
            p.emit("vector", lambda e: e.tensor_scalar(s_xnf[:], s_xnf[:], MAGIC, -MAGIC, ALU.add, ALU.add))
            p.emit("vector", lambda e, lo=lo, hi=hi: e.tensor_scalar(s_xnf[:], s_xnf[:], lo, hi, ALU.max, ALU.min))
            p.emit("vector", lambda e: e.tensor_copy(s_qi[:], s_xnf[:]))
            p.emit("vector", lambda e: e.tensor_copy(s_qb[:], s_qi[:]), inc=[(vd, 1)])
            tq = tok(vd)
            for ic in range(8):
                p.emit("tensor", lambda e, ic=ic: e.transpose(
                    pT[:, ic * 128:(ic + 1) * 128], s_xnb[:, ic * 128:(ic + 1) * 128], s_ident[:]),
                    wait=[txn, tok(vd)] if ic == 0 else [], inc=[(pe, 1)] if ic == 7 else [])
            tpe = tok(pe)
            p.emit("vector", lambda e, j=j: e.tensor_copy(
                s_xnfm[:].rearrange("p (ic t) -> p ic t", ic=8, t=T_OWN)[:, :, j * 128:(j + 1) * 128],
                pT[:].rearrange("p (ic t) -> p ic t", ic=8, t=128)), wait=[tpe], inc=[(vd, 1)])
            for ic in range(8):
                p.emit("tensor", lambda e, ic=ic: e.transpose(
                    pT[:, ic * 128:(ic + 1) * 128], s_qb[:, ic * 128:(ic + 1) * 128], s_ident[:]),
                    wait=[tq, tok(vd)] if ic == 0 else [], inc=[(pe, 1)] if ic == 7 else [])
            tpe = tok(pe)
            p.emit("vector", lambda e: e.tensor_copy(
                s_xqf[:].rearrange("p (ic t) -> p ic t", ic=8, t=128),
                pT[:].rearrange("p (ic t) -> p ic t", ic=8, t=128)), wait=[tpe], inc=[(vd, 1)])
            txq = tok(vd)
            for oc in range(2):
                pban = pA if oc == 0 else pB
                for ic in range(8):
                    p.emit("tensor", lambda e, ic=ic, oc=oc, pban=pban: e.matmul(
                        pban[:, :], s_xqf[:, ic * 128:(ic + 1) * 128],
                        s_w8T[:, ic * E + oc * 512: ic * E + (oc + 1) * 512],
                        start=(ic == 0), stop=(ic == 7)),
                        wait=[txq, w8_tok, tok(vd)] if ic == 0 else [],
                        inc=[(pe, 1)] if ic == 7 else [])
                tpe = tok(pe)
                fs = s_ffn[:, oc * 512:(oc + 1) * 512]
                war_ffn = [tok(st_f)] if (j >= 1 and oc == 0) else []
                p.emit("vector", lambda e, fs=fs, of=of, oc=oc, pban=pban: e.scalar_tensor_tensor(
                    fs, pban[:, :], of, s_bias[:, oc * 512:(oc + 1) * 512], ALU.mult, ALU.add),
                    wait=[tpe, const_tok] + war_ffn, inc=[(vd, 1)])
            tff = tok(vd)
            p.emit("gpsimd", lambda e, j=j: e.dma_start(
                out=ffn_out[j * 128:(j + 1) * 128, :], in_=s_ffn[:]),
                wait=[tff], inc=[(st_f, 16)])
        xnfm_tok = tok(vd)

        def fm_proj(w_sb, w_stride, n_oc, dest_sb, wait_first):
            for oc in range(n_oc):
                for tc in range(T_OWN // 512):
                    pban = pA if (oc * 4 + tc) % 2 == 0 else pB
                    for ic in range(8):
                        p.emit("tensor", lambda e, ic=ic, oc=oc, tc=tc, pban=pban, w_sb=w_sb, w_stride=w_stride: e.matmul(
                            pban[:, :],
                            w_sb[:, ic * w_stride + oc * 128: ic * w_stride + (oc + 1) * 128],
                            s_xnfm[:, ic * T_OWN + tc * 512: ic * T_OWN + (tc + 1) * 512],
                            start=(ic == 0), stop=(ic == 7)),
                            wait=(wait_first + [tok(vd)]) if ic == 0 else [],
                            inc=[(pe, 1)] if ic == 7 else [])
                    tpe = tok(pe)
                    p.emit("vector", lambda e, oc=oc, tc=tc, pban=pban, dest_sb=dest_sb: e.tensor_copy(
                        dest_sb[:, oc * T_OWN + tc * 512: oc * T_OWN + (tc + 1) * 512], pban[:, :]),
                        wait=[tpe], inc=[(vd, 1)])

        if layer == 0:
            fm_proj(s_Wdq, E, 8, s_cq, [xnfm_tok, w_tok])
            cq_tok = tok(vd)
            fm_proj(s_Wdkv, 256, 2, s_ckv, [xnfm_tok, w_tok])
            ckv_tok = tok(vd)

        def tm_proj(act_sb, n_ic, w_sb, w_stride, w_col0, j, pban, wait_first):
            for ic in range(n_ic):
                p.emit("tensor", lambda e, ic=ic: e.matmul(
                    pban[:, :],
                    act_sb[:, ic * T_OWN + j * 128: ic * T_OWN + (j + 1) * 128],
                    w_sb[:, ic * w_stride + w_col0: ic * w_stride + w_col0 + 512],
                    start=(ic == 0), stop=(ic == n_ic - 1)),
                    wait=wait_first if ic == 0 else [], inc=[(pe, 1)] if ic == n_ic - 1 else [])
            return tok(pe)

        def rope_emit(pban, j, dest3d, dest_off, wait_psum):
            x1 = pban[:].rearrange("p (h d) -> p h d", h=8, d=64)[:, :, 0:64:2]
            x2 = pban[:].rearrange("p (h d) -> p h d", h=8, d=64)[:, :, 1:64:2]
            C = s_tabC[:, j * 32:(j + 1) * 32][:, None, :].broadcast_to((128, 8, 32))
            Sx = s_tabS[:, j * 32:(j + 1) * 32][:, None, :].broadcast_to((128, 8, 32))
            ra = s_ra[:].rearrange("p (h d) -> p h d", h=8, d=32)
            rb = s_rb[:].rearrange("p (h d) -> p h d", h=8, d=32)
            d1 = dest3d[:, :, dest_off:dest_off + 32]
            d2 = dest3d[:, :, dest_off + 32:dest_off + 64]
            p.emit("vector", lambda e: e.tensor_tensor(ra, x1, C, ALU.mult), wait=wait_psum)
            p.emit("vector", lambda e: e.tensor_tensor(rb, x2, Sx, ALU.mult))
            p.emit("vector", lambda e: e.tensor_tensor(d1, ra, rb, ALU.subtract))
            p.emit("vector", lambda e: e.tensor_tensor(ra, x1, Sx, ALU.mult))
            p.emit("vector", lambda e: e.tensor_tensor(rb, x2, C, ALU.mult))
            p.emit("vector", lambda e: e.tensor_tensor(d2, ra, rb, ALU.add), inc=[(vd, 1)])
            return tok(vd)

        for j in range(NB):
            qst = s_qst[:]
            kst = s_kst[:]
            vst = s_vst[:]
            war = [tok(st)] if j >= 1 else []
            if layer == 0:
                q3 = qst.rearrange("p (h d) -> p h d", h=16, d=128)
                k3 = kst.rearrange("p (h d) -> p h d", h=16, d=128)
                for oc in range(2):
                    hsl = slice(oc * 8, (oc + 1) * 8)
                    t = tm_proj(s_cq, 8, s_Wuq, E, oc * 512, j, pA, [cq_tok, w_tok, tok(vd)] + war)
                    p.emit("vector", lambda e, q3=q3, hsl=hsl: e.tensor_scalar(
                        q3[:, hsl, 0:64], pA[:].rearrange("p (h d) -> p h d", h=8, d=64),
                        SCALE_SQ[0], None, ALU.mult),
                        wait=[t], inc=[(vd, 1)])
                    t = tm_proj(s_cq, 8, s_Wqr, E, oc * 512, j, pB, [tok(vd)])
                    rope_emit(pB, j, q3[:, hsl, :], 64, [t])
                    t = tm_proj(s_ckv, 2, s_Wuk, E, oc * 512, j, pA, [tok(vd)])
                    p.emit("vector", lambda e, k3=k3, hsl=hsl: e.tensor_scalar(
                        k3[:, hsl, 0:64], pA[:].rearrange("p (h d) -> p h d", h=8, d=64),
                        SCALE_SQ[0], None, ALU.mult),
                        wait=[t], inc=[(vd, 1)])
                    t = tm_proj(s_xnfm, 8, s_Wkr, E, oc * 512, j, pB, [tok(vd)])
                    rope_emit(pB, j, k3[:, hsl, :], 64, [t])
                    t = tm_proj(s_ckv, 2, s_Wuv, E, oc * 512, j, pA, [tok(vd)])
                    p.emit("vector", lambda e, vst=vst, oc=oc: e.tensor_copy(
                        vst[:, oc * 512:(oc + 1) * 512], pA[:, :]), wait=[t], inc=[(vd, 1)])
            else:
                q3 = qst.rearrange("p (h d) -> p h d", h=16, d=64)
                k3 = kst.rearrange("p (h d) -> p h d", h=16, d=64)
                for oc in range(2):
                    hsl = slice(oc * 8, (oc + 1) * 8)
                    t = tm_proj(s_xnfm, 8, s_qkvT, 3 * E, 0 * E + oc * 512, j, pA,
                                [xnfm_tok, qkvT_tok, tok(vd)] + war)
                    p.emit("vector", lambda e, oc=oc: e.tensor_tensor(
                        pA[:, :], pA[:, :], s_qkvb[:, oc * 512:(oc + 1) * 512], ALU.add), wait=[t])
                    rope_emit(pA, j, q3[:, hsl, :], 0, [])
                    t = tm_proj(s_xnfm, 8, s_qkvT, 3 * E, 1 * E + oc * 512, j, pB, [tok(vd)])
                    p.emit("vector", lambda e, oc=oc: e.tensor_tensor(
                        pB[:, :], pB[:, :], s_qkvb[:, E + oc * 512: E + (oc + 1) * 512], ALU.add), wait=[t])
                    rope_emit(pB, j, k3[:, hsl, :], 0, [])
                    t = tm_proj(s_xnfm, 8, s_qkvT, 3 * E, 2 * E + oc * 512, j, pA, [tok(vd)])
                    p.emit("vector", lambda e, vst=vst, oc=oc: e.scalar_tensor_tensor(
                        vst[:, oc * 512:(oc + 1) * 512], pA[:, :], 1.0,
                        s_qkvb[:, 2 * E + oc * 512: 2 * E + (oc + 1) * 512], ALU.mult, ALU.add),
                        wait=[t], inc=[(vd, 1)])
            tdone = tok(vd)
            p.emit("gpsimd", lambda e, j=j, qst=qst: e.dma_start(out=q_out[j * 128:(j + 1) * 128, :], in_=qst),
                   wait=[tdone], inc=[(st, 16)])
            p.emit("gpsimd", lambda e, j=j, kst=kst: e.dma_start(out=k_out[j * 128:(j + 1) * 128, :], in_=kst),
                   inc=[(st, 16)])
            p.emit("gpsimd", lambda e, j=j, vst=vst: e.dma_start(out=v_out[j * 128:(j + 1) * 128, :], in_=vst),
                   inc=[(st, 16)])

        p.emit("sync", lambda e: e.nop(), wait=[tok(st), tok(st_f)])
        p.run(blk)
    return nc


_IDENT = np.eye(128, dtype=BF)
_ONESF = np.ones((1, 128), dtype=np.float32)


def run_pre(layer, x_flat, weights):
    nc = build_pre(layer)
    in_maps = []
    for c in range(N_CORES):
        pos = np.arange((c % 2) * T_OWN, (c % 2 + 1) * T_OWN)
        Ct, St = (_rope_tables_l0(pos) if layer == 0 else _rope_tables_l1(pos))
        ss = SCALE_SQ[layer]
        m = dict(x=np.ascontiguousarray(x_flat[c * T_OWN:(c + 1) * T_OWN]),
                 ident=_IDENT, onesf=_ONESF, tabC=_blocked(Ct * ss), tabS=_blocked(St * ss))
        m.update(weights)
        in_maps.append(m)
    res = run_bass_kernel_spmd(nc, in_maps, core_ids=list(range(N_CORES)))
    return {k: np.concatenate([res.results[c][k] for c in range(N_CORES)], axis=0)
            for k in ("q_out", "k_out", "v_out", "ffn_out")}




def build_att(layer):
    nc = bass.Bass("TRN2", target_bir_lowering=False)
    dt = lambda n, sh, d, k: nc.dram_tensor(n, sh, d, kind=k).ap()
    NU = 8 if layer == 0 else 4
    NH_U = 1 if layer == 0 else 2
    qh = dt("qh", [NU, S, 128], BF16, "ExternalInput")
    kh = dt("kh", [NU, S, 128], BF16, "ExternalInput")
    vh = dt("vh", [8, S, 64], BF16, "ExternalInput")
    onesf = dt("onesf", [1, 128], F32, "ExternalInput")
    ctx_out = dt("ctx", [8, 64, S], BF16, "ExternalOutput")
    if layer == 0:
        Mq = dt("Mq", [128, 128], F32, "ExternalInput")
        MT = dt("MT", [128, 128], F32, "ExternalInput")
        identf = dt("identf", [128, 128], F32, "ExternalInput")

    NKB = S // 128
    NQC = S // 512

    with ExitStack() as ctx:
        sb = lambda nm, sh, d=F32: ctx.enter_context(nc.sbuf_tensor(nm, sh, d))
        sem = lambda nm: Sem(ctx.enter_context(nc.semaphore(nm)))
        p = Prog(nc, ctx)
        vd = p.chain_sem["vector"]
        sd = p.chain_sem["scalar"]
        ld = sem("ld")
        lu0 = sem("lu0")
        lu1 = sem("lu1")
        pe = sem("pe")
        st = sem("st")

        s_ones = sb("s_ones", [1, 128], F32)
        s_q = sb("s_q", [128, 2 * S], BF16)
        s_k = sb("s_k", [128, 2 * S], BF16)
        VW = NKB * 65 * NH_U
        s_v = sb("s_v", [128, 2 * VW], BF16)
        s_pT = sb("s_pT", [128, 2 * 512], BF16)
        s_g = sb("s_g", [64, 512], F32)
        s_ls = sb("s_ls", [1, 2 * 512], F32)
        if layer == 0:
            s_Mq = sb("s_Mq", [128, 128], F32)
            s_MT = sb("s_MT", [128, 128], F32)
            s_idf = sb("s_idf", [128, 128], F32)
            s_nm = sb("s_nm", [128, NKB], F32)
            s_nmrow = sb("s_nmrow", [1, S], F32)
            s_tmp = sb("s_tmp", [128, 2], F32)
        s_co = sb("s_co", [64, 2 * 512], BF16)

        psA = [ctx.enter_context(nc.psum_tensor(f"psA{i}", [128, 512], F32)) for i in range(2)]
        psB = [ctx.enter_context(nc.psum_tensor(f"psB{i}", [128, 512], F32)) for i in range(2)]
        pC = ctx.enter_context(nc.psum_tensor("pC", [128, 512], F32))
        pG = ctx.enter_context(nc.psum_tensor("pG", [128, 512], F32))
        pN = ctx.enter_context(nc.psum_tensor("pN", [128, 512], F32))

        blk = ctx.enter_context(nc.Block())

        p.emit("sync", lambda e: e.dma_start(out=s_ones[:], in_=onesf), inc=[(ld, 16)])
        if layer == 0:
            p.emit("sync", lambda e: e.dma_start(out=s_Mq[:], in_=Mq), inc=[(ld, 16)])
            p.emit("sync", lambda e: e.dma_start(out=s_MT[:], in_=MT), inc=[(ld, 16)])
            p.emit("sync", lambda e: e.dma_start(out=s_idf[:], in_=identf), inc=[(ld, 16)])
        consts_tok = tok(ld)

        pe_end = {}
        for u in range(NU):
            ub = u % 2
            lu = lu0 if ub == 0 else lu1
            war = [pe_end[u - 2]] if u - 2 in pe_end else []
            p.emit("sync", lambda e, u=u, ub=ub: e.dma_start(
                out=s_q[:, ub * S:(ub + 1) * S], in_=qh[u], transpose=True),
                wait=war + [tok(ld)], inc=[(lu, 16)])
            p.emit("sync", lambda e, u=u, ub=ub: e.dma_start(
                out=s_k[:, ub * S:(ub + 1) * S], in_=kh[u], transpose=True),
                inc=[(lu, 16)])
            for hh in range(NH_U):
                v3 = s_v[:, ub * VW + hh * NKB * 65: ub * VW + (hh + 1) * NKB * 65].rearrange(
                    "p (kb c) -> p kb c", kb=NKB, c=65)
                p.emit("sync", lambda e, u=u, hh=hh, v3=v3: e.dma_start(
                    out=v3[:, :, 0:64],
                    in_=vh[u * NH_U + hh].rearrange("(kb p) c -> p kb c", kb=NKB, p=128)),
                    inc=[(lu, 16)])
                p.emit("vector", lambda e, v3=v3: e.memset(v3[:, :, 64:65], 1.0), wait=war)
            t_load = tok(lu)
            t_vload = tok(vd)

            for hh in range(NH_U):
                hb = 64 * hh
                KD = 128 if layer == 0 else 64
                v3c = s_v[:, ub * VW + hh * NKB * 65: ub * VW + (hh + 1) * NKB * 65]

                if layer == 0:
                    for qb in range(NKB):
                        lq = s_q[:, ub * S + qb * 128: ub * S + (qb + 1) * 128]
                        nk = (qb + 1) * 128
                        nch = (nk + 511) // 512
                        for kc in range(nch):
                            w = min(512, nk - kc * 512)
                            lk = s_k[:, ub * S + kc * 512: ub * S + kc * 512 + w]
                            ps = psA[kc % 2]
                            p.emit("tensor", lambda e, lq=lq, lk=lk, w=w, ps=ps: e.matmul(
                                ps[:, 0:w], lq, lk, start=True, stop=True),
                                wait=[t_load, tok(vd)], inc=[(pe, 1)])
                            tmm = tok(pe)
                            if kc * 512 <= qb * 128 < kc * 512 + w:
                                off = qb * 128 - kc * 512
                                p.emit("vector", lambda e, ps=ps, off=off: e.tensor_tensor(
                                    ps[:, off:off + 128], ps[:, off:off + 128], s_Mq[:], ALU.add),
                                    wait=[tmm, consts_tok])
                            if kc == 0:
                                p.emit("vector", lambda e, ps=ps, qb=qb, w=w: e.reduce_max(
                                    s_nm[:, qb:qb + 1], ps[:, 0:w], axis=AX.X, negate=True),
                                    wait=[tmm])
                            else:
                                p.emit("vector", lambda e, ps=ps, w=w: e.reduce_max(
                                    s_tmp[:, 0:1], ps[:, 0:w], axis=AX.X, negate=True),
                                    wait=[tmm])
                                p.emit("vector", lambda e, qb=qb: e.tensor_tensor(
                                    s_nm[:, qb:qb + 1], s_nm[:, qb:qb + 1], s_tmp[:, 0:1], ALU.min))
                        p.emit("tensor", lambda e, qb=qb: e.transpose(
                            pN[0:1, 0:128], s_nm[:, qb:qb + 1], s_idf[:]),
                            wait=[tok(vd)], inc=[(pe, 1)])
                        p.emit("vector", lambda e, qb=qb: e.tensor_copy(
                            s_nmrow[0:1, qb * 128:(qb + 1) * 128], pN[0:1, 0:128]),
                            wait=[tok(pe)])

                for qc in range(NQC):
                    nkb2 = 4 * (qc + 1) if layer == 0 else NKB
                    lq = s_q[hb:hb + KD, ub * S + qc * 512: ub * S + (qc + 1) * 512]
                    for kb in range(nkb2):
                        ps = psB[kb % 2]
                        lk = s_k[hb:hb + KD, ub * S + kb * 128: ub * S + (kb + 1) * 128]
                        p.emit("tensor", lambda e, lk=lk, lq=lq, ps=ps: e.matmul(
                            ps[:, :], lk, lq, start=True, stop=(layer == 1)),
                            wait=[t_load, tok(vd), tok(sd)], inc=[(pe, 1)])
                        if layer == 0:
                            p.emit("tensor", lambda e, qc=qc, ps=ps: e.matmul(
                                ps[:, :], s_ones[:, :], s_nmrow[0:1, qc * 512:(qc + 1) * 512],
                                start=False, stop=True),
                                inc=[(pe, 1)])
                        tmm = tok(pe)
                        mask_wait = [tmm]
                        if layer == 0:
                            if 4 * qc <= kb < 4 * qc + 4:
                                off = (kb - 4 * qc) * 128
                                p.emit("vector", lambda e, ps=ps, off=off: e.tensor_tensor(
                                    ps[:, off:off + 128], ps[:, off:off + 128], s_MT[:], ALU.add),
                                    wait=[tmm, consts_tok])
                                mask_wait = [tok(vd)]
                            nfull = min(kb, 4 * qc + 4) - 4 * qc
                            if nfull > 0:
                                p.emit("vector", lambda e, ps=ps, nfull=nfull: e.tensor_scalar(
                                    ps[:, 0:nfull * 128], ps[:, 0:nfull * 128], -1e9, None, ALU.add),
                                    wait=[tmm, consts_tok])
                                mask_wait = [tok(vd)]
                        pt = s_pT[:, (kb % 2) * 512:(kb % 2 + 1) * 512]
                        p.emit("scalar", lambda e, ps=ps, pt=pt: e.activation(pt, ps[:, :], AF.Exp),
                               wait=mask_wait + [tok(pe)], inc=[(sd, 1)])
                        texp = tok(sd)
                        pv_war = [tok(vd)] if kb == 0 else []
                        p.emit("tensor", lambda e, kb=kb, nkb2=nkb2, pt=pt, v3c=v3c: e.matmul(
                            pC[0:65, :], v3c[:, kb * 65:(kb + 1) * 65], pt,
                            start=(kb == 0), stop=(kb == nkb2 - 1)),
                            wait=[texp, t_vload] + pv_war, inc=[(pe, 1)])
                    tpv = tok(pe)
                    lsl = s_ls[0:1, 0:512]
                    isl = s_ls[0:1, 512:1024]
                    p.emit("scalar", lambda e, lsl=lsl: e.activation(lsl, pC[64:65, :], AF.Ln),
                           wait=[tpv], inc=[(sd, 1)])
                    p.emit("vector", lambda e, lsl=lsl: e.tensor_scalar(lsl, lsl, -1.0, None, ALU.mult),
                           wait=[tok(sd)])
                    p.emit("scalar", lambda e, lsl=lsl, isl=isl: e.activation(isl, lsl, AF.Exp),
                           wait=[tok(vd)], inc=[(sd, 1)])
                    p.emit("tensor", lambda e, isl=isl: e.matmul(
                        pG[0:64, :], s_ones[0:1, 0:64], isl, start=True, stop=True),
                        wait=[tok(sd), tok(vd)], inc=[(pe, 1)])
                    p.emit("vector", lambda e: e.tensor_copy(s_g[:], pG[0:64, :]), wait=[tok(pe)])
                    co = s_co[:, (qc % 2) * 512:(qc % 2 + 1) * 512]
                    p.emit("vector", lambda e, co=co: e.scalar_tensor_tensor(
                        co, pC[0:64, :], 1.0, s_g[:], ALU.mult, ALU.mult),
                        wait=[tok(st)], inc=[(vd, 1)])
                    tco = tok(vd)
                    p.emit("gpsimd", lambda e, u=u, hh=hh, qc=qc, co=co: e.dma_start(
                        out=ctx_out[u * NH_U + hh][:, qc * 512:(qc + 1) * 512], in_=co),
                        wait=[tco], inc=[(st, 16)])
                pe_end[u] = tok(pe)

        p.emit("sync", lambda e: e.nop(), wait=[tok(st)])
        p.run(blk)
    return nc


def run_att(layer, q_tm, k_tm, v_tm):
    nc = build_att(layer)
    HW = 128 if layer == 0 else 64
    qr = np.ascontiguousarray(q_tm.reshape(B, S, H * HW // 128, 128))
    kr = np.ascontiguousarray(k_tm.reshape(B, S, H * HW // 128, 128))
    vr = np.ascontiguousarray(v_tm.reshape(B, S, H, 64))
    NU = 8 if layer == 0 else 4
    in_maps = []
    for c in range(N_CORES):
        qs = np.stack([qr[(c * NU + i) // (H * HW // 128), :, (c * NU + i) % (H * HW // 128), :]
                       for i in range(NU)])
        ks = np.stack([kr[(c * NU + i) // (H * HW // 128), :, (c * NU + i) % (H * HW // 128), :]
                       for i in range(NU)])
        vs = np.stack([vr[(c * 8 + i) // H, :, (c * 8 + i) % H, :] for i in range(8)])
        m = dict(qh=qs, kh=ks, vh=vs, onesf=_ONESF)
        if layer == 0:
            qmask = np.where(np.arange(128)[None, :] <= np.arange(128)[:, None], 0.0, -1e9).astype(np.float32)
            m.update(Mq=qmask, MT=np.ascontiguousarray(qmask.T), identf=np.eye(128, dtype=np.float32))
        in_maps.append(m)
    res = run_bass_kernel_spmd(nc, in_maps, core_ids=list(range(N_CORES)))
    ctx = np.concatenate([res.results[c]["ctx"] for c in range(N_CORES)], axis=0)
    return ctx.reshape(B, H, 64, S)


def build_post(layer):
    nc = bass.Bass("TRN2", target_bir_lowering=False)
    dt = lambda n, sh, d, k: nc.dram_tensor(n, sh, d, kind=k).ap()
    ctxf = dt("ctxf", [E, T_OWN], BF16, "ExternalInput")
    x = dt("x", [T_OWN, E], F32, "ExternalInput")
    ffn = dt("ffn", [T_OWN, E], F32, "ExternalInput")
    Wo = dt("Wo", [E, E], F32, "ExternalInput")
    ident = dt("ident", [128, 128], BF16, "ExternalInput")
    onesf = dt("onesf", [1, 128], F32, "ExternalInput")
    obias = dt("obias", [1, E], F32, "ExternalInput")
    xout = dt("xout", [T_OWN, E], F32, "ExternalOutput")

    with ExitStack() as ctx:
        sb = lambda nm, sh, d=F32: ctx.enter_context(nc.sbuf_tensor(nm, sh, d))
        sem = lambda nm: Sem(ctx.enter_context(nc.semaphore(nm)))
        p = Prog(nc, ctx)
        vd = p.chain_sem["vector"]
        sd = p.chain_sem["scalar"]
        ld = sem("ld")
        gld = sem("gld")
        lctx = sem("lctx")
        lob = sem("lob")
        lx = sem("lx")
        pe = sem("pe")
        st = sem("st")

        s_ident = sb("s_ident", [128, 128], BF16)
        s_ones = sb("s_ones", [1, 128], F32)
        s_ctx = sb("s_ctx", [128, 8 * T_OWN], BF16)
        s_Wo = sb("s_Wo", [128, 8 * E], BF16)
        s_wrow = sb("s_wrow", [128, E], BF16)
        s_x = sb("s_x", [128, E], F32)
        s_ffn = sb("s_ffn", [128, E], F32)
        s_o = sb("s_o", [128, E], F32)
        s_ob = sb("s_ob", [128, E], BF16)

        pA = ctx.enter_context(nc.psum_tensor("pA", [128, 512], F32))
        pB = ctx.enter_context(nc.psum_tensor("pB", [128, 512], F32))
        pT = ctx.enter_context(nc.psum_tensor("pT", [128, 1024], BF16))

        blk = ctx.enter_context(nc.Block())

        def dma(eng, sem_, out_ap, in_ap, wait=()):
            p.emit(eng, lambda e, o=out_ap, i=in_ap: e.dma_start(out=o, in_=i),
                   wait=wait, inc=[(sem_, 16)])

        dma("sync", ld, s_ident[:], ident)
        dma("sync", ld, s_ones[:], onesf)
        consts_tok = tok(ld)
        for ic in range(8):
            dma("sync", lctx, s_ctx[:, ic * T_OWN:(ic + 1) * T_OWN], ctxf[ic * 128:(ic + 1) * 128, :],
                wait=[consts_tok] if ic == 0 else [])
        ctx_tok = tok(lctx)
        if layer == 0:
            for ic in range(8):
                dma("gpsimd", gld, s_Wo[:, ic * E:(ic + 1) * E], Wo[ic * 128:(ic + 1) * 128, :])
            w_tok = tok(gld)
        else:
            prev = None
            for oc in range(8):
                dma("gpsimd", gld, s_wrow[:], Wo[oc * 128:(oc + 1) * 128, :],
                    wait=[prev] if prev else [])
                tg = tok(gld)
                for ic in range(8):
                    p.emit("tensor", lambda e, ic=ic: e.transpose(
                        pT[:, ic * 128:(ic + 1) * 128], s_wrow[:, ic * 128:(ic + 1) * 128], s_ident[:]),
                        wait=[tg, consts_tok] if ic == 0 else [], inc=[(pe, 1)] if ic == 7 else [])
                tpe = tok(pe)
                dst = s_Wo[:].rearrange("p (ic o) -> p ic o", ic=8, o=E)[:, :, oc * 128:(oc + 1) * 128]
                srcp = pT[:].rearrange("p (ic t) -> p ic t", ic=8, t=128)
                p.emit("vector", lambda e, dst=dst, srcp=srcp: e.tensor_copy(dst, srcp),
                       wait=[tpe], inc=[(vd, 1)])
                prev = tok(vd)
            w_tok = prev
        pv = tok(vd)
        dma("sync", lob, s_o[0:1, :], obias, wait=[pv])
        tb = tok(lob)
        for c in range(2):
            p.emit("tensor", lambda e, c=c: e.matmul(
                pB[:, :], s_ones[:, :], s_o[0:1, c * 512:(c + 1) * 512], start=True, stop=True),
                wait=[tb, pv], inc=[(pe, 1)])
            p.emit("vector", lambda e, c=c: e.tensor_copy(s_ob[:, c * 512:(c + 1) * 512], pB[:, :]),
                   wait=[tok(pe)], inc=[(vd, 1)])
            pv = tok(vd)
        bias_tok = tok(vd)

        for j in range(NB):
            p.emit("sync", lambda e, j=j: e.dma_start(out=s_x[:], in_=x[j * 128:(j + 1) * 128, :]),
                   wait=[tok(vd)] if j >= 1 else [tok(ld)], inc=[(lx, 16)])
            p.emit("sync", lambda e, j=j: e.dma_start(out=s_ffn[:], in_=ffn[j * 128:(j + 1) * 128, :]),
                   inc=[(lx, 16)])
            tx = tok(lx)
            for oc in range(2):
                pban = pA if oc == 0 else pB
                for ic in range(8):
                    p.emit("tensor", lambda e, ic=ic, oc=oc, pban=pban, j=j: e.matmul(
                        pban[:, :], s_ctx[:, ic * T_OWN + j * 128: ic * T_OWN + (j + 1) * 128],
                        s_Wo[:, ic * E + oc * 512: ic * E + (oc + 1) * 512],
                        start=(ic == 0), stop=(ic == 7)),
                        wait=[ctx_tok, w_tok, bias_tok, tok(vd)] if ic == 0 else [],
                        inc=[(pe, 1)] if ic == 7 else [])
                tpe = tok(pe)
                osl = s_o[:, oc * 512:(oc + 1) * 512]
                war = [tok(st)] if (j >= 1 and oc == 0) else []
                p.emit("vector", lambda e, osl=osl, oc=oc, pban=pban: e.scalar_tensor_tensor(
                    osl, pban[:, :], 1.0, s_x[:, oc * 512:(oc + 1) * 512], ALU.mult, ALU.add),
                    wait=[tpe, tx] + war)
                p.emit("vector", lambda e, osl=osl, oc=oc: e.tensor_tensor(
                    osl, osl, s_ffn[:, oc * 512:(oc + 1) * 512], ALU.add))
                p.emit("vector", lambda e, osl=osl, oc=oc: e.tensor_tensor(
                    osl, osl, s_ob[:, oc * 512:(oc + 1) * 512], ALU.add), inc=[(vd, 1)])
            to = tok(vd)
            p.emit("gpsimd", lambda e, j=j: e.dma_start(
                out=xout[j * 128:(j + 1) * 128, :], in_=s_o[:]),
                wait=[to], inc=[(st, 16)])

        p.emit("sync", lambda e: e.nop(), wait=[tok(st)])
        p.run(blk)
    return nc


def run_post(layer, ctx_bh, x_flat, ffn_flat, Wo, obias):
    nc = build_post(layer)
    cf = ctx_bh.reshape(B, E, S)
    in_maps = []
    for c in range(N_CORES):
        b = c // 2
        sl = slice((c % 2) * T_OWN, (c % 2 + 1) * T_OWN)
        in_maps.append(dict(
            ctxf=np.ascontiguousarray(cf[b, :, sl]),
            x=np.ascontiguousarray(x_flat[c * T_OWN:(c + 1) * T_OWN]),
            ffn=np.ascontiguousarray(ffn_flat[c * T_OWN:(c + 1) * T_OWN]),
            Wo=Wo, ident=_IDENT, onesf=_ONESF,
            obias=obias.reshape(1, E).astype(np.float32)))
    res = run_bass_kernel_spmd(nc, in_maps, core_ids=list(range(N_CORES)))
    return np.concatenate([res.results[c]["xout"] for c in range(N_CORES)], axis=0)





def build_fused():
    nc = bass.Bass("TRN2", target_bir_lowering=False)
    dt = lambda n, sh, d, k: nc.dram_tensor(n, sh, d, kind=k).ap()

    x_in = dt("x_in", [T_OWN, E], F32, "ExternalInput")
    ident = dt("ident", [128, 128], BF16, "ExternalInput")
    identf = dt("identf", [128, 128], F32, "ExternalInput")
    onesf = dt("onesf", [1, 128], F32, "ExternalInput")
    tabC0 = dt("tabC0", [128, NB * 32], BF16, "ExternalInput")
    tabS0 = dt("tabS0", [128, NB * 32], BF16, "ExternalInput")
    tabC1 = dt("tabC1", [128, NB * 32], BF16, "ExternalInput")
    tabS1 = dt("tabS1", [128, NB * 32], BF16, "ExternalInput")
    MAq = dt("MAq", [128, 128], BF16, "ExternalInput")
    MBq = dt("MBq", [128, 128], BF16, "ExternalInput")
    MAt = dt("MAt", [128, 128], BF16, "ExternalInput")
    MBt = dt("MBt", [128, 128], BF16, "ExternalInput")
    Wdq = dt("Wdq", [E, E], F32, "ExternalInput")
    Wuq = dt("Wuq", [E, E], F32, "ExternalInput")
    Wqr = dt("Wqr", [E, E], F32, "ExternalInput")
    Wdkv = dt("Wdkv", [E, 256], F32, "ExternalInput")
    Wuk = dt("Wuk", [256, E], F32, "ExternalInput")
    Wkr = dt("Wkr", [E, E], F32, "ExternalInput")
    Wuv = dt("Wuv", [256, E], F32, "ExternalInput")
    Wo0 = dt("Wo0", [E, E], F32, "ExternalInput")
    w80 = dt("w80", [E, E], I32, "ExternalInput")
    wscale = dt("wscale", [1, 1], F32, "ExternalInput")
    fbias0 = dt("fbias0", [1, E], F32, "ExternalInput")
    qkv_w = dt("qkv_w", [3 * E, E], F32, "ExternalInput")
    qkv_b = dt("qkv_b", [1, 3 * E], F32, "ExternalInput")
    out_w = dt("out_w", [E, E], F32, "ExternalInput")
    out_b = dt("out_b", [1, E], F32, "ExternalInput")
    w81 = dt("w81", [E, E], I32, "ExternalInput")
    ascale = dt("ascale", [1, 1], F32, "ExternalInput")
    fbias1 = dt("fbias1", [1, E], F32, "ExternalInput")
    x_out = dt("x_out", [T_OWN, E], F32, "ExternalOutput")

    q_scr = dt("q_scr", [T_OWN, 2048], BF16, "Internal")
    cq_scr = dt("cq_scr", [1280, T_OWN], BF16, "Internal")
    cc_in = [dt(f"cc_in{l}", [T_OWN, (2048 if l == 0 else E) + E], BF16, "Internal")
             for l in range(2)]
    cc_out = [dt(f"cc_out{l}", [2 * T_OWN, (2048 if l == 0 else E) + E], BF16, "Internal")
              for l in range(2)]

    with ExitStack() as ctx:
        sb = lambda nm, sh, d=F32: ctx.enter_context(nc.sbuf_tensor(nm, sh, d))
        sem = lambda nm: Sem(ctx.enter_context(nc.semaphore(nm)))
        p = Prog(nc, ctx)
        vd = p.chain_sem["vector"]
        sd = p.chain_sem["scalar"]
        ld = sem("ld")
        lw = sem("lw")
        l8 = sem("l8")
        lcq = sem("lcq")
        lu0 = sem("lu0")
        lu1 = sem("lu1")
        pe = sem("pe")
        st = sem("st")
        stx = sem("stx")
        ccs = [sem("cc0"), sem("cc1")]

        s_ident = sb("s_ident", [128, 128], BF16)
        s_idf = sb("s_idf", [128, 128], F32)
        s_ones = sb("s_ones", [1, 128], F32)
        s_tabC = sb("s_tabC", [128, NB * 32], BF16)
        s_tabS = sb("s_tabS", [128, NB * 32], BF16)
        s_MAq = sb("s_MAq", [128, 128], BF16)
        s_MBq = sb("s_MBq", [128, 128], BF16)
        s_MAt = sb("s_MAt", [128, 128], BF16)
        s_MBt = sb("s_MBt", [128, 128], BF16)
        s_x = sb("s_x", [128, NB * E], F32)
        s_xnf = sb("s_xnf", [128, E], F32)
        s_xnb = sb("s_xnb", [128, E], BF16)
        s_qi = sb("s_qi", [128, 512], I32)
        s_qb = sb("s_qb", [128, E], BF16)
        s_xqf = sb("s_xqf", [128, 8 * 128], BF16)
        s_mean = sb("s_mean", [128, NB], F32)
        s_rstd = sb("s_rstd", [128, NB], F32)
        s_nmr = sb("s_nmr", [128, NB], F32)
        s_var = sb("s_var", [128, NB], F32)
        s_qfac = sb("s_qfac", [128, NB], F32)
        s_ofac = sb("s_ofac", [128, NB], F32)
        s_scr = sb("s_scr", [128, 8], F32)
        s_sc = sb("s_sc", [128, 2], F32)
        s_bias = sb("s_bias", [128, E], BF16)
        s_ra = sb("s_ra", [128, 256], F32)
        s_rb = sb("s_rb", [128, 256], F32)
        s_qst = sb("s_qst", [128, 1024], BF16)
        s_kst = sb("s_kst", [128, 1024], BF16)
        s_vst = sb("s_vst", [128, 512], BF16)
        s_nm = sb("s_nm", [128, NB], F32)
        s_nmrow = sb("s_nmrow", [1, T_OWN], F32)
        s_tmp = sb("s_tmp", [128, 2], F32)
        s_g = sb("s_g", [64, 512], F32)
        s_ls = sb("s_ls", [1, 2 * 512], F32)
        s_wrow = sb("s_wrow", [128, E], BF16)
        s_pT2 = sb("s_pT2", [128, 2 * 1536], BF16)
        s_nmbc = sb("s_nmbc", [128, 512], F32)

        AR = 47 * 1024
        s_ar = sb("s_ar", [128, AR], BF16)

        psB = [ctx.enter_context(nc.psum_tensor(f"psB{i}", [128, 1536], F32)) for i in range(2)]
        pC = ctx.enter_context(nc.psum_tensor("pC", [128, 512], F32))
        pT = ctx.enter_context(nc.psum_tensor("pT", [128, 1024], BF16))
        psA = [psB[0][:, 0:512], psB[0][:, 512:1024]]
        pG = psB[1][:, 0:512]
        pN = psB[1][:, 512:1024]

        blk = ctx.enter_context(nc.Block())

        def dma(eng, sem_, out_ap, in_ap, wait=()):
            p.emit(eng, lambda e, o=out_ap, i=in_ap: e.dma_start(out=o, in_=i),
                   wait=wait, inc=[(sem_, 16)])

        all_dma_sems = [ld, lw, l8, lcq, lu0, lu1, st, stx]

        def barrier():
            toks = [tok(pe), tok(vd), tok(sd)] + [tok(s) for s in all_dma_sems] + \
                   [tok(c) for c in ccs]
            p.emit("tensor", lambda e: e.nop(), wait=[t for t in toks if t[0] is not pe],
                   inc=[(pe, 1)])
            bpe = tok(pe)
            p.emit("vector", lambda e: e.engine_nop(), wait=[t for t in toks if t[0] is not vd])
            bvd = tok(vd)
            p.emit("scalar", lambda e: e.nop(), wait=[t for t in toks if t[0] is not sd])
            bsd = tok(sd)
            p.emit("sync", lambda e: e.nop(), wait=toks)
            p.emit("gpsimd", lambda e: e.nop(), wait=toks)
            return [bpe, bvd, bsd]

        dma("sync", ld, s_ident[:], ident)
        dma("sync", ld, s_idf[:], identf)
        dma("sync", ld, s_ones[:], onesf)
        dma("sync", ld, s_MAq[:], MAq)
        dma("sync", ld, s_MBq[:], MBq)
        dma("sync", ld, s_MAt[:], MAt)
        dma("sync", ld, s_MBt[:], MBt)
        for j in range(NB):
            dma("sync", ld, s_x[:, j * E:(j + 1) * E], x_in[j * 128:(j + 1) * 128, :])
        x_tok = tok(ld)
        consts_tok = x_tok

        nsem = [0]
        def bcast_row(dst_sb, src_row_dram, width, wait_pre):
            pv = tok(vd)
            nsem[0] += 1
            lsem = sem(f"lb{nsem[0]}")
            all_dma_sems.append(lsem)
            dma("sync", lsem, s_xnf[0:1, 0:width], src_row_dram, wait=[pv] + list(wait_pre))
            tb = tok(lsem)
            for c in range(width // 512):
                p.emit("tensor", lambda e, c=c: e.matmul(
                    pG, s_ones[:, :], s_xnf[0:1, c * 512:(c + 1) * 512],
                    start=True, stop=True), wait=[tb, tok(vd)], inc=[(pe, 1)])
                p.emit("vector", lambda e, c=c, dst_sb=dst_sb: e.tensor_copy(
                    dst_sb[:, c * 512:(c + 1) * 512], pG), wait=[tok(pe)])
            return tok(vd)

        def transpose_rows_to(dest_ap, dest_ocols, oc, extra_wait):
            for ic in range(8):
                p.emit("tensor", lambda e, ic=ic: e.transpose(
                    pT[:, ic * 128:(ic + 1) * 128], s_wrow[:, ic * 128:(ic + 1) * 128], s_ident[:]),
                    wait=(list(extra_wait) + [consts_tok, tok(vd)]) if ic == 0 else [],
                    inc=[(pe, 1)] if ic == 7 else [])
            tpe = tok(pe)
            dst = dest_ap.rearrange("p (ic o) -> p ic o", ic=8, o=dest_ocols)[:, :, oc * 128:(oc + 1) * 128]
            srcp = pT[:].rearrange("p (ic t) -> p ic t", ic=8, t=128)
            p.emit("vector", lambda e, dst=dst, srcp=srcp: e.tensor_copy(dst, srcp),
                   wait=[tpe], inc=[(vd, 1)])
            return tok(vd)

        for layer in range(2):
            KW = 2048 if layer == 0 else E
            w8 = w80 if layer == 0 else w81
            tabC = tabC0 if layer == 0 else tabC1
            tabS = tabS0 if layer == 0 else tabS1

            a_w8T = 0
            a_xnfm = 8 * 1024
            a_cq = a_xnfm + NB * 1024
            a_w = a_cq + 10 * 512
            a_k = a_xnfm
            a_q = a_k + 2 * 4096
            a_v = a_q + 2 * 2048
            a_ctx = a_v + 8384
            a_wo = a_w8T

            bar = barrier() if layer == 1 else []
            dma("sync", ld, s_tabC[:], tabC, wait=[tok(ld)] + bar)
            dma("sync", ld, s_tabS[:], tabS)
            tab_tok = tok(ld)
            bias_tok = bcast_row(s_bias, fbias0 if layer == 0 else fbias1, E, bar)
            nsem[0] += 1
            lscs = sem(f"lsc{nsem[0]}")
            all_dma_sems.append(lscs)
            dma("sync", lscs, s_sc[0:1, 0:1], wscale if layer == 0 else ascale, wait=[tok(ld)])
            p.emit("tensor", lambda e: e.matmul(pG[:, 0:1], s_ones[:, :], s_sc[0:1, 0:1],
                                                start=True, stop=True),
                   wait=[tok(lscs), tok(vd)], inc=[(pe, 1)])
            p.emit("vector", lambda e: e.tensor_copy(s_sc[:, 0:1], pG[:, 0:1]), wait=[tok(pe)])
            if layer == 1:
                p.emit("vector", lambda e: e.tensor_scalar(s_sc[:, 0:1], s_sc[:, 0:1], 1e-5, None, ALU.max))
                p.emit("vector", lambda e: e.reciprocal(s_sc[:, 1:2], s_sc[:, 0:1]), inc=[(vd, 1)])
            else:
                p.emit("vector", lambda e: e.tensor_scalar(s_sc[:, 1:2], s_sc[:, 0:1], 1.0 / 127.0,
                                                           None, ALU.mult), inc=[(vd, 1)])
            const_tok = tok(vd)

            prev = None
            for oc in range(8):
                waits = [prev] if prev else ([tok(l8)] + bar)
                dma("gpsimd", l8, s_wrow[:], w8[oc * 128:(oc + 1) * 128, :], wait=waits)
                tg = tok(l8)
                if layer == 1:
                    p.emit("vector", lambda e: e.tensor_scalar(s_wrow[:], s_wrow[:], -1.0, None, ALU.add),
                           wait=[tg], inc=[(vd, 1)])
                    tg = tok(vd)
                prev = transpose_rows_to(s_ar[:, a_w8T:a_w8T + 8 * E], E, oc, [tg])
            w8_tok = prev

            last_xnf = None
            last_pTc = None
            last_stt = [None, None]
            last_xnbT = None
            last_qbT = None
            pend_ffn = None
            last_ffn_pe = None
            for j in range(NB):
                xs = s_x[:, j * E:(j + 1) * E]
                p.emit("scalar", lambda e, xs=xs: e.activation(s_xnf[:], xs, AF.Square),
                       wait=[x_tok] + ([last_xnf] if last_xnf else [bias_tok, const_tok] + bar),
                       inc=[(sd, 1)])
                tsq = tok(sd)
                mj, rj, bj, vjc = s_mean[:, j:j + 1], s_rstd[:, j:j + 1], s_nmr[:, j:j + 1], s_var[:, j:j + 1]
                p.emit("vector", lambda e, xs=xs, mj=mj: e.reduce_sum(mj, xs, axis=AX.X), wait=[x_tok])
                p.emit("vector", lambda e, vjc=vjc: e.reduce_sum(vjc, s_xnf[:], axis=AX.X), wait=[tsq])
                p.emit("vector", lambda e, mj=mj: e.tensor_scalar(mj, mj, 1.0 / E, None, ALU.mult))
                p.emit("vector", lambda e, mj=mj: e.tensor_tensor(s_scr[:, 0:1], mj, mj, ALU.mult))
                p.emit("vector", lambda e, vjc=vjc: e.tensor_scalar(vjc, vjc, 1.0 / E, None, ALU.mult))
                p.emit("vector", lambda e, vjc=vjc: e.tensor_tensor(vjc, vjc, s_scr[:, 0:1], ALU.subtract))
                p.emit("vector", lambda e, vjc=vjc: e.tensor_scalar(s_scr[:, 2:3], vjc, 1e-5, None, ALU.add),
                       inc=[(vd, 1)])
                p.emit("scalar", lambda e: e.activation(s_scr[:, 0:1], s_scr[:, 2:3], AF.Sqrt),
                       wait=[tok(vd)], inc=[(sd, 1)])
                p.emit("vector", lambda e, rj=rj: e.reciprocal(rj, s_scr[:, 0:1]), wait=[tok(sd)])
                p.emit("vector", lambda e, mj=mj, rj=rj, bj=bj: e.scalar_tensor_tensor(
                    bj, mj, -1.0, rj, ALU.mult, ALU.mult))
                p.emit("vector", lambda e, xs=xs, rj=rj, bj=bj: e.tensor_scalar(
                    s_xnf[:], xs, rj, bj, ALU.mult, ALU.add))
                p.emit("vector", lambda e: e.tensor_copy(s_xnb[:], s_xnf[:]),
                       wait=[last_xnbT] if last_xnbT else [], inc=[(vd, 1)])
                txn = tok(vd)
                if layer == 0:
                    p.emit("vector", lambda e, vjc=vjc, rj=rj: e.tensor_tensor(s_scr[:, 1:2], vjc, rj, ALU.mult))
                    p.emit("vector", lambda e, rj=rj: e.tensor_tensor(s_scr[:, 1:2], s_scr[:, 1:2], rj, ALU.mult))
                    p.emit("vector", lambda e: e.tensor_scalar(s_scr[:, 5:6], s_scr[:, 1:2], 1e-6, None, ALU.add),
                           inc=[(vd, 1)])
                    p.emit("scalar", lambda e: e.activation(s_scr[:, 2:3], s_scr[:, 5:6], AF.Sqrt),
                           wait=[tok(vd)], inc=[(sd, 1)])
                    p.emit("vector", lambda e: e.reciprocal(s_scr[:, 1:2], s_scr[:, 2:3]), wait=[tok(sd)])
                    p.emit("vector", lambda e: e.reduce_max(s_scr[:, 3:4], s_xnf[:], axis=AX.X,
                                                            apply_absolute_value=True))
                    p.emit("vector", lambda e: e.tensor_tensor(s_scr[:, 3:4], s_scr[:, 3:4], s_scr[:, 1:2], ALU.mult))
                    p.emit("vector", lambda e: e.tensor_scalar(s_scr[:, 3:4], s_scr[:, 3:4], 1e-5, None, ALU.max))
                    p.emit("vector", lambda e: e.reciprocal(s_scr[:, 4:5], s_scr[:, 3:4]))
                    p.emit("vector", lambda e: e.tensor_scalar(s_scr[:, 4:5], s_scr[:, 4:5], 127.0, None, ALU.mult))
                    qf, of = s_qfac[:, j:j + 1], s_ofac[:, j:j + 1]
                    p.emit("vector", lambda e, qf=qf: e.tensor_tensor(qf, s_scr[:, 1:2], s_scr[:, 4:5], ALU.mult))
                    p.emit("vector", lambda e, of=of: e.tensor_tensor(of, s_scr[:, 3:4], s_sc[:, 1:2], ALU.mult),
                           wait=[const_tok])
                    lo, hi = -128.0, 127.0
                else:
                    qf, of = s_sc[:, 1:2], s_sc[:, 0:1]
                    lo, hi = -127.0, 127.0
                p.emit("vector", lambda e, qf=qf: e.tensor_scalar(s_xnf[:], s_xnf[:], qf, None, ALU.mult),
                       wait=[const_tok] if layer == 1 else [])
                p.emit("vector", lambda e: e.tensor_scalar(s_xnf[:], s_xnf[:], MAGIC, -MAGIC, ALU.add, ALU.add))
                p.emit("vector", lambda e, lo=lo, hi=hi: e.tensor_scalar(s_xnf[:], s_xnf[:], lo, hi, ALU.max, ALU.min))
                p.emit("vector", lambda e: e.tensor_copy(s_qi[:], s_xnf[:, 0:512]))
                p.emit("vector", lambda e: e.tensor_copy(s_qb[:, 0:512], s_qi[:]),
                       wait=[last_qbT] if last_qbT else [])
                p.emit("vector", lambda e: e.tensor_copy(s_qi[:], s_xnf[:, 512:1024]))
                p.emit("vector", lambda e: e.tensor_copy(s_qb[:, 512:1024], s_qi[:]), inc=[(vd, 1)])
                tq = tok(vd)
                last_xnf = tq
                if pend_ffn:
                    pend_ffn()
                    pend_ffn = None
                for ic in range(8):
                    p.emit("tensor", lambda e, ic=ic: e.transpose(
                        pT[:, ic * 128:(ic + 1) * 128], s_xnb[:, ic * 128:(ic + 1) * 128], s_ident[:]),
                        wait=([txn] + ([last_pTc] if last_pTc else [])) if ic == 0 else [],
                        inc=[(pe, 1)] if ic == 7 else [])
                tpe = tok(pe)
                last_xnbT = tpe
                p.emit("vector", lambda e, j=j: e.tensor_copy(
                    s_ar[:, a_xnfm:a_xnfm + NB * 1024].rearrange(
                        "p (ic t) -> p ic t", ic=8, t=T_OWN)[:, :, j * 128:(j + 1) * 128],
                    pT[:].rearrange("p (ic t) -> p ic t", ic=8, t=128)), wait=[tpe], inc=[(vd, 1)])
                last_pTc = tok(vd)
                for ic in range(8):
                    p.emit("tensor", lambda e, ic=ic: e.transpose(
                        pT[:, ic * 128:(ic + 1) * 128], s_qb[:, ic * 128:(ic + 1) * 128], s_ident[:]),
                        wait=[tq, last_pTc] if ic == 0 else [], inc=[(pe, 1)] if ic == 7 else [])
                tpe = tok(pe)
                last_qbT = tpe
                xqw = [tpe]
                if last_ffn_pe:
                    xqw.append(last_ffn_pe)
                p.emit("vector", lambda e: e.tensor_copy(
                    s_xqf[:].rearrange("p (ic t) -> p ic t", ic=8, t=128),
                    pT[:].rearrange("p (ic t) -> p ic t", ic=8, t=128)), wait=xqw, inc=[(vd, 1)])
                txq = tok(vd)
                last_pTc = txq

                def mk_ffn(j=j, txq=txq, of=of):
                    def go():
                        nonlocal last_ffn_pe
                        for oc in range(2):
                            pban = psA[oc]
                            for ic in range(8):
                                p.emit("tensor", lambda e, ic=ic, oc=oc, pban=pban: e.matmul(
                                    pban, s_xqf[:, ic * 128:(ic + 1) * 128],
                                    s_ar[:, a_w8T + ic * E + oc * 512: a_w8T + ic * E + (oc + 1) * 512],
                                    start=(ic == 0), stop=(ic == 7)),
                                    wait=([txq, w8_tok] + ([last_stt[oc]] if last_stt[oc] else [])) if ic == 0 else [],
                                    inc=[(pe, 1)] if ic == 7 else [])
                            tpe = tok(pe)
                            last_ffn_pe = tpe
                            xsl = s_x[:, j * E + oc * 512: j * E + (oc + 1) * 512]
                            p.emit("vector", lambda e, xsl=xsl, of=of, pban=pban: e.scalar_tensor_tensor(
                                xsl, pban, of, xsl, ALU.mult, ALU.add),
                                wait=[tpe, const_tok, bias_tok])
                            p.emit("vector", lambda e, xsl=xsl, oc=oc: e.tensor_tensor(
                                xsl, xsl, s_bias[:, oc * 512:(oc + 1) * 512], ALU.add), inc=[(vd, 1)])
                            last_stt[oc] = tok(vd)
                    return go
                pend_ffn = mk_ffn()
            if pend_ffn:
                pend_ffn()
                pend_ffn = None
            xnfm_tok = tok(vd)

            def tm_proj(act_off, n_ic, act_tw, w_off, w_stride, w_col0, j, pban, wait_first):
                for ic in range(n_ic):
                    act = s_ar[:, act_off + ic * act_tw + j * 128: act_off + ic * act_tw + (j + 1) * 128]
                    wsl = s_ar[:, w_off + ic * w_stride + w_col0: w_off + ic * w_stride + w_col0 + 512]
                    p.emit("tensor", lambda e, act=act, wsl=wsl, ic=ic, pban=pban, n_ic=n_ic: e.matmul(
                        pban, act, wsl, start=(ic == 0), stop=(ic == n_ic - 1)),
                        wait=wait_first if ic == 0 else [], inc=[(pe, 1)] if ic == n_ic - 1 else [])
                return tok(pe)

            def rope_emit(pban, j, dest3d, dest_off, wait_psum):
                x1 = pban[:].rearrange("p (h d) -> p h d", h=8, d=64)[:, :, 0:64:2]
                x2 = pban[:].rearrange("p (h d) -> p h d", h=8, d=64)[:, :, 1:64:2]
                C = s_tabC[:, j * 32:(j + 1) * 32][:, None, :].broadcast_to((128, 8, 32))
                Sx = s_tabS[:, j * 32:(j + 1) * 32][:, None, :].broadcast_to((128, 8, 32))
                ra = s_ra[:].rearrange("p (h d) -> p h d", h=8, d=32)
                rb = s_rb[:].rearrange("p (h d) -> p h d", h=8, d=32)
                d1 = dest3d[:, :, dest_off:dest_off + 32]
                d2 = dest3d[:, :, dest_off + 32:dest_off + 64]
                p.emit("vector", lambda e: e.tensor_tensor(ra, x1, C, ALU.mult), wait=wait_psum + [tab_tok])
                p.emit("vector", lambda e: e.tensor_tensor(rb, x2, Sx, ALU.mult))
                p.emit("vector", lambda e: e.tensor_tensor(d1, ra, rb, ALU.subtract))
                p.emit("vector", lambda e: e.tensor_tensor(ra, x1, Sx, ALU.mult))
                p.emit("vector", lambda e: e.tensor_tensor(rb, x2, C, ALU.mult))
                p.emit("vector", lambda e: e.tensor_tensor(d2, ra, rb, ALU.add), inc=[(vd, 1)])
                return tok(vd)

            if layer == 0:
                aw_dq = a_w
                aw_dkv = a_w + 8 * E
                for ic in range(8):
                    dma("gpsimd", lw, s_ar[:, aw_dq + ic * E: aw_dq + (ic + 1) * E],
                        Wdq[ic * 128:(ic + 1) * 128, :], wait=[tok(lw), tok(l8)] + bar if ic == 0 else [])
                for ic in range(8):
                    dma("gpsimd", lw, s_ar[:, aw_dkv + ic * 256: aw_dkv + (ic + 1) * 256],
                        Wdkv[ic * 128:(ic + 1) * 128, :])
                wA_tok = tok(lw)
                psC4 = [psB[0][:, 0:512], psB[0][:, 512:1024], psB[1][:, 0:512], psB[1][:, 512:1024]]
                last_cqA = [None, None, None, None]
                last_cq_store2 = [None, None]
                for tc in range(4):
                    for co in range(10):
                        bk = (tc * 10 + co) % 4
                        par = (tc * 10 + co) % 2
                        buf = s_qst[:, par * 512:(par + 1) * 512]
                        pban = psC4[bk]
                        woff = (aw_dq + co * 128) if co < 8 else (aw_dkv + (co - 8) * 128)
                        wstr = E if co < 8 else 256
                        for ic in range(8):
                            p.emit("tensor", lambda e, co=co, ic=ic, tc=tc, pban=pban, woff=woff, wstr=wstr: e.matmul(
                                pban,
                                s_ar[:, woff + ic * wstr: woff + ic * wstr + 128],
                                s_ar[:, a_xnfm + ic * T_OWN + tc * 512: a_xnfm + ic * T_OWN + (tc + 1) * 512],
                                start=(ic == 0), stop=(ic == 7)),
                                wait=([xnfm_tok, wA_tok] + ([last_cqA[bk]] if last_cqA[bk] else [])) if ic == 0 else [],
                                inc=[(pe, 1)] if ic == 7 else [])
                        tpe = tok(pe)
                        p.emit("vector", lambda e, pban=pban, buf=buf: e.tensor_copy(buf, pban),
                               wait=[tpe] + ([last_cq_store2[par]] if last_cq_store2[par] else []), inc=[(vd, 1)])
                        last_cqA[bk] = tok(vd)
                        p.emit("gpsimd", lambda e, co=co, tc=tc, buf=buf: e.dma_start(
                            out=cq_scr[co * 128:(co + 1) * 128, tc * 512:(tc + 1) * 512],
                            in_=buf), wait=[tok(vd)], inc=[(st, 16)])
                        last_cq_store2[par] = tok(st)
                cqs_tok = tok(st)
                aw_uk = a_w
                aw_uv = aw_uk + 2 * E
                aw_uq = aw_uv + 2 * E
                aw_qr = aw_uq + 8 * 512
                aw_kr = aw_qr + 8 * 512
                pbar = barrier()
                for ic in range(2):
                    dma("gpsimd", lw, s_ar[:, aw_uk + ic * E: aw_uk + (ic + 1) * E],
                        Wuk[ic * 128:(ic + 1) * 128, :], wait=[tok(lw)] + pbar if ic == 0 else [])
                for ic in range(2):
                    dma("gpsimd", lw, s_ar[:, aw_uv + ic * E: aw_uv + (ic + 1) * E],
                        Wuv[ic * 128:(ic + 1) * 128, :])
                wB_tok = tok(lw)
                psA4 = [psB[0][:, 0:512], psB[0][:, 512:1024], psB[1][:, 0:512], psB[1][:, 512:1024]]
                last_psA4 = [None, None, None, None]
                pj = [0]
                last_qkv_store = None
                for oc in range(2):
                    nsem[0] += 1
                    hl = sem(f"hl{nsem[0]}")
                    all_dma_sems.append(hl)
                    for ic in range(8):
                        dma("gpsimd", hl, s_ar[:, aw_uq + ic * 512: aw_uq + (ic + 1) * 512],
                            Wuq[ic * 128:(ic + 1) * 128, oc * 512:(oc + 1) * 512],
                            wait=[tok(hl), tok(pe)] if ic == 0 else [])
                    for ic in range(8):
                        dma("gpsimd", hl, s_ar[:, aw_qr + ic * 512: aw_qr + (ic + 1) * 512],
                            Wqr[ic * 128:(ic + 1) * 128, oc * 512:(oc + 1) * 512])
                    for ic in range(8):
                        dma("gpsimd", hl, s_ar[:, aw_kr + ic * 512: aw_kr + (ic + 1) * 512],
                            Wkr[ic * 128:(ic + 1) * 128, oc * 512:(oc + 1) * 512])
                    half_tok = tok(hl)
                    for tc in range(4):
                        for co in range(10):
                            dma("sync", lcq, s_ar[:, a_cq + co * 512: a_cq + (co + 1) * 512],
                                cq_scr[co * 128:(co + 1) * 128, tc * 512:(tc + 1) * 512],
                                wait=[cqs_tok, tok(pe)] if co == 0 else [])
                        cqc_tok = tok(lcq)
                        for jj in range(4):
                            j = tc * 4 + jj
                            q3 = s_qst[:].rearrange("p (h d) -> p h d", h=8, d=128)
                            k3 = s_kst[:].rearrange("p (h d) -> p h d", h=8, d=128)
                            def nxt():
                                bi = pj[0] % 4
                                pj[0] += 1
                                return bi, psA4[bi], ([last_psA4[bi]] if last_psA4[bi] else [])
                            bi, pban, w4 = nxt()
                            t = tm_proj(a_cq, 8, 512, aw_uq, 512, 0, jj, pban,
                                        [cqc_tok, half_tok, wB_tok] + w4 +
                                        ([last_qkv_store] if last_qkv_store else []))
                            p.emit("vector", lambda e, q3=q3, pban=pban: e.tensor_scalar(
                                q3[:, :, 0:64], pban.rearrange("p (h d) -> p h d", h=8, d=64),
                                SCALE_SQ[0], None, ALU.mult), wait=[t], inc=[(vd, 1)])
                            last_psA4[bi] = tok(vd)
                            bi, pban, w4 = nxt()
                            t = tm_proj(a_cq, 8, 512, aw_qr, 512, 0, jj, pban, w4)
                            last_psA4[bi] = rope_emit(pban, j, q3, 64, [t])
                            bi, pban, w4 = nxt()
                            t = tm_proj(a_cq + 8 * 512, 2, 512, aw_uk, E, oc * 512, jj, pban, w4)
                            p.emit("vector", lambda e, k3=k3, pban=pban: e.tensor_scalar(
                                k3[:, :, 0:64], pban.rearrange("p (h d) -> p h d", h=8, d=64),
                                SCALE_SQ[0], None, ALU.mult), wait=[t], inc=[(vd, 1)])
                            last_psA4[bi] = tok(vd)
                            bi, pban, w4 = nxt()
                            t = tm_proj(a_xnfm, 8, T_OWN, aw_kr, 512, 0, j, pban, w4)
                            last_psA4[bi] = rope_emit(pban, j, k3, 64, [t])
                            bi, pban, w4 = nxt()
                            t = tm_proj(a_cq + 8 * 512, 2, 512, aw_uv, E, oc * 512, jj, pban, w4)
                            p.emit("vector", lambda e, pban=pban: e.tensor_copy(s_vst[:], pban),
                                   wait=[t], inc=[(vd, 1)])
                            last_psA4[bi] = tok(vd)
                            tdone = tok(vd)
                            p.emit("gpsimd", lambda e, j=j, oc=oc: e.dma_start(
                                out=q_scr[j * 128:(j + 1) * 128, oc * 1024:(oc + 1) * 1024],
                                in_=s_qst[:]), wait=[tdone], inc=[(st, 16)])
                            p.emit("gpsimd", lambda e, j=j, oc=oc: e.dma_start(
                                out=cc_in[0][j * 128:(j + 1) * 128, oc * 1024:(oc + 1) * 1024],
                                in_=s_kst[:]), inc=[(st, 16)])
                            p.emit("gpsimd", lambda e, j=j, oc=oc: e.dma_start(
                                out=cc_in[0][j * 128:(j + 1) * 128, 2048 + oc * 512: 2048 + (oc + 1) * 512],
                                in_=s_vst[:]), inc=[(st, 16)])
                            last_qkv_store = tok(st)
            else:
                aw_qkvT = a_w
                psA4 = [psB[0][:, 0:512], psB[0][:, 512:1024], psB[1][:, 0:512], psB[1][:, 512:1024]]
                last_psA4 = [None, None, None, None]
                pj = [0]
                last_qkv_store = None
                store3_tok = [None, None]
                for oc in range(2):
                    prev = w8_tok if oc == 0 else prev
                    for third in range(3):
                        for rr in range(4):
                            rc = third * 8 + oc * 4 + rr
                            dma("gpsimd", l8, s_wrow[:], qkv_w[rc * 128:(rc + 1) * 128, :],
                                wait=[prev, tok(pe)] + ([store3_tok[1]] if store3_tok[1] else []))
                            tg = tok(l8)
                            prev = transpose_rows_to(
                                s_ar[:, aw_qkvT:aw_qkvT + 8 * 1536], 1536, third * 4 + rr, [tg])
                    half_tok = prev
                    a_qkvb = aw_qkvT + 8 * 1536
                    qb_tok = tok(vd)
                    for third in range(3):
                        qb_tok = bcast_row(
                            s_ar[:, a_qkvb + third * 512: a_qkvb + (third + 1) * 512],
                            qkv_b[0:1, third * E + oc * 512: third * E + (oc + 1) * 512], 512, [qb_tok])
                    for j in range(NB):
                        par = (oc * NB + j) % 2
                        qb2 = s_qst[:, par * 512:(par + 1) * 512]
                        kb2 = s_kst[:, par * 512:(par + 1) * 512]
                        vb2 = s_vst[:] if par == 0 else s_wrow[:, 0:512]
                        q3 = qb2.rearrange("p (h d) -> p h d", h=8, d=64)
                        k3 = kb2.rearrange("p (h d) -> p h d", h=8, d=64)
                        war = [store3_tok[par]] if store3_tok[par] else []
                        def nxt():
                            bi = pj[0] % 4
                            pj[0] += 1
                            return bi, psA4[bi], ([last_psA4[bi]] if last_psA4[bi] else [])
                        bi, pban, w4 = nxt()
                        t = tm_proj(a_xnfm, 8, T_OWN, aw_qkvT, 1536, 0, j, pban,
                                    [xnfm_tok, half_tok, qb_tok] + w4)
                        p.emit("vector", lambda e, pban=pban: e.tensor_tensor(
                            pban[:, :], pban[:, :], s_ar[:, a_qkvb:a_qkvb + 512], ALU.add), wait=[t])
                        last_psA4[bi] = rope_emit(pban, j, q3, 0, war)
                        bi, pban, w4 = nxt()
                        t = tm_proj(a_xnfm, 8, T_OWN, aw_qkvT, 1536, 512, j, pban, w4)
                        p.emit("vector", lambda e, pban=pban: e.tensor_tensor(
                            pban[:, :], pban[:, :], s_ar[:, a_qkvb + 512:a_qkvb + 1024], ALU.add), wait=[t])
                        last_psA4[bi] = rope_emit(pban, j, k3, 0, war)
                        bi, pban, w4 = nxt()
                        t = tm_proj(a_xnfm, 8, T_OWN, aw_qkvT, 1536, 1024, j, pban, w4)
                        p.emit("vector", lambda e, vb2=vb2, pban=pban: e.scalar_tensor_tensor(
                            vb2, pban[:, :], 1.0, s_ar[:, a_qkvb + 1024:a_qkvb + 1536],
                            ALU.mult, ALU.add), wait=[t] + war, inc=[(vd, 1)])
                        last_psA4[bi] = tok(vd)
                        tdone = tok(vd)
                        p.emit("gpsimd", lambda e, j=j, oc=oc, qb2=qb2: e.dma_start(
                            out=q_scr[j * 128:(j + 1) * 128, oc * 512:(oc + 1) * 512],
                            in_=qb2), wait=[tdone], inc=[(st, 16)])
                        p.emit("gpsimd", lambda e, j=j, oc=oc, kb2=kb2: e.dma_start(
                            out=cc_in[1][j * 128:(j + 1) * 128, oc * 512:(oc + 1) * 512],
                            in_=kb2), inc=[(st, 16)])
                        p.emit("gpsimd", lambda e, j=j, oc=oc, vb2=vb2: e.dma_start(
                            out=cc_in[1][j * 128:(j + 1) * 128, E + oc * 512: E + (oc + 1) * 512],
                            in_=vb2), inc=[(st, 16)])
                        store3_tok[par] = tok(st)
                        last_qkv_store = tok(st)

            p.emit("gpsimd", lambda e, layer=layer: e.collective_compute(
                "AllGather", ALU.bypass,
                replica_groups=[[0, 1], [2, 3], [4, 5], [6, 7]],
                ins=[cc_in[layer]], outs=[cc_out[layer]]),
                wait=[tok(st)], inc=[(ccs[layer], 1)])
            cc_tok = (ccs[layer], ccs[layer].n)

            abar = barrier()
            NU = 16 if layer == 0 else 8
            NH_U = 1 if layer == 0 else 2
            pe_end = {}
            first_pe = {}
            last_redA = [None, None]
            last_redB = [None, None]
            last_nmbc = None
            last_expB = [None] * 6
            last_pvT = [None] * 6
            ctx_stt_tok = None
            last_g = None
            for u in range(NU):
                ub = u % 2
                lu = lu0 if ub == 0 else lu1
                kof = a_k + ub * 4096
                qof = a_q + ub * 2048
                vof = a_v + ub * (2 * NB * 65 * NH_U)
                if u >= 2:
                    lwait = [pe_end[u - 2]]
                elif u == 0:
                    lwait = [abar[0], cc_tok]
                else:
                    lwait = [first_pe[0]]
                p.emit("sync", lambda e, u=u, qof=qof: e.dma_start(
                    out=s_ar[:, qof:qof + T_OWN],
                    in_=q_scr[:, u * 128:(u + 1) * 128], transpose=True),
                    wait=lwait + [tok(lu)], inc=[(lu, 16)])
                for sec in range(2):
                    p.emit("sync", lambda e, u=u, sec=sec, kof=kof, layer=layer: e.dma_start(
                        out=s_ar[:, kof + sec * T_OWN: kof + (sec + 1) * T_OWN],
                        in_=cc_out[layer][sec * T_OWN:(sec + 1) * T_OWN, u * 128:(u + 1) * 128],
                        transpose=True), inc=[(lu, 16)])
                for hh in range(NH_U):
                    for sec in range(2):
                        v3 = s_ar[:, vof + (hh * 2 + sec) * NB * 65:
                                  vof + (hh * 2 + sec + 1) * NB * 65].rearrange(
                            "p (kb c) -> p kb c", kb=NB, c=65)
                        hcol = KW + (u * NH_U + hh) * 64
                        p.emit("sync", lambda e, v3=v3, sec=sec, hcol=hcol, layer=layer: e.dma_start(
                            out=v3[:, :, 0:64],
                            in_=cc_out[layer][sec * T_OWN:(sec + 1) * T_OWN, hcol:hcol + 64].rearrange(
                                "(kb p) c -> p kb c", kb=NB, p=128)),
                            inc=[(lu, 16)])
                        p.emit("vector", lambda e, v3=v3: e.memset(v3[:, :, 64:65], 1.0),
                               wait=(lwait + [abar[1]]) if u < 2 else [pe_end[u - 2]])
                t_load = tok(lu)
                t_vload = tok(vd)

                for hh in range(NH_U):
                    hb = 64 * hh
                    KD = 128 if layer == 0 else 64
                    vbase = vof + hh * 2 * NB * 65
                    femitted = False

                    nmrow_tok = None
                    if layer == 0:
                        ph1c = 0
                        for qb in range(NB):
                            lq = s_ar[:, qof + qb * 128: qof + (qb + 1) * 128]
                            nkb1 = qb + 1
                            for sec in range(2):
                                nch = (nkb1 * 128 + 1535) // 1536
                                for kc in range(nch):
                                    w = min(1536, nkb1 * 128 - kc * 1536)
                                    bk = ph1c % 2
                                    ph1c += 1
                                    ps = psB[bk]
                                    for wh in range(0, w, 512):
                                        ww = min(512, w - wh)
                                        lk = s_ar[:, kof + sec * T_OWN + kc * 1536 + wh:
                                                  kof + sec * T_OWN + kc * 1536 + wh + ww]
                                        wt = ([t_load] + ([last_redB[bk]] if last_redB[bk] else []) +
                                              ([last_expB[bk]] if last_expB[bk] else []) +
                                              ([last_g] if (bk == 1 and last_g) else [])) if wh == 0 else []
                                        p.emit("tensor", lambda e, lq=lq, lk=lk, ww=ww, wh=wh, ps=ps: e.matmul(
                                            ps[:, wh:wh + ww], lq, lk, start=True, stop=True),
                                            wait=wt, inc=[(pe, 1)])
                                    tmm = tok(pe)
                                    if not femitted:
                                        first_pe[u] = tmm
                                        femitted = True
                                    if kc * 1536 <= qb * 128 < kc * 1536 + w:
                                        off = qb * 128 - kc * 1536
                                        msk = s_MAq if sec == 0 else s_MBq
                                        p.emit("vector", lambda e, ps=ps, off=off, msk=msk: e.tensor_tensor(
                                            ps[:, off:off + 128], ps[:, off:off + 128], msk[:], ALU.add),
                                            wait=[tmm, consts_tok])
                                    first = (sec == 0 and kc == 0)
                                    if first:
                                        p.emit("vector", lambda e, ps=ps, qb=qb, w=w: e.reduce_max(
                                            s_nm[:, qb:qb + 1], ps[:, 0:w], axis=AX.X, negate=True),
                                            wait=[tmm])
                                    else:
                                        p.emit("vector", lambda e, ps=ps, w=w: e.reduce_max(
                                            s_tmp[:, 0:1], ps[:, 0:w], axis=AX.X, negate=True),
                                            wait=[tmm])
                                        p.emit("vector", lambda e, qb=qb: e.tensor_tensor(
                                            s_nm[:, qb:qb + 1], s_nm[:, qb:qb + 1], s_tmp[:, 0:1], ALU.min))
                                    last_redB[bk] = tok(vd)
                            p.emit("tensor", lambda e, qb=qb: e.transpose(
                                pC[0:1, 0:128], s_nm[:, qb:qb + 1], s_idf[:]),
                                wait=[tok(vd)] + ([last_redA[1]] if last_redA[1] else []) +
                                     ([ctx_stt_tok] if ctx_stt_tok else []), inc=[(pe, 1)])
                            p.emit("vector", lambda e, qb=qb: e.tensor_copy(
                                s_nmrow[0:1, qb * 128:(qb + 1) * 128], pC[0:1, 0:128]),
                                wait=[tok(pe)])
                            last_redA[1] = tok(vd)
                        nmrow_tok = tok(vd)

                    pend = []
                    pending_tail = None
                    last_pv_any = None
                    for qc in range(4):
                        njs = (4 * qc + 4) if layer == 0 else NB
                        kb_list = [(sec, jk) for sec in range(2) for jk in range(njs)]
                        nk = len(kb_list)
                        if layer == 0:
                            p.emit("tensor", lambda e, qc=qc: e.matmul(
                                pG, s_ones[:, :], s_nmrow[0:1, qc * 512:(qc + 1) * 512],
                                start=True, stop=True),
                                wait=[nmrow_tok] + ([last_nmbc] if last_nmbc else []) +
                                     ([last_redA[0]] if last_redA[0] else []), inc=[(pe, 1)])
                            p.emit("vector", lambda e: e.tensor_copy(s_nmbc[:], pG),
                                   wait=[tok(pe)], inc=[(vd, 1)])
                            nmbc_tok = tok(vd)
                            last_nmbc = nmbc_tok
                            last_redA[0] = nmbc_tok
                        for kidx in range(nk):
                            g = kidx % 6
                            sec, jk = kb_list[kidx]
                            ps = psB[g // 3][:, (g % 3) * 512:(g % 3 + 1) * 512]
                            pt = s_pT2[:, g * 512:(g + 1) * 512]
                            off = max(0, jk - 4 * qc) * 128 if layer == 0 else 0
                            lq = s_ar[hb:hb + KD, qof + qc * 512 + off: qof + (qc + 1) * 512]
                            lk = s_ar[hb:hb + KD, kof + sec * T_OWN + jk * 128: kof + sec * T_OWN + (jk + 1) * 128]
                            w0 = [t_load] + ([last_expB[g]] if last_expB[g] else [])
                            if last_redB[g // 3]:
                                w0.append(last_redB[g // 3])
                                last_redB[g // 3] = None
                            if layer == 0 and kidx == 0:
                                w0.append(nmbc_tok)
                            if g == 3 and last_g:
                                w0.append(last_g)
                            p.emit("tensor", lambda e, lk=lk, lq=lq, ps=ps, off=off: e.matmul(
                                ps[:, off:512], lk, lq, start=True, stop=True),
                                wait=w0, inc=[(pe, 1)])
                            if not femitted:
                                first_pe[u] = tok(pe)
                                femitted = True
                            tmm = tok(pe)
                            ew = [tmm]
                            if layer == 0:
                                if 4 * qc <= jk < 4 * qc + 4:
                                    p.emit("vector", lambda e, ps=ps, off=off: e.tensor_tensor(
                                        ps[:, off:512], ps[:, off:512],
                                        s_nmbc[:, off:512], ALU.add), wait=[tmm])
                                    moff = (jk - 4 * qc) * 128
                                    msk = s_MAt if sec == 0 else s_MBt
                                    p.emit("vector", lambda e, ps=ps, moff=moff, msk=msk: e.tensor_tensor(
                                        ps[:, moff:moff + 128], ps[:, moff:moff + 128],
                                        msk[:], ALU.add), wait=[consts_tok], inc=[(vd, 1)])
                                else:
                                    p.emit("vector", lambda e, ps=ps, off=off: e.tensor_tensor(
                                        ps[:, off:512], ps[:, off:512],
                                        s_nmbc[:, off:512], ALU.add), wait=[tmm], inc=[(vd, 1)])
                                ew = [tok(vd)]
                            while len(pend) > 4:
                                pend.pop(0)()
                            if pending_tail:
                                pending_tail()
                                pending_tail = None
                            p.emit("scalar", lambda e, ps=ps, pt=pt, off=off: e.activation(
                                pt[:, off:512], ps[:, off:512], AF.Exp),
                                wait=ew + ([last_pvT[g]] if last_pvT[g] else []),
                                inc=[(sd, 1)])
                            texp = tok(sd)
                            last_expB[g] = texp

                            def mk_pv(sec=sec, jk=jk, g=g, pt=pt, texp=texp, off=off,
                                      kidx=kidx, nk=nk, vbase=vbase):
                                def go():
                                    nonlocal last_pv_any
                                    pv_war = [ctx_stt_tok] if (kidx == 0 and ctx_stt_tok) else []
                                    if kidx == 0 and layer == 0:
                                        pv_war.append(nmrow_tok)
                                    p.emit("tensor", lambda e, pt=pt, sec=sec, jk=jk, vbase=vbase, kidx=kidx, nk=nk, off=off: e.matmul(
                                        pC[0:65, off:512],
                                        s_ar[:, vbase + sec * NB * 65 + jk * 65: vbase + sec * NB * 65 + (jk + 1) * 65],
                                        pt[:, off:512],
                                        start=(kidx == 0), stop=(kidx == nk - 1)),
                                        wait=[texp, t_vload] + pv_war, inc=[(pe, 1)])
                                    last_pvT[g] = tok(pe)
                                    last_pv_any = tok(pe)
                                return go
                            pend.append(mk_pv())
                        while pend:
                            pend.pop(0)()

                        def mk_tail(u=u, hh=hh, qc=qc, tpv_tok=None):
                            def go():
                                nonlocal ctx_stt_tok, last_g, last_redA
                                tpv = last_pv_any
                                lsl = s_ls[0:1, 0:512]
                                isl = s_ls[0:1, 512:1024]
                                p.emit("scalar", lambda e, lsl=lsl: e.activation(lsl, pC[64:65, :], AF.Ln),
                                       wait=[tpv], inc=[(sd, 1)])
                                p.emit("scalar", lambda e, lsl=lsl, isl=isl: e.activation(isl, lsl, AF.Exp, scale=-1.0),
                                       wait=[tok(sd)], inc=[(sd, 1)])
                                p.emit("tensor", lambda e, isl=isl: e.matmul(
                                    pG[0:64, :], s_ones[0:1, 0:64], isl, start=True, stop=True),
                                    wait=[tok(sd)] + ([last_g] if last_g else []) +
                                         ([last_redA[0]] if last_redA[0] else []), inc=[(pe, 1)])
                                p.emit("vector", lambda e: e.tensor_copy(s_g[:], pG[0:64, :]), wait=[tok(pe)])
                                last_g = tok(vd)
                                last_redA[0] = last_g
                                h = u * NH_U + hh
                                cbase = a_ctx + (h // 2) * T_OWN + qc * 512
                                crow = (h % 2) * 64
                                p.emit("vector", lambda e, cbase=cbase, crow=crow: e.scalar_tensor_tensor(
                                    s_ar[crow:crow + 64, cbase:cbase + 512], pC[0:64, :], 1.0, s_g[:],
                                    ALU.mult, ALU.mult), inc=[(vd, 1)])
                                ctx_stt_tok = tok(vd)
                            return go
                        pending_tail = mk_tail()
                    if pending_tail:
                        pending_tail()
                        pending_tail = None
                pe_end[u] = tok(pe)
            ctx_tok = tok(vd)

            obar = barrier()
            if layer == 0:
                for ic in range(8):
                    dma("gpsimd", lw, s_ar[:, a_wo + ic * E: a_wo + (ic + 1) * E],
                        Wo0[ic * 128:(ic + 1) * 128, :], wait=([tok(lw)] + obar) if ic == 0 else [])
                wo_tok = tok(lw)
                ob_tok = None
            else:
                prev2 = None
                for oc in range(8):
                    dma("gpsimd", l8, s_wrow[:], out_w[oc * 128:(oc + 1) * 128, :],
                        wait=[prev2] if prev2 else ([tok(l8)] + obar))
                    tg = tok(l8)
                    prev2 = transpose_rows_to(s_ar[:, a_wo:a_wo + 8 * E], E, oc, [tg])
                wo_tok = prev2
                ob_tok = bcast_row(s_bias, out_b, E, obar)
            last_wstt = [None, None]
            for j in range(NB):
                for oc in range(2):
                    pban = psA[oc]
                    for ic in range(8):
                        p.emit("tensor", lambda e, ic=ic, oc=oc, pban=pban, j=j: e.matmul(
                            pban,
                            s_ar[:, a_ctx + ic * T_OWN + j * 128: a_ctx + ic * T_OWN + (j + 1) * 128],
                            s_ar[:, a_wo + ic * E + oc * 512: a_wo + ic * E + (oc + 1) * 512],
                            start=(ic == 0), stop=(ic == 7)),
                            wait=([ctx_tok, wo_tok] + ([last_wstt[oc]] if last_wstt[oc] else []) +
                                  ([ob_tok] if ob_tok else [])) if ic == 0 else [],
                            inc=[(pe, 1)] if ic == 7 else [])
                    tpe = tok(pe)
                    xsl = s_x[:, j * E + oc * 512: j * E + (oc + 1) * 512]
                    p.emit("vector", lambda e, xsl=xsl, pban=pban: e.scalar_tensor_tensor(
                        xsl, pban, 1.0, xsl, ALU.mult, ALU.add),
                        wait=[tpe] + ([ob_tok] if ob_tok else []))
                    if layer == 1:
                        p.emit("vector", lambda e, xsl=xsl, oc=oc: e.tensor_tensor(
                            xsl, xsl, s_bias[:, oc * 512:(oc + 1) * 512], ALU.add), inc=[(vd, 1)])
                    else:
                        p.emit("vector", lambda e: e.engine_nop(), inc=[(vd, 1)])
                    last_wstt[oc] = tok(vd)

        fin = tok(vd)
        for j in range(NB):
            p.emit("gpsimd", lambda e, j=j: e.dma_start(
                out=x_out[j * 128:(j + 1) * 128, :], in_=s_x[:, j * E:(j + 1) * E]),
                wait=[fin] if j == 0 else [], inc=[(stx, 16)])
        p.emit("sync", lambda e: e.nop(), wait=[tok(stx)])
        p.run(blk)
    return nc




def run_fused(inp):
    nc = build_fused()
    f32 = np.float32
    x = np.ascontiguousarray(inp["L0_attn_x"], dtype=f32).reshape(B, S, E)
    tri = np.where(np.arange(128)[None, :] <= np.arange(128)[:, None], 0.0, -1e9).astype(BF)
    zer = np.zeros((128, 128), dtype=BF)
    ful = np.full((128, 128), -1e9, dtype=BF)
    common = dict(
        ident=_IDENT, identf=np.eye(128, dtype=f32), onesf=_ONESF,
        Wdq=inp["L0_attn_Wdq"].astype(f32), Wuq=inp["L0_attn_Wuq"].astype(f32),
        Wqr=inp["L0_attn_Wqr"].astype(f32), Wdkv=inp["L0_attn_Wdkv"].astype(f32),
        Wuk=inp["L0_attn_Wuk"].astype(f32), Wkr=inp["L0_attn_Wkr"].astype(f32),
        Wuv=inp["L0_attn_Wuv"].astype(f32), Wo0=inp["L0_attn_Wo"].astype(f32),
        w80=inp["L0_ffn_w_int8"].astype(np.int32),
        wscale=np.asarray(inp["L0_ffn_weight_scale"], f32).reshape(1, 1),
        fbias0=np.ascontiguousarray(inp["L0_ffn_bias"], f32).reshape(1, E),
        qkv_w=inp["L1_attn_qkv_w"].astype(f32),
        qkv_b=np.ascontiguousarray(inp["L1_attn_qkv_b"], f32).reshape(1, 3 * E),
        out_w=inp["L1_attn_out_w"].astype(f32),
        out_b=np.ascontiguousarray(inp["L1_attn_out_b"], f32).reshape(1, E),
        w81=inp["L1_ffn_ternary_weight"].astype(np.int32),
        ascale=np.asarray(inp["L1_ffn_act_scale"], f32).reshape(1, 1),
        fbias1=np.ascontiguousarray(inp["L1_ffn_bias"], f32).reshape(1, E),
    )
    in_maps = []
    for c in range(N_CORES):
        b, par = c // 2, c % 2
        blocks = np.arange(NB) * 2 + par
        pos = (blocks[:, None] * 128 + np.arange(128)[None, :]).reshape(-1)
        xc = np.ascontiguousarray(x[b].reshape(S // 128, 128, E)[blocks].reshape(T_OWN, E))
        C0, S0 = _rope_tables_l0(pos)
        C1, S1 = _rope_tables_l1(pos)
        m = dict(common)
        m.update(x_in=xc,
                 tabC0=_blocked(C0 * SCALE_SQ[0]), tabS0=_blocked(S0 * SCALE_SQ[0]),
                 tabC1=_blocked(C1 * SCALE_SQ[1]), tabS1=_blocked(S1 * SCALE_SQ[1]),
                 MAq=tri if par == 0 else zer,
                 MBq=ful if par == 0 else tri,
                 MAt=np.ascontiguousarray(tri.T) if par == 0 else zer,
                 MBt=np.ascontiguousarray(ful.T) if par == 0 else np.ascontiguousarray(tri.T))
        in_maps.append(m)
    res = run_bass_kernel_spmd(nc, in_maps, core_ids=list(range(N_CORES)))
    out = np.empty((B, S, E), dtype=f32)
    for c in range(N_CORES):
        b, par = c // 2, c % 2
        blocks = np.arange(NB) * 2 + par
        r = np.asarray(res.results[c]["x_out"], dtype=f32).reshape(NB, 128, E)
        out[b].reshape(S // 128, 128, E)[blocks] = r
    return out


def _bf(a):
    if a.dtype == BF:
        return a
    if a.dtype.itemsize == 2:
        return a.view(BF)
    return a.astype(BF)


def kernel(L0_attn_x, L0_attn_Wdq, L0_attn_Wuq, L0_attn_Wqr, L0_attn_Wdkv,
           L0_attn_Wuk, L0_attn_Wkr, L0_attn_Wuv, L0_attn_Wo,
           L0_ffn_w_int8, L0_ffn_weight_scale, L0_ffn_bias,
           L1_attn_qkv_w, L1_attn_qkv_b, L1_attn_out_w, L1_attn_out_b,
           L1_ffn_ternary_weight, L1_ffn_bias, L1_ffn_act_scale):
    inp = dict(L0_attn_x=L0_attn_x, L0_attn_Wdq=L0_attn_Wdq, L0_attn_Wuq=L0_attn_Wuq,
               L0_attn_Wqr=L0_attn_Wqr, L0_attn_Wdkv=L0_attn_Wdkv, L0_attn_Wuk=L0_attn_Wuk,
               L0_attn_Wkr=L0_attn_Wkr, L0_attn_Wuv=L0_attn_Wuv, L0_attn_Wo=L0_attn_Wo,
               L0_ffn_w_int8=L0_ffn_w_int8, L0_ffn_weight_scale=L0_ffn_weight_scale,
               L0_ffn_bias=L0_ffn_bias, L1_attn_qkv_w=L1_attn_qkv_w, L1_attn_qkv_b=L1_attn_qkv_b,
               L1_attn_out_w=L1_attn_out_w, L1_attn_out_b=L1_attn_out_b,
               L1_ffn_ternary_weight=L1_ffn_ternary_weight, L1_ffn_bias=L1_ffn_bias,
               L1_ffn_act_scale=L1_ffn_act_scale)
    return run_fused(inp)
